# version 12
# speedup vs baseline: 1.3324x; 1.0462x over previous
import numpy as np
from contextlib import ExitStack

import concourse.bass as bass
import concourse.mybir as mybir
import concourse.bass_utils as _bu
from concourse.bass_utils import run_bass_kernel_spmd

if not getattr(_bu, "_ldw_patched", False):
    _orig_run_command = _bu.run_command

    def _run_command_ldw(cmd, *a, **kw):
        if isinstance(cmd, list):
            cmd = ["--enable-ldw-opt=true" if c == "--enable-ldw-opt=false"
                   else c for c in cmd]
        return _orig_run_command(cmd, *a, **kw)

    _bu.run_command = _run_command_ldw
    _bu._ldw_patched = True

F32 = mybir.dt.float32
BF16 = mybir.dt.bfloat16
U8 = mybir.dt.uint8
AF = mybir.ActivationFunctionType
ALU = mybir.AluOpType
AX = mybir.AxisListType

D = 2048
H = 16
DK = 128
CS = 64
TPC = 1024
NT = 8
NK = 16
NCH = 16
NSLAB = 8
DH = 8192
NHG = 4
HPG = 4
NCORE = 8

OFF_XN = 0
OFF_GA = 32768
OFF_ATTN = 32768
OFF_K = (65536, 65536 + 8192)
OFF_V = (65536 + 16384, 65536 + 24576)
OFF_G = 65536 + 32768
OFF_Q = 65536 + 40960
OFF_BIG = 65536 + 49152
BIG_SZ = 82304
ARENA_BYTES = OFF_BIG + BIG_SZ

OFF_XB0 = OFF_BIG + 32768
OFF_XB1 = OFF_BIG + 40960
OFF_X2 = OFF_BIG + 49152
OFF_XNS0 = OFF_BIG + 57344
OFF_XNS1 = OFF_BIG + 61440
OFF_UPS = OFF_BIG
OFF_DNS = OFF_BIG + 32768
OFF_HID = OFF_BIG + 65536
OFF_DSREP = OFF_BIG + 65536
OFF_DSROW = OFF_BIG + 73856
OFF_WSA = OFF_BIG
OFF_WSB = OFF_BIG + 16384
OFF_GFFN = OFF_BIG + 32768
OFF_QTMP = OFF_BIG + 65536
OFF_NUMSB = OFF_BIG + 65536
OFF_RT = OFF_BIG + 73856
OFF_GATH = OFF_BIG + 73856
OFF_WO = OFF_BIG
OFF_P3X0 = OFF_BIG + 65536
OFF_P3X1 = OFF_BIG + 69632
OFF_P3O0 = OFF_BIG + 73856
OFF_P3O1 = OFF_BIG + 77952

ENGS = ("tensor", "vector", "scalar", "gpsimd", "sync")


class _Stop(Exception):
    pass


class Sem:
    __slots__ = ("h", "n")

    def __init__(self, h):
        self.h = h
        self.n = 0


def build_nc(debug=(), upto="full"):
    order = ("p0", "p1", "p2proj", "p2fm", "p2scan", "p2cc", "p2", "full")
    lvl = order.index(upto)
    nc = bass.Bass("TRN2", target_bir_lowering=False,
                   detect_race_conditions=False)

    d_x = nc.dram_tensor("x", [TPC, D], F32, kind="ExternalInput").ap()
    d_w = {nm: nc.dram_tensor(nm, [D, D], F32, kind="ExternalInput").ap()
           for nm in ("wq", "wk", "wv", "wg", "wo")}
    d_up = nc.dram_tensor("upw", [D, DH], F32, kind="ExternalInput").ap()
    d_dn = nc.dram_tensor("dnw", [DH, D], F32, kind="ExternalInput").ap()
    d_b = {nm: nc.dram_tensor(nm, [1, D], F32, kind="ExternalInput").ap()
           for nm in ("bq", "bk", "bv", "bg", "bo")}
    d_usc = nc.dram_tensor("usc", [128, 64], F32, kind="ExternalInput").ap()
    d_dsc = nc.dram_tensor("dsc", [1, D], F32, kind="ExternalInput").ap()
    d_osc = nc.dram_tensor("osc", [1, 1], F32, kind="ExternalInput").ap()
    d_sin = nc.dram_tensor("rsin", [TPC, 64], F32, kind="ExternalInput").ap()
    d_cos = nc.dram_tensor("rcos", [TPC, 64], F32, kind="ExternalInput").ap()
    d_id = nc.dram_tensor("ident", [128, 128], F32, kind="ExternalInput").ap()
    d_msk = nc.dram_tensor("mask8", [128, 8], F32, kind="ExternalInput").ap()
    d_y = nc.dram_tensor("y", [TPC, D], F32, kind="ExternalOutput").ap()
    if lvl >= 5:
        d_ccs = [nc.dram_tensor(f"cc_src{h}", [128, 516], BF16)
                 for h in range(NHG)]
        d_ccd = [nc.dram_tensor(f"cc_dst{h}", [128 * NCORE, 516], BF16,
                                addr_space="Shared") for h in range(NHG)]
        d_wus = nc.dram_tensor("cc_wsrc", [1, 16], BF16)
        d_wud = nc.dram_tensor("cc_wdst", [NCORE, 16], BF16,
                               addr_space="Shared")
    d_dbg = {nm: nc.dram_tensor("dbg_" + nm, list(shape), F32,
                                kind="ExternalOutput").ap()
             for (nm, shape) in debug}

    P = {e: [] for e in ENGS}

    def do(eng, method, *a, inc=(), **kw):
        if method == "dma_start":
            for s, k in inc:
                if k == 16 and s.n:
                    wt(eng, (s, s.n))
        incs = [(s.h, k) for (s, k) in inc]

        def f(E, m=method, a=a, kw=kw, incs=incs):
            ins = getattr(E, m)(*a, **kw)
            for hh, kk in incs:
                ins = ins.then_inc(hh, kk)
        f._incs = incs
        P[eng].append(f)
        for s, k in inc:
            s.n += k

    def inc_last(eng, *pairs):
        f = P[eng][-1]
        for s, k in pairs:
            for idx, (hh, kk) in enumerate(f._incs):
                if hh is s.h:
                    f._incs[idx] = (hh, kk + k)
                    break
            else:
                f._incs.append((s.h, k))
            s.n += k

    def wt(eng, *pairs):
        ps = tuple((s.h, v) for (s, v) in pairs)

        def f(E, ps=ps):
            for hh, vv in ps:
                E.wait_ge(hh, vv)
        P[eng].append(f)

    with ExitStack() as stk:
        arena = stk.enter_context(nc.sbuf_tensor("arena", [128, ARENA_BYTES], U8))
        t_sin = stk.enter_context(nc.sbuf_tensor("t_sin", [128, NT * 64], F32))
        t_cos = stk.enter_context(nc.sbuf_tensor("t_cos", [128, NT * 64], F32))
        t_usc = stk.enter_context(nc.sbuf_tensor("t_usc", [128, 64], F32))
        t_id = stk.enter_context(nc.sbuf_tensor("t_id", [128, 128], BF16))
        t_ones = stk.enter_context(nc.sbuf_tensor("t_ones", [1, 128], BF16))
        t_ones32 = stk.enter_context(nc.sbuf_tensor("t_ones32", [1, 128], F32))
        t_onescol = stk.enter_context(nc.sbuf_tensor("t_onescol", [128, 1], BF16))
        t_bias4 = stk.enter_context(nc.sbuf_tensor("t_bias4", [1, 1024], BF16))
        t_cst = stk.enter_context(nc.sbuf_tensor("t_cst", [128, 4], F32))
        t_stats = stk.enter_context(nc.sbuf_tensor("t_stats", [128, 8], F32))
        t_msk = stk.enter_context(nc.sbuf_tensor("t_msk", [128, 8], F32))
        t_den = stk.enter_context(nc.sbuf_tensor("t_den", [128, NT * HPG], F32))
        t_dtmp = stk.enter_context(nc.sbuf_tensor("t_dtmp", [128, HPG], F32))
        t_rden = stk.enter_context(nc.sbuf_tensor("t_rden", [128, HPG], F32))
        t_szh = stk.enter_context(nc.sbuf_tensor("t_szh", [128, 516], BF16))
        t_sfin = stk.enter_context(nc.sbuf_tensor("t_sfin", [128, 516], F32))
        t_atok = stk.enter_context(nc.sbuf_tensor("t_atok", [128, 512], BF16))
        t_c129 = stk.enter_context(nc.sbuf_tensor("t_c129", [128, 520], F32))
        t_os = stk.enter_context(nc.sbuf_tensor("t_os", [1, 1], F32))
        t_os128 = stk.enter_context(nc.sbuf_tensor("t_os128", [1, 128], F32))
        t_eps = stk.enter_context(nc.sbuf_tensor("t_eps", [1, 2], BF16))

        psw0 = stk.enter_context(nc.psum_tensor("psw0", [128, 2048], F32))
        psw1 = stk.enter_context(nc.psum_tensor("psw1", [128, 2048], F32))

        s_dx = Sem(stk.enter_context(nc.semaphore("s_dx")))
        s_dw = Sem(stk.enter_context(nc.semaphore("s_dw")))
        s_out = Sem(stk.enter_context(nc.semaphore("s_out")))
        s_pe = Sem(stk.enter_context(nc.semaphore("s_pe")))
        s_dv = Sem(stk.enter_context(nc.semaphore("s_dv")))
        s_ac = Sem(stk.enter_context(nc.semaphore("s_ac")))
        s_gp = Sem(stk.enter_context(nc.semaphore("s_gp")))
        s_cc = Sem(stk.enter_context(nc.semaphore("s_cc")))

        def view(off, nelem, dt):
            sz = 4 if dt == F32 else 2
            return arena[:, off:off + nelem * sz].bitcast(dt)

        xn_fm = view(OFF_XN, NK * 1024, BF16)
        ga = view(OFF_GA, NT * 2048, F32)
        attn_fm = view(OFF_ATTN, NK * 1024, BF16)
        k_tok2 = [view(OFF_K[0], NT * 512, BF16), view(OFF_K[1], NT * 512, BF16)]
        v_tok2 = [view(OFF_V[0], NT * 512, BF16), view(OFF_V[1], NT * 512, BF16)]
        g_tok = view(OFF_G, NT * 512, BF16)
        q_fm = view(OFF_Q, HPG * 1024, BF16)
        gffn = view(OFF_GFFN, NT * 2048, BF16)
        up_sgn = view(OFF_UPS, NK * 1024, BF16)
        dn_sgn = view(OFF_DNS, 8 * 2048, BF16)
        hid = view(OFF_HID, 8 * 1024, BF16)
        wsA = view(OFF_WSA, NK * 512, BF16)
        wsB = view(OFF_WSB, NK * 512, BF16)
        qtmp = view(OFF_QTMP, NT * 512, BF16)
        ropeT = [view(OFF_RT + i * 2048, 1024, BF16) for i in range(4)]
        num_sb = view(OFF_NUMSB, NT * 516, BF16)
        gath = view(OFF_GATH, 8 * 516, BF16)
        szh2 = t_c129[:, 0:258].bitcast(BF16)

        try:
            do("gpsimd", "memset", t_ones[:, :], 1.0)
            do("gpsimd", "memset", t_ones32[:, :], 1.0)
            do("gpsimd", "memset", t_onescol[:, :], 1.0)
            do("gpsimd", "memset", t_cst[:, 0:1], 1e-5)
            do("gpsimd", "memset", t_cst[:, 1:2], -0.5)
            do("gpsimd", "memset", t_cst[:, 2:3], 1.0 / D)
            do("gpsimd", "memset", t_eps[:, :], 1e-6, inc=((s_gp, 1),))
            m_pre = s_gp.n
            do("gpsimd", "dma_start",
               out=t_sin[:, :].rearrange("p (t f) -> p t f", t=NT),
               in_=d_sin.rearrange("(t p) f -> p t f", p=128), inc=((s_dw, 16),))
            do("gpsimd", "dma_start",
               out=t_cos[:, :].rearrange("p (t f) -> p t f", t=NT),
               in_=d_cos.rearrange("(t p) f -> p t f", p=128), inc=((s_dw, 16),))
            do("gpsimd", "dma_start", out=t_usc[:, :], in_=d_usc, inc=((s_dw, 16),))
            do("gpsimd", "dma_start", out=t_id[:, :], in_=d_id, inc=((s_dw, 16),))
            do("gpsimd", "dma_start", out=t_msk[:, :], in_=d_msk, inc=((s_dw, 16),))
            do("gpsimd", "dma_start", out=t_os[:, :], in_=d_osc, inc=((s_dw, 16),))
            m_tables = s_dw.n
            do("gpsimd", "dma_start", out=d_wus[:, :], in_=t_ones[0:1, 0:16],
               inc=((s_dw, 16),))
            wt("gpsimd", (s_dw, s_dw.n))
            do("gpsimd", "collective_compute", "AllGather", ALU.bypass,
               replica_groups=[list(range(NCORE))],
               ins=[d_wus[:, :]], outs=[d_wud[:, :]], inc=((s_cc, 1),))

            def upslab_ap(s):
                return d_up[:, s * 1024:(s + 1) * 1024].rearrange(
                    "(k p) h -> p k h", p=128)

            def dnslab_ap(s):
                return d_dn[s * 1024:(s + 1) * 1024, :].rearrange(
                    "(t p) j -> p t j", p=128)

            do("gpsimd", "dma_start",
               out=up_sgn.rearrange("p (k h) -> p k h", k=NK), in_=upslab_ap(0),
               inc=((s_dw, 16),))
            m_updma = [s_dw.n]

            xb = [view(OFF_XB0, 2048, F32), view(OFF_XB1, 2048, F32)]
            x2 = view(OFF_X2, 2048, F32)
            xns = [view(OFF_XNS0, 2048, BF16), view(OFF_XNS1, 2048, BF16)]
            m_xfree = [0] * NT
            m_tp_ev = [0, 0]

            for j in range(NT):
                if j >= 2:
                    wt("sync", (s_dv, m_xfree[j - 2]))
                do("sync", "dma_start", out=xb[j % 2],
                   in_=d_x[j * 128:(j + 1) * 128, :], inc=((s_dx, 16),))
                m_xdma = s_dx.n
                wt("vector", (s_dx, m_xdma))
                do("vector", "reduce_sum", t_stats[:, 0:1], xb[j % 2], axis=AX.X,
                   inc=((s_dv, 1),))
                wt("scalar", (s_dv, s_dv.n), (s_gp, m_pre))
                do("scalar", "activation", out=t_stats[:, 4:5],
                   in_=t_stats[:, 0:1], func=AF.Copy, scale=t_cst[:, 2:3],
                   inc=((s_ac, 1),))
                wt("vector", (s_ac, s_ac.n))
                do("vector", "tensor_scalar", xb[j % 2], xb[j % 2],
                   t_stats[:, 4:5], None, ALU.subtract, inc=((s_dv, 1),))
                wt("scalar", (s_dv, s_dv.n))
                do("scalar", "activation", out=x2, in_=xb[j % 2], func=AF.Square,
                   inc=((s_ac, 1),))
                wt("vector", (s_ac, s_ac.n))
                do("vector", "reduce_sum", t_stats[:, 1:2], x2, axis=AX.X,
                   inc=((s_dv, 1),))
                wt("scalar", (s_dv, s_dv.n))
                do("scalar", "activation", out=t_stats[:, 3:4], in_=t_stats[:, 1:2],
                   func=AF.Ln, scale=t_cst[:, 2:3], bias=t_cst[:, 0:1])
                do("scalar", "activation", out=t_stats[:, 3:4], in_=t_stats[:, 3:4],
                   func=AF.Exp, scale=t_cst[:, 1:2], inc=((s_ac, 1),))
                wt("vector", (s_ac, s_ac.n))
                do("vector", "tensor_scalar", xns[j % 2], xb[j % 2],
                   t_stats[:, 3:4], None, ALU.mult, inc=((s_dv, 1),))
                m_xn_j = s_dv.n
                m_xfree[j] = m_xn_j
                wt("tensor", (s_dv, m_xn_j), (s_dw, m_tables))
                for kq in range(4):
                    pbank = psw0[:, (kq % 2) * 512:(kq % 2) * 512 + 256].bitcast(BF16)
                    if m_tp_ev[kq % 2]:
                        wt("tensor", (s_dv, m_tp_ev[kq % 2]))
                    for kk in range(4):
                        k = kq * 4 + kk
                        do("tensor", "transpose", pbank[:, kk * 128:(kk + 1) * 128],
                           xns[j % 2][:, k * 128:(k + 1) * 128], t_id[:, :])
                    inc_last("tensor", (s_pe, 1))
                    wt("vector", (s_pe, s_pe.n))
                    do("vector", "tensor_copy",
                       xn_fm.rearrange("p (k t) -> p k t", k=NK)[
                           :, kq * 4:(kq + 1) * 4, j * 128:(j + 1) * 128],
                       pbank.rearrange("p (k t) -> p k t", k=4), inc=((s_dv, 1),))
                    m_tp_ev[kq % 2] = s_dv.n
            m_xn_all = s_dv.n

            wt("gpsimd", (s_dv, m_xn_all))
            do("gpsimd", "dma_start",
               out=dn_sgn.rearrange("p (t j) -> p t j", t=8), in_=dnslab_ap(0),
               inc=((s_dw, 16),))
            m_dndma = [s_dw.n]

            if lvl < 1:
                raise _Stop

            m_upmm = [0] * NSLAB
            m_dnmm = [0] * NSLAB
            m_gadd = {}

            for s in range(NSLAB):
                wt("scalar", (s_dw, m_updma[s]))
                do("scalar", "activation", out=up_sgn, in_=up_sgn, func=AF.Sign,
                   inc=((s_ac, 1),))
                m_upsgn = s_ac.n
                wt("scalar", (s_dw, m_dndma[s]))
                do("scalar", "activation", out=dn_sgn, in_=dn_sgn, func=AF.Sign,
                   inc=((s_ac, 1),))
                m_dnsgn = s_ac.n

                if s == 0:
                    wt("tensor", (s_ac, m_upsgn), (s_dv, m_xn_all))
                else:
                    wt("tensor", (s_ac, m_upsgn), (s_dv, m_gadd[(s - 1, NT - 1)]))
                ev_ms = {}
                for ht in range(8):
                    pA = psw0[:, (ht % 2) * 1024:(ht % 2) * 1024 + 512]
                    pB = psw0[:, (ht % 2) * 1024 + 512:(ht % 2) * 1024 + 1024]
                    if ht >= 2:
                        wt("tensor", (s_dv, ev_ms[ht - 2]))
                    for k in range(NK):
                        lhsT = up_sgn[:, k * 1024 + ht * 128:k * 1024 + (ht + 1) * 128]
                        do("tensor", "matmul", pA, lhsT,
                           xn_fm[:, k * 1024:k * 1024 + 512],
                           start=(k == 0), stop=(k == NK - 1), skip_group_check=True)
                        do("tensor", "matmul", pB, lhsT,
                           xn_fm[:, k * 1024 + 512:k * 1024 + 1024],
                           start=(k == 0), stop=(k == NK - 1), skip_group_check=True)
                    inc_last("tensor", (s_pe, 1))
                    wt("vector", (s_pe, s_pe.n))
                    do("vector", "tensor_scalar", hid[:, ht * 1024:ht * 1024 + 512],
                       pA, t_usc[:, s * 8 + ht:s * 8 + ht + 1], 0.0, ALU.mult, ALU.max)
                    do("vector", "tensor_scalar",
                       hid[:, ht * 1024 + 512:(ht + 1) * 1024],
                       pB, t_usc[:, s * 8 + ht:s * 8 + ht + 1], 0.0, ALU.mult, ALU.max,
                       inc=((s_dv, 1),))
                    ev_ms[ht] = s_dv.n
                inc_last("tensor", (s_pe, 1))
                m_upmm[s] = s_pe.n
                m_hid_done = ev_ms[7]

                if s + 1 < NSLAB:
                    wt("gpsimd", (s_pe, m_upmm[s]))
                    do("gpsimd", "dma_start",
                       out=up_sgn.rearrange("p (k h) -> p k h", k=NK),
                       in_=upslab_ap(s + 1), inc=((s_dw, 16),))
                    m_updma.append(s_dw.n)

                wt("tensor", (s_dv, m_hid_done), (s_ac, m_dnsgn))
                for tt in range(NT):
                    ps = psw0 if tt % 2 == 0 else psw1
                    if tt >= 2:
                        wt("tensor", (s_dv, m_gadd[(s, tt - 2)]))
                    for ht in range(8):
                        lhsT = hid[:, ht * 1024 + tt * 128:ht * 1024 + (tt + 1) * 128]
                        for jc in range(4):
                            do("tensor", "matmul", ps[:, jc * 512:(jc + 1) * 512],
                               lhsT,
                               dn_sgn[:, ht * 2048 + jc * 512:ht * 2048 + (jc + 1) * 512],
                               start=(ht == 0), stop=(ht == 7), skip_group_check=True)
                    inc_last("tensor", (s_pe, 1))
                    wt("vector", (s_pe, s_pe.n))
                    if s == 0:
                        do("vector", "tensor_copy", ga[:, tt * 2048:(tt + 1) * 2048],
                           ps[:, :], inc=((s_dv, 1),))
                    else:
                        do("vector", "tensor_tensor", ga[:, tt * 2048:(tt + 1) * 2048],
                           ga[:, tt * 2048:(tt + 1) * 2048], ps[:, :], ALU.add,
                           inc=((s_dv, 1),))
                    m_gadd[(s, tt)] = s_dv.n
                inc_last("tensor", (s_pe, 1))
                m_dnmm[s] = s_pe.n

                if s + 1 < NSLAB:
                    wt("gpsimd", (s_pe, m_dnmm[s]))
                    do("gpsimd", "dma_start",
                       out=dn_sgn.rearrange("p (t j) -> p t j", t=8),
                       in_=dnslab_ap(s + 1), inc=((s_dw, 16),))
                    m_dndma.append(s_dw.n)

            ds_rep = view(OFF_DSREP, 2048, F32)
            ds_tmp = view(OFF_DSROW, 2048, F32)
            wt("sync", (s_pe, m_dnmm[NSLAB - 1]))
            do("sync", "dma_start", out=ds_tmp[0:1, :], in_=d_dsc, inc=((s_dx, 16),))
            m_dsrow = s_dx.n
            wt("vector", (s_dw, m_tables))
            do("vector", "tensor_scalar", t_os128[0:1, :], t_ones32[0:1, :],
               t_os[0:1, 0:1], None, ALU.mult, inc=((s_dv, 1),))
            m_os128 = s_dv.n
            wt("tensor", (s_dv, m_os128), (s_dx, m_dsrow))
            for jc in range(4):
                do("tensor", "matmul", psw0[:, jc * 512:(jc + 1) * 512],
                   t_os128[0:1, :], ds_tmp[0:1, jc * 512:(jc + 1) * 512],
                   start=True, stop=True, skip_group_check=True)
            inc_last("tensor", (s_pe, 1))
            wt("vector", (s_pe, s_pe.n), (s_dv, m_gadd[(NSLAB - 1, NT - 1)]))
            do("vector", "tensor_copy", ds_rep, psw0[:, :], inc=((s_dv, 1),))
            m_silu = 0
            for tt in range(NT):
                if m_silu:
                    wt("vector", (s_ac, m_silu))
                do("vector", "tensor_tensor", ds_tmp, ga[:, tt * 2048:(tt + 1) * 2048],
                   ds_rep, ALU.mult, inc=((s_dv, 1),))
                wt("scalar", (s_dv, s_dv.n))
                do("scalar", "activation", out=gffn[:, tt * 2048:(tt + 1) * 2048],
                   in_=ds_tmp, func=AF.Silu, inc=((s_ac, 1),))
                m_silu = s_ac.n
            m_gffn = s_ac.n
            m_dsrep_read = s_dv.n

            if lvl < 2:
                raise _Stop


            sA = psw1[:, 0:512]
            sB = psw1[:, 512:1024]
            tp_bf = psw1[:, 1024:1280].bitcast(BF16)
            tp_bf2 = psw1[:, 1536:1792].bitcast(BF16)

            def wslab_ap(wname, hg):
                return d_w[wname][:, hg * 512:(hg + 1) * 512].rearrange(
                    "(k p) j -> p k j", p=128)

            m_gate = [0] * NHG
            m_kvmm = [0] * (NHG + 1)
            ev_kv = [dict() for _ in range(NHG + 1)]
            m_b4 = [0] * NHG
            m_gqmm = [0] * NHG

            wt("gpsimd", (s_pe, m_upmm[NSLAB - 1]))
            do("gpsimd", "dma_start", out=wsA.rearrange("p (k j) -> p k j", k=NK),
               in_=wslab_ap("wk", 0), inc=((s_dw, 16),))
            do("gpsimd", "dma_start", out=wsB.rearrange("p (k j) -> p k j", k=NK),
               in_=wslab_ap("wv", 0), inc=((s_dw, 16),))
            m_wkv = s_dw.n
            for bi, bn in enumerate(("bk", "bv")):
                do("gpsimd", "dma_start",
                   out=t_bias4[0:1, bi * 512:(bi + 1) * 512],
                   in_=d_b[bn][0:1, 0:512], inc=((s_dw, 16),))
            m_b4[0] = s_dw.n

            def emit_kv_group(hg, tt, m_w):
                ks = k_tok2[hg % 2]
                vs = v_tok2[hg % 2]
                kps = psw0[:, 0:512]
                vps = psw0[:, 512:1024]
                if tt == 0:
                    wt("tensor", (s_dw, m_w), (s_dw, m_b4[hg]),
                       (s_dv, m_dsrep_read))
                else:
                    wt("tensor", (s_dv, ev_kv[hg][tt - 1]))
                do("tensor", "matmul", kps, t_ones[0:1, :],
                   t_bias4[0:1, 0:512], start=True, stop=False,
                   skip_group_check=True)
                do("tensor", "matmul", vps, t_ones[0:1, :],
                   t_bias4[0:1, 512:1024], start=True, stop=False,
                   skip_group_check=True)
                for k in range(NK):
                    lhsT = xn_fm[:, k * 1024 + tt * 128:k * 1024 + (tt + 1) * 128]
                    do("tensor", "matmul", kps, lhsT,
                       wsA[:, k * 512:(k + 1) * 512],
                       start=False, stop=(k == NK - 1), skip_group_check=True)
                    do("tensor", "matmul", vps, lhsT,
                       wsB[:, k * 512:(k + 1) * 512],
                       start=False, stop=(k == NK - 1), skip_group_check=True)
                inc_last("tensor", (s_pe, 1))
                wt("vector", (s_pe, s_pe.n))
                do("vector", "tensor_copy", ks[:, tt * 512:(tt + 1) * 512], kps)
                do("vector", "tensor_copy", vs[:, tt * 512:(tt + 1) * 512], vps,
                   inc=((s_dv, 1),))
                ev_kv[hg][tt] = s_dv.n

            def emit_q_proj(hg, m_wq, m_bq, m_qdst_free):
                qdst = v_tok2[(hg + 1) % 2]
                ev_prev = [0, 0]
                for tt in range(NT):
                    qps = psw0[:, 1024 + (tt % 2) * 512:1536 + (tt % 2) * 512]
                    if tt == 0:
                        wt("tensor", (s_dw, m_wq), (s_dv, m_qdst_free))
                    elif tt >= 2:
                        wt("tensor", (s_dv, ev_prev[tt % 2]))
                    for k in range(NK):
                        lhsT = xn_fm[:, k * 1024 + tt * 128:k * 1024 + (tt + 1) * 128]
                        do("tensor", "matmul", qps, lhsT,
                           wsB[:, k * 512:(k + 1) * 512],
                           start=(k == 0), stop=False, skip_group_check=True)
                    if tt == 0:
                        wt("tensor", (s_dw, m_bq))
                    do("tensor", "matmul", qps, t_ones[0:1, :],
                       t_bias4[0:1, 512:1024], start=False, stop=True,
                       skip_group_check=True)
                    inc_last("tensor", (s_pe, 1))
                    wt("vector", (s_pe, s_pe.n))
                    do("vector", "tensor_copy", qdst[:, tt * 512:(tt + 1) * 512],
                       qps, inc=((s_dv, 1),))
                    ev_prev[tt % 2] = s_dv.n
                inc_last("tensor", (s_pe, 1))
                return s_pe.n, s_dv.n

            ev_k = [dict() for _ in range(NHG + 1)]
            ev_v = [dict() for _ in range(NHG + 1)]
            m_ktok = [0] * (NHG + 1)

            def emit_k_group(hg, tt, m_w, m_b):
                ks = k_tok2[hg % 2]
                kps = psw0[:, 0:512]
                if tt == 0:
                    wt("tensor", (s_dw, m_w), (s_dw, m_b))
                else:
                    wt("tensor", (s_dv, ev_k[hg][tt - 1]))
                for k in range(NK):
                    lhsT = xn_fm[:, k * 1024 + tt * 128:k * 1024 + (tt + 1) * 128]
                    do("tensor", "matmul", kps, lhsT,
                       wsB[:, k * 512:(k + 1) * 512],
                       start=(k == 0), stop=False, skip_group_check=True)
                do("tensor", "matmul", kps, t_ones[0:1, :],
                   t_bias4[0:1, 0:512], start=False, stop=True,
                   skip_group_check=True)
                inc_last("tensor", (s_pe, 1))
                wt("vector", (s_pe, s_pe.n))
                do("vector", "tensor_copy", ks[:, tt * 512:(tt + 1) * 512], kps,
                   inc=((s_dv, 1),))
                ev_k[hg][tt] = s_dv.n

            def emit_v_group(hg, tt, m_w):
                vs = v_tok2[hg % 2]
                vps = psw0[:, 512:1024]
                if tt == 0:
                    wt("tensor", (s_dw, m_w))
                else:
                    wt("tensor", (s_dv, ev_v[hg][tt - 1]))
                for k in range(NK):
                    lhsT = xn_fm[:, k * 1024 + tt * 128:k * 1024 + (tt + 1) * 128]
                    do("tensor", "matmul", vps, lhsT,
                       wsA[:, k * 512:(k + 1) * 512],
                       start=(k == 0), stop=False, skip_group_check=True)
                do("tensor", "matmul", vps, t_ones[0:1, :],
                   t_bias4[0:1, 512:1024], start=False, stop=True,
                   skip_group_check=True)
                inc_last("tensor", (s_pe, 1))
                wt("vector", (s_pe, s_pe.n))
                do("vector", "tensor_copy", vs[:, tt * 512:(tt + 1) * 512], vps,
                   inc=((s_dv, 1),))
                ev_v[hg][tt] = s_dv.n

            for tt in range(NT):
                emit_kv_group(0, tt, m_wkv)
            inc_last("tensor", (s_pe, 1))
            m_kvmm[0] = s_pe.n
            m_ktok[0] = ev_kv[0][NT - 1]

            wt("gpsimd", (s_pe, m_kvmm[0]))
            do("gpsimd", "dma_start", out=wsB.rearrange("p (k j) -> p k j", k=NK),
               in_=wslab_ap("wq", 0), inc=((s_dw, 16),))
            do("gpsimd", "dma_start",
               out=t_bias4[0:1, 512:1024], in_=d_b["bq"][0:1, 0:512],
               inc=((s_dw, 16),))
            m_wq = s_dw.n
            do("gpsimd", "dma_start", out=wsA.rearrange("p (k j) -> p k j", k=NK),
               in_=wslab_ap("wg", 0), inc=((s_dw, 16),))
            do("gpsimd", "dma_start",
               out=t_bias4[0:1, 0:512], in_=d_b["bg"][0:1, 0:512],
               inc=((s_dw, 16),))
            m_wg = s_dw.n
            m_qmm, m_qev = emit_q_proj(0, m_wq, m_wq, m_dsrep_read)

            def rope_hb(buf, hb):
                b4 = buf.rearrange("p (t h f) -> p t h f", t=NT, h=HPG)
                ropeT = [view(OFF_RT + i * 2048, 1024, BF16) for i in range(4)]
                x1 = b4[:, hb * 4:(hb + 1) * 4, :, 0:64]
                x2 = b4[:, hb * 4:(hb + 1) * 4, :, 64:128]
                cosb = t_cos[:, :].rearrange("p (t f) -> p t f", t=NT)[
                    :, hb * 4:(hb + 1) * 4, :].unsqueeze(2).broadcast_to(
                    [128, 4, HPG, 64])
                sinb = t_sin[:, :].rearrange("p (t f) -> p t f", t=NT)[
                    :, hb * 4:(hb + 1) * 4, :].unsqueeze(2).broadcast_to(
                    [128, 4, HPG, 64])
                r4 = [r.rearrange("p (t h f) -> p t h f", t=4, h=HPG)
                      for r in ropeT]
                do("vector", "tensor_tensor", r4[0], x1, cosb, ALU.mult)
                do("vector", "tensor_tensor", r4[1], x1, sinb, ALU.mult)
                do("vector", "tensor_tensor", r4[2], x2, sinb, ALU.mult)
                do("vector", "tensor_tensor", r4[3], x2, cosb, ALU.mult)
                do("vector", "tensor_tensor", x1, r4[0], r4[2], ALU.subtract)
                do("vector", "tensor_tensor", x2, r4[1], r4[3], ALU.add,
                   inc=((s_dv, 1),))

            def elu_q(buf, q4):
                ropeT = [view(OFF_RT + i * 2048, 1024, BF16) for i in range(4)]
                tmin = ropeT[0]
                texp = ropeT[2]
                sl = buf[:, q4 * 1024:(q4 + 1) * 1024]
                do("vector", "tensor_single_scalar", tmin, sl, 0.0, ALU.min,
                   inc=((s_dv, 1),))
                wt("scalar", (s_dv, s_dv.n))
                do("scalar", "activation", out=texp, in_=tmin, func=AF.Exp,
                   inc=((s_ac, 1),))
                wt("vector", (s_ac, s_ac.n))
                do("vector", "scalar_tensor_tensor", sl, sl, 0.0, texp,
                   ALU.max, ALU.add, inc=((s_dv, 1),))

            for hg in range(NHG):
                slot = hg % 2
                k_tok = k_tok2[slot]
                v_tok = v_tok2[slot]
                qtok = v_tok2[(hg + 1) % 2]

                if hg + 1 < NHG:
                    wt("gpsimd", (s_pe, m_qmm))
                    do("gpsimd", "dma_start",
                       out=wsB.rearrange("p (k j) -> p k j", k=NK),
                       in_=wslab_ap("wk", hg + 1), inc=((s_dw, 16),))
                    do("gpsimd", "dma_start",
                       out=t_bias4[0:1, 512:1024],
                       in_=d_b["bv"][0:1, (hg + 1) * 512:(hg + 2) * 512],
                       inc=((s_dw, 16),))
                    m_wkb = s_dw.n

                fm_stage = 0
                wt("tensor", (s_dw, m_wg))
                if hg > 0:
                    wt("tensor", (s_dv, m_gate[hg - 1]))
                else:
                    wt("tensor", (s_dv, ev_kv[0][NT - 1]), (s_dv, m_qev))
                ev_g = [0, 0]
                for tt in range(NT):
                    gps = psw0[:, (tt % 2) * 512:(tt % 2) * 512 + 512]
                    if tt >= 2:
                        wt("tensor", (s_ac, ev_g[tt % 2]))
                    do("tensor", "matmul", gps, t_ones[0:1, :],
                       t_bias4[0:1, 0:512], start=True, stop=False,
                       skip_group_check=True)
                    for k in range(NK):
                        lhsT = xn_fm[:, k * 1024 + tt * 128:k * 1024 + (tt + 1) * 128]
                        do("tensor", "matmul", gps, lhsT,
                           wsA[:, k * 512:(k + 1) * 512],
                           start=False, stop=(k == NK - 1), skip_group_check=True)
                    inc_last("tensor", (s_pe, 1))
                    if tt == 0 and hg > 0:
                        wt("scalar", (s_pe, s_pe.n), (s_dv, m_gate[hg - 1]))
                    else:
                        wt("scalar", (s_pe, s_pe.n))
                    do("scalar", "activation",
                       out=g_tok[:, tt * 512:(tt + 1) * 512],
                       in_=gps, func=AF.Sigmoid, inc=((s_ac, 1),))
                    ev_g[tt % 2] = s_ac.n
                    if fm_stage < 8:
                        if fm_stage == 0:
                            wts = [(s_dv, m_ktok[hg]), (s_dw, m_tables)]
                            if hg == 0:
                                wts.append((s_ac, m_gffn))
                            wt("vector", *wts)
                            rope_hb(k_tok, 0)
                        elif fm_stage == 1:
                            rope_hb(k_tok, 1)
                        elif fm_stage == 2:
                            elu_q(k_tok, 0)
                            elu_q(k_tok, 1)
                        elif fm_stage == 3:
                            elu_q(k_tok, 2)
                            elu_q(k_tok, 3)
                        elif fm_stage == 4:
                            wt("vector", (s_dv, m_qev))
                            rope_hb(qtok, 0)
                        elif fm_stage == 5:
                            rope_hb(qtok, 1)
                        elif fm_stage == 6:
                            elu_q(qtok, 0)
                            elu_q(qtok, 1)
                        else:
                            elu_q(qtok, 2)
                            elu_q(qtok, 3)
                        fm_stage += 1
                inc_last("tensor", (s_pe, 1))
                m_gqmm[hg] = s_pe.n
                m_fmk = 0
                m_fmq = s_dv.n

                if lvl == 2:
                    raise _Stop
                if lvl == 3:
                    raise _Stop

                wt("tensor", (s_dv, m_fmq))
                m_qT_ev = [0, 0]
                for tt in range(NT):
                    tp = tp_bf if tt % 2 == 0 else tp_bf2
                    if m_qT_ev[tt % 2]:
                        wt("tensor", (s_dv, m_qT_ev[tt % 2]))
                    for hh in range(HPG):
                        do("tensor", "transpose",
                           tp[:, hh * 128:(hh + 1) * 128],
                           qtok[:, tt * 512 + hh * 128:tt * 512 + (hh + 1) * 128],
                           t_id[:, :])
                    inc_last("tensor", (s_pe, 1))
                    wt("vector", (s_pe, s_pe.n))
                    do("vector", "tensor_copy",
                       q_fm.rearrange("p (h t) -> p h t", h=HPG)[
                           :, :, tt * 128:(tt + 1) * 128],
                       tp.rearrange("p (h t) -> p h t", h=HPG),
                       inc=((s_dv, 1),))
                    m_qT_ev[tt % 2] = s_dv.n
                m_qT = s_dv.n

                if hg + 1 < NHG:
                    wt("gpsimd", (s_pe, m_gqmm[hg]))
                    do("gpsimd", "dma_start",
                       out=t_bias4[0:1, 0:512],
                       in_=d_b["bk"][0:1, (hg + 1) * 512:(hg + 2) * 512],
                       inc=((s_dw, 16),))
                    m_bk = s_dw.n
                    do("gpsimd", "dma_start",
                       out=wsA.rearrange("p (k j) -> p k j", k=NK),
                       in_=wslab_ap("wv", hg + 1), inc=((s_dw, 16),))
                    m_wv = s_dw.n

                m_snap = {}
                m_numev = {}
                szh_buf = [t_szh, szh2]

                def emit_num(j):
                    jt, jhalf = j // 2, j % 2
                    jpb = jhalf * 64
                    wt("tensor", (s_dv, m_snap[j]))
                    if j == 0:
                        wt("tensor", (s_dv, m_qT))
                    if j >= 2:
                        wt("tensor", (s_dv, m_numev[j - 2]))
                    npair = (psw1[:, 1024:2048] if jhalf == 0
                             else psw0[:, 1024:2048])
                    szh = szh_buf[j % 2]
                    for hh in range(HPG):
                        hb, hl = hh // 2, hh % 2
                        do("tensor", "matmul",
                           npair[jpb:jpb + 64,
                                 hb * 512 + hl * 128:hb * 512 + hl * 128 + 128],
                           q_fm[:, hh * 1024 + j * 64:hh * 1024 + (j + 1) * 64],
                           szh[:, hh * 129:hh * 129 + 128],
                           start=(hl == 0), stop=False, skip_group_check=True,
                           tile_position=(0, jpb) if jpb else None)
                    for hh in range(HPG):
                        hb, hl = hh // 2, hh % 2
                        do("tensor", "matmul",
                           npair[jpb:jpb + 64,
                                 hb * 512 + 256 + hl:hb * 512 + 257 + hl],
                           q_fm[:, hh * 1024 + j * 64:hh * 1024 + (j + 1) * 64],
                           szh[:, hh * 129 + 128:hh * 129 + 129],
                           start=False, stop=(hl == 1), skip_group_check=True,
                           tile_position=(0, jpb) if jpb else None)
                    inc_last("tensor", (s_pe, 1))
                    wt("vector", (s_pe, s_pe.n))
                    n4 = npair.rearrange("p (b f) -> p b f", b=2)
                    dst = num_sb[jpb:jpb + 64,
                                 jt * 516:(jt + 1) * 516].rearrange(
                        "p (bh f) -> p bh f", f=129).rearrange(
                        "p (b h) f -> p b h f", b=2)
                    do("vector", "tensor_copy", dst[:, :, :, 0:128],
                       n4[jpb:jpb + 64, :, 0:256].rearrange(
                           "p b (h f) -> p b h f", h=2))
                    do("vector", "tensor_copy", dst[:, :, :, 128:129],
                       n4[jpb:jpb + 64, :, 256:258].rearrange(
                           "p b (h f) -> p b h f", h=2), inc=((s_dv, 1),))
                    m_numev[j] = s_dv.n

                for i in range(NCH):
                    tt, half = i // 2, i % 2
                    pb = half * 64
                    if i > 0:
                        wt("tensor", (s_dv, m_snap[i - 1]))
                    else:
                        wt("tensor", (s_dv, m_fmq))
                    for hh in range(HPG):
                        sbank = sA if hh < 2 else sB
                        hl = hh % 2
                        do("tensor", "matmul",
                           sbank[:, hl * 128:hl * 128 + 128],
                           k_tok[pb:pb + 64,
                                 tt * 512 + hh * 128:tt * 512 + (hh + 1) * 128],
                           v_tok[pb:pb + 64,
                                 tt * 512 + hh * 128:tt * 512 + (hh + 1) * 128],
                           start=(i == 0 and hl == 0), stop=False,
                           skip_group_check=True)
                    for hh in range(HPG):
                        sbank = sA if hh < 2 else sB
                        hl = hh % 2
                        do("tensor", "matmul", sbank[:, 256 + hl:257 + hl],
                           k_tok[pb:pb + 64,
                                 tt * 512 + hh * 128:tt * 512 + (hh + 1) * 128],
                           t_onescol[pb:pb + 64, 0:1],
                           start=False, stop=(i == NCH - 1 and hl == 1),
                           skip_group_check=True)
                    inc_last("tensor", (s_pe, 1))
                    m_mm1 = s_pe.n
                    wt("vector", (s_pe, m_mm1))
                    spair = psw1[:, 0:1024].rearrange("p (b f) -> p b f", b=2)
                    zdst = szh_buf[i % 2][:, 0:516].rearrange(
                        "p (bh f) -> p bh f", f=129).rearrange(
                        "p (b h) f -> p b h f", b=2)
                    do("vector", "tensor_copy", zdst[:, :, :, 0:128],
                       spair[:, :, 0:256].rearrange("p b (h f) -> p b h f", h=2))
                    do("vector", "tensor_copy", zdst[:, :, :, 128:129],
                       spair[:, :, 256:258].rearrange("p b (h f) -> p b h f", h=2),
                       inc=((s_dv, 1),))
                    m_snap[i] = s_dv.n
                    if i == NCH - 1:
                        t_sf16 = gath[:, 0:516]
                        fdst = t_sf16.rearrange(
                            "p (bh f) -> p bh f", f=129).rearrange(
                            "p (b h) f -> p b h f", b=2)
                        do("vector", "tensor_copy", fdst[:, :, :, 0:128],
                           spair[:, :, 0:256].rearrange(
                               "p b (h f) -> p b h f", h=2))
                        do("vector", "tensor_copy", fdst[:, :, :, 128:129],
                           spair[:, :, 256:258].rearrange(
                               "p b (h f) -> p b h f", h=2), inc=((s_dv, 1),))
                        m_sfin = s_dv.n
                        wt("sync", (s_dv, m_sfin))
                        do("sync", "dma_start", out=d_ccs[hg][:, :],
                           in_=t_sf16, inc=((s_dx, 16),))
                        m_stg = s_dx.n
                        wt("gpsimd", (s_dx, m_stg))
                        do("gpsimd", "collective_compute", "AllGather",
                           ALU.bypass, replica_groups=[list(range(NCORE))],
                           ins=[d_ccs[hg][:, :]], outs=[d_ccd[hg][:, :]],
                           inc=((s_cc, 1),))
                    if hg + 1 < NHG and 1 <= i <= 8:
                        emit_k_group(hg + 1, i - 1, m_wkb, m_bk)
                        if i == 8:
                            inc_last("tensor", (s_pe, 1))
                            m_kmm = s_pe.n
                            m_ktok[hg + 1] = ev_k[hg + 1][NT - 1]
                            wt("gpsimd", (s_pe, m_kmm))
                            do("gpsimd", "dma_start",
                               out=wsB.rearrange("p (k j) -> p k j", k=NK),
                               in_=wslab_ap("wq", hg + 1), inc=((s_dw, 16),))
                            do("gpsimd", "dma_start",
                               out=t_bias4[0:1, 0:512],
                               in_=d_b["bg"][0:1,
                                             (hg + 1) * 512:(hg + 2) * 512],
                               inc=((s_dw, 16),))
                            m_wq = s_dw.n
                    if hg + 1 < NHG and 9 <= i <= 15:
                        emit_v_group(hg + 1, i - 9, m_wv)
                    if i > 0:
                        emit_num(i - 1)
                if hg + 1 < NHG:
                    emit_v_group(hg + 1, 7, m_wv)
                    inc_last("tensor", (s_pe, 1))
                    m_vmm = s_pe.n
                emit_num(NCH - 1)
                m_scan = s_pe.n
                if hg + 1 < NHG:
                    wt("gpsimd", (s_pe, m_vmm))
                    do("gpsimd", "dma_start",
                       out=t_bias4[0:1, 512:1024],
                       in_=d_b["bq"][0:1, (hg + 1) * 512:(hg + 2) * 512],
                       inc=((s_dw, 16),))
                    m_bq = s_dw.n
                    do("gpsimd", "dma_start",
                       out=wsA.rearrange("p (k j) -> p k j", k=NK),
                       in_=wslab_ap("wg", hg + 1), inc=((s_dw, 16),))
                    m_wg = s_dw.n

                if lvl == 4:
                    raise _Stop

                if hg + 1 < NHG:
                    m_qmm, m_qev = emit_q_proj(hg + 1, m_wq, m_bq,
                                               m_numev[NCH - 1])
                wt("sync", (s_cc, s_cc.n), (s_dv, m_fmq))
                do("sync", "dma_start",
                   out=gath[:, 0:2064].rearrange("p (s f) -> p s f", s=4),
                   in_=d_ccd[hg].rearrange("(u v p) f -> p u v f", v=2, p=128)[
                       :, :, 0, :],
                   inc=((s_dx, 16),))
                m_gath = s_dx.n
                wt("vector", (s_dx, m_gath), (s_pe, m_scan))
                do("vector", "tensor_tensor",
                   gath[:, 0:2064].rearrange("p (s f) -> p s f", s=4),
                   gath[:, 0:2064].rearrange("p (s f) -> p s f", s=4),
                   t_msk[:, :].rearrange("p (u v) -> p u v", v=2)[
                       :, :, 0].unsqueeze(2).broadcast_to([128, 4, 516]),
                   ALU.mult)
                do("vector", "tensor_tensor", gath[:, 0:1032], gath[:, 0:1032],
                   gath[:, 1032:2064], ALU.add)
                do("vector", "tensor_tensor", t_szh[:, 0:516], gath[:, 0:516],
                   gath[:, 516:1032], ALU.add, inc=((s_dv, 1),))
                m_szh = s_dv.n

                if lvl == 5:
                    raise _Stop

                gtp = [psw1[:, 0:256].bitcast(BF16),
                       psw1[:, 512:768].bitcast(BF16)]
                atok2 = [t_atok[:, 0:512], view(OFF_GATH + 4128, 512, BF16)]
                c129_b = [t_c129[:, 0:516], view(OFF_GATH + 5152, 516, F32)]
                m_c1war = m_qev if hg + 1 < NHG else m_numev[NCH - 1]
                m_corr = {}
                m_cfree = {}
                m_atok = {}
                m_attcopy = {}

                def emit_corr(tt):
                    if tt == 0:
                        wt("tensor", (s_dv, m_szh))
                    elif tt == 1:
                        wt("tensor", (s_dv, m_c1war))
                    else:
                        wt("tensor", (s_ac, m_cfree[tt - 2]))
                    cpair = (psw1[:, 1024:2048] if tt % 2 == 0
                             else psw0[:, 1024:2048])
                    for hh in range(HPG):
                        hb, hl = hh // 2, hh % 2
                        do("tensor", "matmul",
                           cpair[:, hb * 512 + hl * 128:hb * 512 + hl * 128 + 128],
                           q_fm[:, hh * 1024 + tt * 128:hh * 1024 + (tt + 1) * 128],
                           t_szh[:, hh * 129:hh * 129 + 128],
                           start=(hl == 0), stop=False, skip_group_check=True)
                    for hh in range(HPG):
                        hb, hl = hh // 2, hh % 2
                        do("tensor", "matmul",
                           cpair[:, hb * 512 + 256 + hl:hb * 512 + 257 + hl],
                           q_fm[:, hh * 1024 + tt * 128:hh * 1024 + (tt + 1) * 128],
                           t_szh[:, hh * 129 + 128:hh * 129 + 129],
                           start=False, stop=False, skip_group_check=True)
                    for hb in range(2):
                        do("tensor", "matmul",
                           cpair[:, hb * 512 + 256:hb * 512 + 258],
                           t_ones[0:1, :], t_eps[0:1, :],
                           start=False, stop=True, skip_group_check=True)
                    inc_last("tensor", (s_pe, 1))
                    m_corr[tt] = s_pe.n
                    return cpair

                def emit_gate_dve(tt, cpair):
                    c129 = c129_b[tt % 2].rearrange(
                        "p (bh f) -> p bh f", f=129).rearrange(
                        "p (b h) f -> p b h f", b=2)
                    c4 = cpair.rearrange("p (b f) -> p b f", b=2)
                    wt("scalar", (s_pe, m_corr[tt]))
                    do("scalar", "copy", c129[:, :, :, 128:129],
                       c4[:, :, 256:258].rearrange("p b (h f) -> p b h f", h=2))
                    do("scalar", "copy", c129[:, :, :, 0:128],
                       c4[:, :, 0:256].rearrange("p b (h f) -> p b h f", h=2),
                       inc=((s_ac, 1),))
                    m_cfree[tt] = s_ac.n
                    t129 = t_sfin[:, 0:516].rearrange(
                        "p (bh f) -> p bh f", f=129).rearrange(
                        "p (b h) f -> p b h f", b=2)
                    wt("vector", (s_ac, m_cfree[tt]))
                    do("vector", "tensor_tensor", t129,
                       num_sb[:, tt * 516:(tt + 1) * 516].rearrange(
                           "p (bh f) -> p bh f", f=129).rearrange(
                           "p (b h) f -> p b h f", b=2),
                       c129, ALU.add, inc=((s_dv, 1),))
                    wt("scalar", (s_dv, s_dv.n))
                    do("scalar", "copy", t_dtmp[:, :],
                       t129[:, :, :, 128:129].rearrange("p b h f -> p (b h f)"),
                       inc=((s_ac, 1),))
                    m_den = s_ac.n
                    a4 = atok2[tt % 2].rearrange(
                        "p (b h f) -> p b h f", b=2, h=2)
                    g4 = g_tok[:, tt * 512:(tt + 1) * 512].rearrange(
                        "p (b h f) -> p b h f", b=2, h=2)
                    do("vector", "tensor_tensor", a4[:, 0:1],
                       t129[:, 0:1, :, 0:128], g4[:, 0:1], ALU.mult)
                    wt("vector", (s_ac, m_den))
                    do("vector", "reciprocal", t_rden[:, :], t_dtmp[:, :])
                    do("vector", "tensor_tensor", a4[:, 1:2],
                       t129[:, 1:2, :, 0:128], g4[:, 1:2], ALU.mult)
                    do("vector", "tensor_tensor", a4, a4,
                       t_rden[:, :].rearrange("p (b h) -> p b h", b=2).unsqueeze(
                           3).broadcast_to([128, 2, 2, 128]), ALU.mult,
                       inc=((s_dv, 1),))
                    m_atok[tt] = s_dv.n

                def emit_attnT(tt):
                    wts = [(s_dv, m_atok[tt])]
                    if tt >= 2:
                        wts.append((s_dv, m_attcopy[tt - 2]))
                    wt("tensor", *wts)
                    tp = gtp[tt % 2]
                    src = atok2[tt % 2]
                    for hh in range(HPG):
                        do("tensor", "transpose",
                           tp[:, hh * 128:(hh + 1) * 128],
                           src[:, hh * 128:(hh + 1) * 128], t_id[:, :])
                    inc_last("tensor", (s_pe, 1))
                    wt("vector", (s_pe, s_pe.n))
                    do("vector", "tensor_copy",
                       attn_fm.rearrange("p (a t) -> p a t", a=NK)[
                           :, hg * 4:(hg + 1) * 4, tt * 128:(tt + 1) * 128],
                       tp.rearrange("p (a t) -> p a t", a=HPG),
                       inc=((s_dv, 1),))
                    m_attcopy[tt] = s_dv.n

                emit_gate_dve(0, emit_corr(0))
                for tt in range(1, NT):
                    emit_gate_dve(tt, emit_corr(tt))
                    emit_attnT(tt - 1)
                emit_attnT(NT - 1)
                m_gate[hg] = m_attcopy[NT - 1]

            if lvl < 7:
                raise _Stop


            wo_h = [view(OFF_WO, NK * 1024, BF16), view(OFF_XN, NK * 1024, BF16)]
            xld = [view(OFF_P3X0, 1024, F32), view(OFF_P3X1, 1024, F32)]
            ost = [view(OFF_P3O0, 1024, F32), view(OFF_P3O1, 1024, F32)]

            def wohalf_ap(hf):
                return d_w["wo"][:, hf * 1024:(hf + 1) * 1024].rearrange(
                    "(k p) j -> p k j", p=128)

            wt("gpsimd", (s_pe, m_gqmm[NHG - 1]))
            do("gpsimd", "dma_start", out=t_bias4[0:1, :],
               in_=d_b["bo"][0:1, 0:1024], inc=((s_dw, 16),))
            m_bo = [s_dw.n]
            do("gpsimd", "dma_start",
               out=wo_h[0].rearrange("p (k j) -> p k j", k=NK), in_=wohalf_ap(0),
               inc=((s_dw, 16),))
            m_wo = [s_dw.n]
            do("gpsimd", "dma_start",
               out=wo_h[1].rearrange("p (k j) -> p k j", k=NK), in_=wohalf_ap(1),
               inc=((s_dw, 16),))
            m_wo.append(s_dw.n)

            m_comb = {}
            m_p3mm = {}
            for hf in range(2):
                wo_half = wo_h[hf]
                wt("tensor", (s_dw, m_wo[hf]))
                if hf == 0:
                    wt("tensor", (s_dv, m_gate[NHG - 1]))
                for tt in range(NT):
                    ps = psw0 if tt % 2 == 0 else psw1
                    pshalf = ps[:, 0:1024]
                    prev = None
                    if tt >= 2:
                        prev = m_comb[(hf, tt - 2)]
                    elif hf == 1:
                        prev = m_comb[(0, NT - 2 + tt)]
                    if prev is not None:
                        wt("tensor", (s_dv, prev))
                    for k in range(NK):
                        lhsT = attn_fm[:, k * 1024 + tt * 128:k * 1024 + (tt + 1) * 128]
                        for jc in range(2):
                            do("tensor", "matmul",
                               pshalf[:, jc * 512:(jc + 1) * 512], lhsT,
                               wo_half[:, k * 1024 + jc * 512:k * 1024 + (jc + 1) * 512],
                               start=(k == 0), stop=False,
                               skip_group_check=True)
                    if tt == 0:
                        wt("tensor", (s_dw, m_bo[hf]))
                    for jc in range(2):
                        do("tensor", "matmul", pshalf[:, jc * 512:(jc + 1) * 512],
                           t_ones[0:1, :],
                           t_bias4[0:1, jc * 512:(jc + 1) * 512],
                           start=False, stop=True, skip_group_check=True)
                    inc_last("tensor", (s_pe, 1))
                    m_smm = s_pe.n
                    m_p3mm[(hf, tt)] = m_smm
                    if prev is not None:
                        wt("sync", (s_dv, prev))
                    elif hf == 0 and tt == 0:
                        wt("sync", (s_dv, m_gate[NHG - 1]))
                    do("sync", "dma_start", out=xld[tt % 2],
                       in_=d_x[tt * 128:(tt + 1) * 128, hf * 1024:(hf + 1) * 1024],
                       inc=((s_dx, 16),))
                    m_xr = s_dx.n
                    wt("vector", (s_pe, m_smm), (s_dx, m_xr))
                    if hf == 0 and tt == 0:
                        wt("vector", (s_ac, m_gffn))
                    do("vector", "tensor_tensor", ost[tt % 2], pshalf,
                       gffn[:, tt * 2048 + hf * 1024:tt * 2048 + (hf + 1) * 1024],
                       ALU.mult)
                    do("vector", "tensor_tensor", ost[tt % 2], ost[tt % 2],
                       xld[tt % 2], ALU.add, inc=((s_dv, 1),))
                    m_comb[(hf, tt)] = s_dv.n
                    wt("sync", (s_dv, m_comb[(hf, tt)]))
                    do("sync", "dma_start",
                       out=d_y[tt * 128:(tt + 1) * 128, hf * 1024:(hf + 1) * 1024],
                       in_=ost[tt % 2], inc=((s_out, 16),))
                if hf == 0:
                    wt("gpsimd", (s_pe, m_p3mm[(0, NT - 1)]))
                    do("gpsimd", "dma_start", out=t_bias4[0:1, :],
                       in_=d_b["bo"][0:1, 1024:2048], inc=((s_dw, 16),))
                    m_bo.append(s_dw.n)


        except _Stop:
            pass

        dbg_srcs = {
            "xn_fm": xn_fm, "gffn": gffn, "ga": ga, "k_tok": k_tok,
            "v_tok": v_tok, "g_tok": g_tok, "q_fm": q_fm, "num_sb": num_sb,
            "den": t_den[:, :], "attn_fm": attn_fm, "szh": t_szh[:, :],
            "stats": t_stats[:, :], "xns1": xns[1], "xb1": xb[1],
        }
        if debug:
            dchunk = view(OFF_BIG, 2048, F32)
            for (nm, shape) in debug:
                src_ap = dbg_srcs[nm]
                n = src_ap.shape[1]
                for c0 in range(0, n, 2048):
                    c1 = min(c0 + 2048, n)
                    wt("vector", (s_dv, s_dv.n), (s_ac, s_ac.n),
                       (s_pe, s_pe.n), (s_dx, s_dx.n), (s_dw, s_dw.n))
                    do("vector", "tensor_copy", dchunk[:, 0:c1 - c0],
                       src_ap[:, c0:c1], inc=((s_dv, 1),))
                    wt("gpsimd", (s_dv, s_dv.n))
                    do("gpsimd", "dma_start", out=d_dbg[nm][:, c0:c1],
                       in_=dchunk[:, 0:c1 - c0], inc=((s_dw, 16),))

        wt("sync", (s_out, s_out.n))
        wt("gpsimd", (s_dw, s_dw.n))

        with nc.Block() as block:
            @block.tensor
            def _(t):
                for f in P["tensor"]:
                    f(t)

            @block.vector
            def _(v):
                for f in P["vector"]:
                    f(v)

            @block.scalar
            def _(sc):
                for f in P["scalar"]:
                    f(sc)

            @block.gpsimd
            def _(g):
                for f in P["gpsimd"]:
                    f(g)

            @block.sync
            def _(sy):
                for f in P["sync"]:
                    f(sy)

    return nc


def _perm_deint():
    p = np.zeros(D, np.int64)
    for h in range(H):
        b = h * DK
        for i in range(DK // 2):
            p[b + i] = b + 2 * i
            p[b + 64 + i] = b + 2 * i + 1
    return p


def _rope_tables():
    inv = (1.0 / 10000.0 ** (np.arange(0, DK, 2, dtype=np.float32)
                             / np.float32(DK))).astype(np.float32)
    pos = np.arange(2048, dtype=np.float32)
    ang = pos[:, None] * inv[None, :]
    return np.sin(ang).astype(np.float32), np.cos(ang).astype(np.float32)


_DEBUG = ()


def kernel(L0_attn_x, L0_attn_Wq, L0_attn_bq, L0_attn_Wk, L0_attn_bk,
           L0_attn_Wv, L0_attn_bv, L0_attn_Wg, L0_attn_bg, L0_attn_Wo,
           L0_attn_bo, L0_ffn_up_weight, L0_ffn_down_weight, L0_ffn_up_scale,
           L0_ffn_down_scale, L0_ffn_output_scale):
    x = np.ascontiguousarray(
        np.asarray(L0_attn_x, np.float32).reshape(4 * 2048, D))
    p = _perm_deint()
    wq = np.ascontiguousarray(np.asarray(L0_attn_Wq, np.float32).T[:, p])
    wk = np.ascontiguousarray(np.asarray(L0_attn_Wk, np.float32).T[:, p])
    wv = np.ascontiguousarray(np.asarray(L0_attn_Wv, np.float32).T)
    wg = np.ascontiguousarray(np.asarray(L0_attn_Wg, np.float32).T)
    wo = np.ascontiguousarray(np.asarray(L0_attn_Wo, np.float32).T)
    upw = np.ascontiguousarray(np.asarray(L0_ffn_up_weight, np.float32).T)
    dnw = np.ascontiguousarray(np.asarray(L0_ffn_down_weight, np.float32).T)
    bq = np.ascontiguousarray(np.asarray(L0_attn_bq, np.float32)[p][None, :])
    bk = np.ascontiguousarray(np.asarray(L0_attn_bk, np.float32)[p][None, :])
    bv = np.ascontiguousarray(np.asarray(L0_attn_bv, np.float32)[None, :])
    bg = np.ascontiguousarray(np.asarray(L0_attn_bg, np.float32)[None, :])
    bo = np.ascontiguousarray(np.asarray(L0_attn_bo, np.float32)[None, :])
    usc = np.ascontiguousarray(
        np.asarray(L0_ffn_up_scale, np.float32).reshape(64, 128).T)
    dsc = np.ascontiguousarray(
        np.asarray(L0_ffn_down_scale, np.float32)[None, :])
    osc = np.ascontiguousarray(
        np.asarray(L0_ffn_output_scale, np.float32).reshape(1, 1))
    sin_f, cos_f = _rope_tables()
    ident = np.eye(128, dtype=np.float32)

    nc = build_nc(debug=_DEBUG)

    in_maps = []
    for c in range(NCORE):
        half = c % 2
        t0 = half * 1024
        mask = np.zeros((128, 8), np.float32)
        if half == 1:
            mask[:, c - 1] = 1.0
        in_maps.append({
            "x": np.ascontiguousarray(x[c * TPC:(c + 1) * TPC]),
            "wq": wq, "wk": wk, "wv": wv, "wg": wg, "wo": wo,
            "upw": upw, "dnw": dnw,
            "bq": bq, "bk": bk, "bv": bv, "bg": bg, "bo": bo,
            "usc": usc, "dsc": dsc, "osc": osc,
            "rsin": np.ascontiguousarray(sin_f[t0:t0 + 1024]),
            "rcos": np.ascontiguousarray(cos_f[t0:t0 + 1024]),
            "ident": ident,
            "mask8": mask,
        })

    res = run_bass_kernel_spmd(nc, in_maps, core_ids=list(range(NCORE)))
    out = np.concatenate([res.results[c]["y"] for c in range(NCORE)], axis=0)
    if _DEBUG:
        kernel._dbg = [{nm: res.results[c]["dbg_" + nm] for (nm, _s) in _DEBUG}
                       for c in range(NCORE)]
    return out.reshape(4, 2048, D)


# version 13
# speedup vs baseline: 1.3520x; 1.0147x over previous
import numpy as np
from contextlib import ExitStack

import concourse.bass as bass
import concourse.mybir as mybir
import concourse.bass_utils as _bu
from concourse.bass_utils import run_bass_kernel_spmd

if not getattr(_bu, "_ldw_patched", False):
    _orig_run_command = _bu.run_command

    def _run_command_ldw(cmd, *a, **kw):
        if isinstance(cmd, list):
            cmd = ["--enable-ldw-opt=true" if c == "--enable-ldw-opt=false"
                   else c for c in cmd]
        return _orig_run_command(cmd, *a, **kw)

    _bu.run_command = _run_command_ldw
    _bu._ldw_patched = True

F32 = mybir.dt.float32
BF16 = mybir.dt.bfloat16
U8 = mybir.dt.uint8
AF = mybir.ActivationFunctionType
ALU = mybir.AluOpType
AX = mybir.AxisListType

D = 2048
H = 16
DK = 128
CS = 64
TPC = 1024
NT = 8
NK = 16
NCH = 16
NSLAB = 8
DH = 8192
NHG = 4
HPG = 4
NCORE = 8

OFF_XN = 0
OFF_GA = 32768
OFF_ATTN = 32768
OFF_K = (65536, 65536 + 8192)
OFF_V = (65536 + 16384, 65536 + 24576)
OFF_G = 65536 + 32768
OFF_Q = 65536 + 40960
OFF_BIG = 65536 + 49152
BIG_SZ = 82304
ARENA_BYTES = OFF_BIG + BIG_SZ

OFF_XB0 = OFF_BIG + 32768
OFF_XB1 = OFF_BIG + 40960
OFF_X2 = OFF_BIG + 49152
OFF_XNS0 = OFF_BIG + 57344
OFF_XNS1 = OFF_BIG + 61440
OFF_UPS = OFF_BIG
OFF_DNS = OFF_BIG + 32768
OFF_HID = OFF_BIG + 65536
OFF_DSREP = OFF_BIG + 65536
OFF_DSROW = OFF_BIG + 73856
OFF_WSA = OFF_BIG
OFF_WSB = OFF_BIG + 16384
OFF_GFFN = OFF_BIG + 32768
OFF_QTMP = OFF_BIG + 65536
OFF_NUMSB = OFF_BIG + 65536
OFF_RT = OFF_BIG + 73856
OFF_GATH = OFF_BIG + 73856
OFF_WO = OFF_BIG
OFF_P3X0 = OFF_BIG + 65536
OFF_P3X1 = OFF_BIG + 69632
OFF_P3O0 = OFF_BIG + 73856
OFF_P3O1 = OFF_BIG + 77952

ENGS = ("tensor", "vector", "scalar", "gpsimd", "sync")


class _Stop(Exception):
    pass


class Sem:
    __slots__ = ("h", "n")

    def __init__(self, h):
        self.h = h
        self.n = 0


def build_nc(debug=(), upto="full"):
    order = ("p0", "p1", "p2proj", "p2fm", "p2scan", "p2cc", "p2", "full")
    lvl = order.index(upto)
    nc = bass.Bass("TRN2", target_bir_lowering=False,
                   detect_race_conditions=False)

    d_x = nc.dram_tensor("x", [TPC, D], F32, kind="ExternalInput").ap()
    d_w = {nm: nc.dram_tensor(nm, [D, D], F32, kind="ExternalInput").ap()
           for nm in ("wq", "wk", "wv", "wg", "wo")}
    d_up = nc.dram_tensor("upw", [D, DH], F32, kind="ExternalInput").ap()
    d_dn = nc.dram_tensor("dnw", [DH, D], F32, kind="ExternalInput").ap()
    d_b = {nm: nc.dram_tensor(nm, [1, D], F32, kind="ExternalInput").ap()
           for nm in ("bq", "bk", "bv", "bg", "bo")}
    d_usc = nc.dram_tensor("usc", [128, 64], F32, kind="ExternalInput").ap()
    d_dsc = nc.dram_tensor("dsc", [1, D], F32, kind="ExternalInput").ap()
    d_osc = nc.dram_tensor("osc", [1, 1], F32, kind="ExternalInput").ap()
    d_sin = nc.dram_tensor("rsin", [TPC, 64], F32, kind="ExternalInput").ap()
    d_cos = nc.dram_tensor("rcos", [TPC, 64], F32, kind="ExternalInput").ap()
    d_id = nc.dram_tensor("ident", [128, 128], F32, kind="ExternalInput").ap()
    d_msk = nc.dram_tensor("mask8", [128, 8], F32, kind="ExternalInput").ap()
    d_y = nc.dram_tensor("y", [TPC, D], F32, kind="ExternalOutput").ap()
    if lvl >= 5:
        d_ccs = [nc.dram_tensor(f"cc_src{h}", [128, 516], BF16)
                 for h in range(NHG)]
        d_ccd = [nc.dram_tensor(f"cc_dst{h}", [128 * NCORE, 516], BF16,
                                addr_space="Shared") for h in range(NHG)]
        d_wus = nc.dram_tensor("cc_wsrc", [1, 16], BF16)
        d_wud = nc.dram_tensor("cc_wdst", [NCORE, 16], BF16,
                               addr_space="Shared")
    d_dbg = {nm: nc.dram_tensor("dbg_" + nm, list(shape), F32,
                                kind="ExternalOutput").ap()
             for (nm, shape) in debug}

    P = {e: [] for e in ENGS}

    def do(eng, method, *a, inc=(), **kw):
        if method == "dma_start":
            for s, k in inc:
                if k == 16 and s.n:
                    wt(eng, (s, s.n))
        incs = [(s.h, k) for (s, k) in inc]

        def f(E, m=method, a=a, kw=kw, incs=incs):
            ins = getattr(E, m)(*a, **kw)
            for hh, kk in incs:
                ins = ins.then_inc(hh, kk)
        f._incs = incs
        P[eng].append(f)
        for s, k in inc:
            s.n += k

    def inc_last(eng, *pairs):
        f = P[eng][-1]
        for s, k in pairs:
            for idx, (hh, kk) in enumerate(f._incs):
                if hh is s.h:
                    f._incs[idx] = (hh, kk + k)
                    break
            else:
                f._incs.append((s.h, k))
            s.n += k

    def wt(eng, *pairs):
        ps = tuple((s.h, v) for (s, v) in pairs)

        def f(E, ps=ps):
            for hh, vv in ps:
                E.wait_ge(hh, vv)
        P[eng].append(f)

    with ExitStack() as stk:
        arena = stk.enter_context(nc.sbuf_tensor("arena", [128, ARENA_BYTES], U8))
        t_sin = stk.enter_context(nc.sbuf_tensor("t_sin", [128, NT * 64], F32))
        t_cos = stk.enter_context(nc.sbuf_tensor("t_cos", [128, NT * 64], F32))
        t_usc = stk.enter_context(nc.sbuf_tensor("t_usc", [128, 64], F32))
        t_id = stk.enter_context(nc.sbuf_tensor("t_id", [128, 128], BF16))
        t_ones = stk.enter_context(nc.sbuf_tensor("t_ones", [1, 128], BF16))
        t_ones32 = stk.enter_context(nc.sbuf_tensor("t_ones32", [1, 128], F32))
        t_onescol = stk.enter_context(nc.sbuf_tensor("t_onescol", [128, 1], BF16))
        t_bias4 = stk.enter_context(nc.sbuf_tensor("t_bias4", [1, 1024], BF16))
        t_cst = stk.enter_context(nc.sbuf_tensor("t_cst", [128, 4], F32))
        t_stats = stk.enter_context(nc.sbuf_tensor("t_stats", [128, 8], F32))
        t_msk = stk.enter_context(nc.sbuf_tensor("t_msk", [128, 8], F32))
        t_den = stk.enter_context(nc.sbuf_tensor("t_den", [128, NT * HPG], F32))
        t_dtmp = stk.enter_context(nc.sbuf_tensor("t_dtmp", [128, HPG], F32))
        t_rden = stk.enter_context(nc.sbuf_tensor("t_rden", [128, HPG], F32))
        t_szh = stk.enter_context(nc.sbuf_tensor("t_szh", [128, 516], BF16))
        t_sfin = stk.enter_context(nc.sbuf_tensor("t_sfin", [128, 516], F32))
        t_atok = stk.enter_context(nc.sbuf_tensor("t_atok", [128, 512], BF16))
        t_c129 = stk.enter_context(nc.sbuf_tensor("t_c129", [128, 520], F32))
        t_os = stk.enter_context(nc.sbuf_tensor("t_os", [1, 1], F32))
        t_os128 = stk.enter_context(nc.sbuf_tensor("t_os128", [1, 128], F32))
        t_eps = stk.enter_context(nc.sbuf_tensor("t_eps", [1, 2], BF16))

        psw0 = stk.enter_context(nc.psum_tensor("psw0", [128, 2048], F32))
        psw1 = stk.enter_context(nc.psum_tensor("psw1", [128, 2048], F32))

        s_dx = Sem(stk.enter_context(nc.semaphore("s_dx")))
        s_dw = Sem(stk.enter_context(nc.semaphore("s_dw")))
        s_out = Sem(stk.enter_context(nc.semaphore("s_out")))
        s_pe = Sem(stk.enter_context(nc.semaphore("s_pe")))
        s_dv = Sem(stk.enter_context(nc.semaphore("s_dv")))
        s_ac = Sem(stk.enter_context(nc.semaphore("s_ac")))
        s_gp = Sem(stk.enter_context(nc.semaphore("s_gp")))
        s_cc = Sem(stk.enter_context(nc.semaphore("s_cc")))

        def view(off, nelem, dt):
            sz = 4 if dt == F32 else 2
            return arena[:, off:off + nelem * sz].bitcast(dt)

        xn_fm = view(OFF_XN, NK * 1024, BF16)
        ga = view(OFF_GA, NT * 2048, F32)
        attn_fm = view(OFF_ATTN, NK * 1024, BF16)
        k_tok2 = [view(OFF_K[0], NT * 512, BF16), view(OFF_K[1], NT * 512, BF16)]
        v_tok2 = [view(OFF_V[0], NT * 512, BF16), view(OFF_V[1], NT * 512, BF16)]
        g_tok = view(OFF_G, NT * 512, BF16)
        q_fm = view(OFF_Q, HPG * 1024, BF16)
        gffn = view(OFF_GFFN, NT * 2048, BF16)
        up_sgn = view(OFF_UPS, NK * 1024, BF16)
        dn_sgn = view(OFF_DNS, 8 * 2048, BF16)
        hid = view(OFF_HID, 8 * 1024, BF16)
        wsA = view(OFF_WSA, NK * 512, BF16)
        wsB = view(OFF_WSB, NK * 512, BF16)
        qtmp = view(OFF_QTMP, NT * 512, BF16)
        ropeT = [view(OFF_RT + i * 2048, 1024, BF16) for i in range(4)]
        num_sb = view(OFF_NUMSB, NT * 516, BF16)
        gath = view(OFF_GATH, 8 * 516, BF16)
        szh2 = t_c129[:, 0:258].bitcast(BF16)

        try:
            do("gpsimd", "memset", t_ones[:, :], 1.0)
            do("gpsimd", "memset", t_ones32[:, :], 1.0)
            do("gpsimd", "memset", t_onescol[:, :], 1.0)
            do("gpsimd", "memset", t_cst[:, 0:1], 1e-5)
            do("gpsimd", "memset", t_cst[:, 1:2], -0.5)
            do("gpsimd", "memset", t_cst[:, 2:3], 1.0 / D)
            do("gpsimd", "memset", t_eps[:, :], 1e-6, inc=((s_gp, 1),))
            m_pre = s_gp.n
            do("gpsimd", "dma_start",
               out=t_sin[:, :].rearrange("p (t f) -> p t f", t=NT),
               in_=d_sin.rearrange("(t p) f -> p t f", p=128), inc=((s_dw, 16),))
            do("gpsimd", "dma_start",
               out=t_cos[:, :].rearrange("p (t f) -> p t f", t=NT),
               in_=d_cos.rearrange("(t p) f -> p t f", p=128), inc=((s_dw, 16),))
            do("gpsimd", "dma_start", out=t_usc[:, :], in_=d_usc, inc=((s_dw, 16),))
            do("gpsimd", "dma_start", out=t_id[:, :], in_=d_id, inc=((s_dw, 16),))
            do("gpsimd", "dma_start", out=t_msk[:, :], in_=d_msk, inc=((s_dw, 16),))
            do("gpsimd", "dma_start", out=t_os[:, :], in_=d_osc, inc=((s_dw, 16),))
            m_tables = s_dw.n
            do("gpsimd", "dma_start", out=d_wus[:, :], in_=t_ones[0:1, 0:16],
               inc=((s_dw, 16),))
            wt("gpsimd", (s_dw, s_dw.n))
            do("gpsimd", "collective_compute", "AllGather", ALU.bypass,
               replica_groups=[list(range(NCORE))],
               ins=[d_wus[:, :]], outs=[d_wud[:, :]], inc=((s_cc, 1),))

            def upslab_ap(s):
                return d_up[:, s * 1024:(s + 1) * 1024].rearrange(
                    "(k p) h -> p k h", p=128)

            def dnslab_ap(s):
                return d_dn[s * 1024:(s + 1) * 1024, :].rearrange(
                    "(t p) j -> p t j", p=128)

            do("gpsimd", "dma_start",
               out=up_sgn.rearrange("p (k h) -> p k h", k=NK), in_=upslab_ap(0),
               inc=((s_dw, 16),))
            m_updma = [s_dw.n]

            xb = [view(OFF_XB0, 2048, F32), view(OFF_XB1, 2048, F32)]
            x2 = view(OFF_X2, 2048, F32)
            xns = [view(OFF_XNS0, 2048, BF16), view(OFF_XNS1, 2048, BF16)]
            m_xfree = [0] * NT
            m_tp_ev = [0, 0]

            for j in range(NT):
                if j >= 2:
                    wt("sync", (s_dv, m_xfree[j - 2]))
                do("sync", "dma_start", out=xb[j % 2],
                   in_=d_x[j * 128:(j + 1) * 128, :], inc=((s_dx, 16),))
                m_xdma = s_dx.n
                wt("vector", (s_dx, m_xdma))
                do("vector", "reduce_sum", t_stats[:, 0:1], xb[j % 2], axis=AX.X,
                   inc=((s_dv, 1),))
                wt("scalar", (s_dv, s_dv.n), (s_gp, m_pre))
                do("scalar", "activation", out=t_stats[:, 4:5],
                   in_=t_stats[:, 0:1], func=AF.Copy, scale=t_cst[:, 2:3],
                   inc=((s_ac, 1),))
                wt("vector", (s_ac, s_ac.n))
                do("vector", "tensor_scalar", xb[j % 2], xb[j % 2],
                   t_stats[:, 4:5], None, ALU.subtract, inc=((s_dv, 1),))
                wt("scalar", (s_dv, s_dv.n))
                do("scalar", "activation", out=x2, in_=xb[j % 2], func=AF.Square,
                   inc=((s_ac, 1),))
                wt("vector", (s_ac, s_ac.n))
                do("vector", "reduce_sum", t_stats[:, 1:2], x2, axis=AX.X,
                   inc=((s_dv, 1),))
                wt("scalar", (s_dv, s_dv.n))
                do("scalar", "activation", out=t_stats[:, 3:4], in_=t_stats[:, 1:2],
                   func=AF.Ln, scale=t_cst[:, 2:3], bias=t_cst[:, 0:1])
                do("scalar", "activation", out=t_stats[:, 3:4], in_=t_stats[:, 3:4],
                   func=AF.Exp, scale=t_cst[:, 1:2], inc=((s_ac, 1),))
                wt("vector", (s_ac, s_ac.n))
                do("vector", "tensor_scalar", xns[j % 2], xb[j % 2],
                   t_stats[:, 3:4], None, ALU.mult, inc=((s_dv, 1),))
                m_xn_j = s_dv.n
                m_xfree[j] = m_xn_j
                wt("tensor", (s_dv, m_xn_j), (s_dw, m_tables))
                for kq in range(4):
                    pbank = psw0[:, (kq % 2) * 512:(kq % 2) * 512 + 256].bitcast(BF16)
                    if m_tp_ev[kq % 2]:
                        wt("tensor", (s_dv, m_tp_ev[kq % 2]))
                    for kk in range(4):
                        k = kq * 4 + kk
                        do("tensor", "transpose", pbank[:, kk * 128:(kk + 1) * 128],
                           xns[j % 2][:, k * 128:(k + 1) * 128], t_id[:, :])
                    inc_last("tensor", (s_pe, 1))
                    wt("vector", (s_pe, s_pe.n))
                    do("vector", "tensor_copy",
                       xn_fm.rearrange("p (k t) -> p k t", k=NK)[
                           :, kq * 4:(kq + 1) * 4, j * 128:(j + 1) * 128],
                       pbank.rearrange("p (k t) -> p k t", k=4), inc=((s_dv, 1),))
                    m_tp_ev[kq % 2] = s_dv.n
            m_xn_all = s_dv.n

            wt("gpsimd", (s_dv, m_xn_all))
            do("gpsimd", "dma_start",
               out=dn_sgn.rearrange("p (t j) -> p t j", t=8), in_=dnslab_ap(0),
               inc=((s_dw, 16),))
            m_dndma = [s_dw.n]

            if lvl < 1:
                raise _Stop

            m_upmm = [0] * NSLAB
            m_dnmm = [0] * NSLAB
            m_gadd = {}

            for s in range(NSLAB):
                wt("scalar", (s_dw, m_updma[s]))
                do("scalar", "activation", out=up_sgn, in_=up_sgn, func=AF.Sign,
                   inc=((s_ac, 1),))
                m_upsgn = s_ac.n
                wt("scalar", (s_dw, m_dndma[s]))
                do("scalar", "activation", out=dn_sgn, in_=dn_sgn, func=AF.Sign,
                   inc=((s_ac, 1),))
                m_dnsgn = s_ac.n

                if s == 0:
                    wt("tensor", (s_ac, m_upsgn), (s_dv, m_xn_all))
                else:
                    wt("tensor", (s_ac, m_upsgn), (s_dv, m_gadd[(s - 1, NT - 1)]))
                ev_ms = {}
                for ht in range(8):
                    pA = psw0[:, (ht % 2) * 1024:(ht % 2) * 1024 + 512]
                    pB = psw0[:, (ht % 2) * 1024 + 512:(ht % 2) * 1024 + 1024]
                    if ht >= 2:
                        wt("tensor", (s_dv, ev_ms[ht - 2]))
                    for k in range(NK):
                        lhsT = up_sgn[:, k * 1024 + ht * 128:k * 1024 + (ht + 1) * 128]
                        do("tensor", "matmul", pA, lhsT,
                           xn_fm[:, k * 1024:k * 1024 + 512],
                           start=(k == 0), stop=(k == NK - 1), skip_group_check=True)
                        do("tensor", "matmul", pB, lhsT,
                           xn_fm[:, k * 1024 + 512:k * 1024 + 1024],
                           start=(k == 0), stop=(k == NK - 1), skip_group_check=True)
                    inc_last("tensor", (s_pe, 1))
                    wt("vector", (s_pe, s_pe.n))
                    do("vector", "tensor_scalar", hid[:, ht * 1024:ht * 1024 + 512],
                       pA, t_usc[:, s * 8 + ht:s * 8 + ht + 1], 0.0, ALU.mult, ALU.max)
                    do("vector", "tensor_scalar",
                       hid[:, ht * 1024 + 512:(ht + 1) * 1024],
                       pB, t_usc[:, s * 8 + ht:s * 8 + ht + 1], 0.0, ALU.mult, ALU.max,
                       inc=((s_dv, 1),))
                    ev_ms[ht] = s_dv.n
                inc_last("tensor", (s_pe, 1))
                m_upmm[s] = s_pe.n
                m_hid_done = ev_ms[7]

                if s + 1 < NSLAB:
                    wt("gpsimd", (s_pe, m_upmm[s]))
                    do("gpsimd", "dma_start",
                       out=up_sgn.rearrange("p (k h) -> p k h", k=NK),
                       in_=upslab_ap(s + 1), inc=((s_dw, 16),))
                    m_updma.append(s_dw.n)

                wt("tensor", (s_dv, m_hid_done), (s_ac, m_dnsgn))
                for tt in range(NT):
                    ps = psw0 if tt % 2 == 0 else psw1
                    if tt >= 2:
                        wt("tensor", (s_dv, m_gadd[(s, tt - 2)]))
                    for ht in range(8):
                        lhsT = hid[:, ht * 1024 + tt * 128:ht * 1024 + (tt + 1) * 128]
                        for jc in range(4):
                            do("tensor", "matmul", ps[:, jc * 512:(jc + 1) * 512],
                               lhsT,
                               dn_sgn[:, ht * 2048 + jc * 512:ht * 2048 + (jc + 1) * 512],
                               start=(ht == 0), stop=(ht == 7), skip_group_check=True)
                    inc_last("tensor", (s_pe, 1))
                    wt("vector", (s_pe, s_pe.n))
                    if s == 0:
                        do("vector", "tensor_copy", ga[:, tt * 2048:(tt + 1) * 2048],
                           ps[:, :], inc=((s_dv, 1),))
                    else:
                        do("vector", "tensor_tensor", ga[:, tt * 2048:(tt + 1) * 2048],
                           ga[:, tt * 2048:(tt + 1) * 2048], ps[:, :], ALU.add,
                           inc=((s_dv, 1),))
                    m_gadd[(s, tt)] = s_dv.n
                inc_last("tensor", (s_pe, 1))
                m_dnmm[s] = s_pe.n

                if s + 1 < NSLAB:
                    wt("gpsimd", (s_pe, m_dnmm[s]))
                    do("gpsimd", "dma_start",
                       out=dn_sgn.rearrange("p (t j) -> p t j", t=8),
                       in_=dnslab_ap(s + 1), inc=((s_dw, 16),))
                    m_dndma.append(s_dw.n)

            ds_rep = view(OFF_DSREP, 2048, F32)
            ds_tmp = view(OFF_DSROW, 2048, F32)
            wt("sync", (s_pe, m_dnmm[NSLAB - 1]))
            do("sync", "dma_start", out=ds_tmp[0:1, :], in_=d_dsc, inc=((s_dx, 16),))
            m_dsrow = s_dx.n
            wt("vector", (s_dw, m_tables))
            do("vector", "tensor_scalar", t_os128[0:1, :], t_ones32[0:1, :],
               t_os[0:1, 0:1], None, ALU.mult, inc=((s_dv, 1),))
            m_os128 = s_dv.n
            wt("tensor", (s_dv, m_os128), (s_dx, m_dsrow))
            for jc in range(4):
                do("tensor", "matmul", psw0[:, jc * 512:(jc + 1) * 512],
                   t_os128[0:1, :], ds_tmp[0:1, jc * 512:(jc + 1) * 512],
                   start=True, stop=True, skip_group_check=True)
            inc_last("tensor", (s_pe, 1))
            wt("vector", (s_pe, s_pe.n), (s_dv, m_gadd[(NSLAB - 1, NT - 1)]))
            do("vector", "tensor_copy", ds_rep, psw0[:, :], inc=((s_dv, 1),))
            m_silu = 0
            for tt in range(NT):
                if m_silu:
                    wt("vector", (s_ac, m_silu))
                do("vector", "tensor_tensor", ds_tmp, ga[:, tt * 2048:(tt + 1) * 2048],
                   ds_rep, ALU.mult, inc=((s_dv, 1),))
                wt("scalar", (s_dv, s_dv.n))
                do("scalar", "activation", out=gffn[:, tt * 2048:(tt + 1) * 2048],
                   in_=ds_tmp, func=AF.Silu, inc=((s_ac, 1),))
                m_silu = s_ac.n
            m_gffn = s_ac.n
            m_dsrep_read = s_dv.n

            if lvl < 2:
                raise _Stop


            sA = psw1[:, 0:512]
            sB = psw1[:, 512:1024]
            tp_bf = psw1[:, 1024:1280].bitcast(BF16)
            tp_bf2 = psw1[:, 1536:1792].bitcast(BF16)

            def wslab_ap(wname, hg):
                return d_w[wname][:, hg * 512:(hg + 1) * 512].rearrange(
                    "(k p) j -> p k j", p=128)

            m_gate = [0] * NHG
            m_kvmm = [0] * (NHG + 1)
            ev_kv = [dict() for _ in range(NHG + 1)]
            m_b4 = [0] * NHG
            m_gqmm = [0] * NHG

            wt("gpsimd", (s_pe, m_upmm[NSLAB - 1]))
            do("gpsimd", "dma_start", out=wsA.rearrange("p (k j) -> p k j", k=NK),
               in_=wslab_ap("wk", 0), inc=((s_dw, 16),))
            do("gpsimd", "dma_start", out=wsB.rearrange("p (k j) -> p k j", k=NK),
               in_=wslab_ap("wv", 0), inc=((s_dw, 16),))
            m_wkv = s_dw.n
            for bi, bn in enumerate(("bk", "bv")):
                do("gpsimd", "dma_start",
                   out=t_bias4[0:1, bi * 512:(bi + 1) * 512],
                   in_=d_b[bn][0:1, 0:512], inc=((s_dw, 16),))
            m_b4[0] = s_dw.n

            def emit_kv_group(hg, tt, m_w):
                ks = k_tok2[hg % 2]
                vs = v_tok2[hg % 2]
                kps = psw0[:, 0:512]
                vps = psw0[:, 512:1024]
                if tt == 0:
                    wt("tensor", (s_dw, m_w), (s_dw, m_b4[hg]),
                       (s_dv, m_dsrep_read))
                else:
                    wt("tensor", (s_dv, ev_kv[hg][tt - 1]))
                do("tensor", "matmul", kps, t_ones[0:1, :],
                   t_bias4[0:1, 0:512], start=True, stop=False,
                   skip_group_check=True)
                do("tensor", "matmul", vps, t_ones[0:1, :],
                   t_bias4[0:1, 512:1024], start=True, stop=False,
                   skip_group_check=True)
                for k in range(NK):
                    lhsT = xn_fm[:, k * 1024 + tt * 128:k * 1024 + (tt + 1) * 128]
                    do("tensor", "matmul", kps, lhsT,
                       wsA[:, k * 512:(k + 1) * 512],
                       start=False, stop=(k == NK - 1), skip_group_check=True)
                    do("tensor", "matmul", vps, lhsT,
                       wsB[:, k * 512:(k + 1) * 512],
                       start=False, stop=(k == NK - 1), skip_group_check=True)
                inc_last("tensor", (s_pe, 1))
                wt("vector", (s_pe, s_pe.n))
                do("vector", "tensor_copy", ks[:, tt * 512:(tt + 1) * 512], kps)
                do("vector", "tensor_copy", vs[:, tt * 512:(tt + 1) * 512], vps,
                   inc=((s_dv, 1),))
                ev_kv[hg][tt] = s_dv.n

            def emit_q_proj(hg, m_wq, m_bq, m_qdst_free):
                qdst = v_tok2[(hg + 1) % 2]
                ev_prev = [0, 0]
                for tt in range(NT):
                    qps = psw0[:, 1024 + (tt % 2) * 512:1536 + (tt % 2) * 512]
                    if tt == 0:
                        wt("tensor", (s_dw, m_wq), (s_dv, m_qdst_free))
                    elif tt >= 2:
                        wt("tensor", (s_dv, ev_prev[tt % 2]))
                    for k in range(NK):
                        lhsT = xn_fm[:, k * 1024 + tt * 128:k * 1024 + (tt + 1) * 128]
                        do("tensor", "matmul", qps, lhsT,
                           wsB[:, k * 512:(k + 1) * 512],
                           start=(k == 0), stop=False, skip_group_check=True)
                    if tt == 0:
                        wt("tensor", (s_dw, m_bq))
                    do("tensor", "matmul", qps, t_ones[0:1, :],
                       t_bias4[0:1, 512:1024], start=False, stop=True,
                       skip_group_check=True)
                    inc_last("tensor", (s_pe, 1))
                    wt("vector", (s_pe, s_pe.n))
                    do("vector", "tensor_copy", qdst[:, tt * 512:(tt + 1) * 512],
                       qps, inc=((s_dv, 1),))
                    ev_prev[tt % 2] = s_dv.n
                inc_last("tensor", (s_pe, 1))
                return s_pe.n, s_dv.n

            ev_k = [dict() for _ in range(NHG + 1)]
            ev_v = [dict() for _ in range(NHG + 1)]
            m_ktok = [0] * (NHG + 1)

            def emit_k_group(hg, tt, m_w, m_b):
                ks = k_tok2[hg % 2]
                kps = psw0[:, 0:512]
                if tt == 0:
                    wt("tensor", (s_dw, m_w), (s_dw, m_b))
                else:
                    wt("tensor", (s_dv, ev_k[hg][tt - 1]))
                for k in range(NK):
                    lhsT = xn_fm[:, k * 1024 + tt * 128:k * 1024 + (tt + 1) * 128]
                    do("tensor", "matmul", kps, lhsT,
                       wsB[:, k * 512:(k + 1) * 512],
                       start=(k == 0), stop=False, skip_group_check=True)
                do("tensor", "matmul", kps, t_ones[0:1, :],
                   t_bias4[0:1, 0:512], start=False, stop=True,
                   skip_group_check=True)
                inc_last("tensor", (s_pe, 1))
                wt("vector", (s_pe, s_pe.n))
                do("vector", "tensor_copy", ks[:, tt * 512:(tt + 1) * 512], kps,
                   inc=((s_dv, 1),))
                ev_k[hg][tt] = s_dv.n

            def emit_v_group(hg, tt, m_w):
                vs = v_tok2[hg % 2]
                vps = psw0[:, 512:1024]
                if tt == 0:
                    wt("tensor", (s_dw, m_w))
                else:
                    wt("tensor", (s_dv, ev_v[hg][tt - 1]))
                for k in range(NK):
                    lhsT = xn_fm[:, k * 1024 + tt * 128:k * 1024 + (tt + 1) * 128]
                    do("tensor", "matmul", vps, lhsT,
                       wsA[:, k * 512:(k + 1) * 512],
                       start=(k == 0), stop=False, skip_group_check=True)
                do("tensor", "matmul", vps, t_ones[0:1, :],
                   t_bias4[0:1, 512:1024], start=False, stop=True,
                   skip_group_check=True)
                inc_last("tensor", (s_pe, 1))
                wt("vector", (s_pe, s_pe.n))
                do("vector", "tensor_copy", vs[:, tt * 512:(tt + 1) * 512], vps,
                   inc=((s_dv, 1),))
                ev_v[hg][tt] = s_dv.n

            for tt in range(NT):
                emit_kv_group(0, tt, m_wkv)
            inc_last("tensor", (s_pe, 1))
            m_kvmm[0] = s_pe.n
            m_ktok[0] = ev_kv[0][NT - 1]

            wt("gpsimd", (s_pe, m_kvmm[0]))
            do("gpsimd", "dma_start", out=wsB.rearrange("p (k j) -> p k j", k=NK),
               in_=wslab_ap("wq", 0), inc=((s_dw, 16),))
            do("gpsimd", "dma_start",
               out=t_bias4[0:1, 512:1024], in_=d_b["bq"][0:1, 0:512],
               inc=((s_dw, 16),))
            m_wq = s_dw.n
            do("gpsimd", "dma_start", out=wsA.rearrange("p (k j) -> p k j", k=NK),
               in_=wslab_ap("wg", 0), inc=((s_dw, 16),))
            do("gpsimd", "dma_start",
               out=t_bias4[0:1, 0:512], in_=d_b["bg"][0:1, 0:512],
               inc=((s_dw, 16),))
            m_wg = s_dw.n
            m_qmm, m_qev = emit_q_proj(0, m_wq, m_wq, m_dsrep_read)

            def rope_hb(buf, hb):
                b4 = buf.rearrange("p (t h f) -> p t h f", t=NT, h=HPG)
                ropeT = [view(OFF_RT + i * 2048, 1024, BF16) for i in range(4)]
                x1 = b4[:, hb * 4:(hb + 1) * 4, :, 0:64]
                x2 = b4[:, hb * 4:(hb + 1) * 4, :, 64:128]
                cosb = t_cos[:, :].rearrange("p (t f) -> p t f", t=NT)[
                    :, hb * 4:(hb + 1) * 4, :].unsqueeze(2).broadcast_to(
                    [128, 4, HPG, 64])
                sinb = t_sin[:, :].rearrange("p (t f) -> p t f", t=NT)[
                    :, hb * 4:(hb + 1) * 4, :].unsqueeze(2).broadcast_to(
                    [128, 4, HPG, 64])
                r4 = [r.rearrange("p (t h f) -> p t h f", t=4, h=HPG)
                      for r in ropeT]
                do("vector", "tensor_tensor", r4[0], x1, cosb, ALU.mult)
                do("vector", "tensor_tensor", r4[1], x1, sinb, ALU.mult)
                do("vector", "tensor_tensor", r4[2], x2, sinb, ALU.mult)
                do("vector", "tensor_tensor", r4[3], x2, cosb, ALU.mult)
                do("vector", "tensor_tensor", x1, r4[0], r4[2], ALU.subtract)
                do("vector", "tensor_tensor", x2, r4[1], r4[3], ALU.add,
                   inc=((s_dv, 1),))

            def elu_q(buf, q4):
                ropeT = [view(OFF_RT + i * 2048, 1024, BF16) for i in range(4)]
                tmin = ropeT[0]
                texp = ropeT[2]
                sl = buf[:, q4 * 1024:(q4 + 1) * 1024]
                do("vector", "tensor_single_scalar", tmin, sl, 0.0, ALU.min,
                   inc=((s_dv, 1),))
                wt("scalar", (s_dv, s_dv.n))
                do("scalar", "activation", out=texp, in_=tmin, func=AF.Exp,
                   inc=((s_ac, 1),))
                wt("vector", (s_ac, s_ac.n))
                do("vector", "scalar_tensor_tensor", sl, sl, 0.0, texp,
                   ALU.max, ALU.add, inc=((s_dv, 1),))

            for hg in range(NHG):
                slot = hg % 2
                k_tok = k_tok2[slot]
                v_tok = v_tok2[slot]
                qtok = v_tok2[(hg + 1) % 2]

                if hg + 1 < NHG:
                    wt("gpsimd", (s_pe, m_qmm))
                    do("gpsimd", "dma_start",
                       out=wsB.rearrange("p (k j) -> p k j", k=NK),
                       in_=wslab_ap("wk", hg + 1), inc=((s_dw, 16),))
                    do("gpsimd", "dma_start",
                       out=t_bias4[0:1, 512:1024],
                       in_=d_b["bv"][0:1, (hg + 1) * 512:(hg + 2) * 512],
                       inc=((s_dw, 16),))
                    m_wkb = s_dw.n

                fm_stage = 0
                wt("tensor", (s_dw, m_wg))
                if hg > 0:
                    wt("tensor", (s_dv, m_gate[hg - 1]),
                       (s_ac, m_gatepsw0))
                else:
                    wt("tensor", (s_dv, ev_kv[0][NT - 1]), (s_dv, m_qev))
                ev_g = [0, 0]
                for tt in range(NT):
                    gps = psw0[:, (tt % 4) * 512:(tt % 4) * 512 + 512]
                    if tt >= 4:
                        wt("tensor", (s_ac, ev_g[(tt // 2) % 2]))
                    do("tensor", "matmul", gps, t_ones[0:1, :],
                       t_bias4[0:1, 0:512], start=True, stop=False,
                       skip_group_check=True)
                    for k in range(NK):
                        lhsT = xn_fm[:, k * 1024 + tt * 128:k * 1024 + (tt + 1) * 128]
                        do("tensor", "matmul", gps, lhsT,
                           wsA[:, k * 512:(k + 1) * 512],
                           start=False, stop=(k == NK - 1), skip_group_check=True)
                    inc_last("tensor", (s_pe, 1))
                    if tt % 2 == 1:
                        if tt == 1 and hg > 0:
                            wt("scalar", (s_pe, s_pe.n), (s_dv, m_gate[hg - 1]))
                        else:
                            wt("scalar", (s_pe, s_pe.n))
                        for t2 in (tt - 1, tt):
                            do("scalar", "activation",
                               out=g_tok[:, t2 * 512:(t2 + 1) * 512],
                               in_=psw0[:, (t2 % 4) * 512:(t2 % 4) * 512 + 512],
                               func=AF.Sigmoid,
                               inc=((s_ac, 1),) if t2 == tt else ())
                        ev_g[(tt // 2) % 2] = s_ac.n
                    if fm_stage < 8:
                        if fm_stage == 0:
                            wts = [(s_dv, m_ktok[hg]), (s_dw, m_tables)]
                            if hg == 0:
                                wts.append((s_ac, m_gffn))
                            wt("vector", *wts)
                            rope_hb(k_tok, 0)
                        elif fm_stage == 1:
                            rope_hb(k_tok, 1)
                        elif fm_stage == 2:
                            wt("vector", (s_dv, m_qev))
                            rope_hb(qtok, 0)
                        elif fm_stage == 3:
                            rope_hb(qtok, 1)
                        elif fm_stage == 4:
                            elu_q(k_tok, 0)
                            elu_q(k_tok, 1)
                        elif fm_stage == 5:
                            elu_q(k_tok, 2)
                            elu_q(k_tok, 3)
                        elif fm_stage == 6:
                            elu_q(qtok, 0)
                            elu_q(qtok, 1)
                        else:
                            elu_q(qtok, 2)
                            elu_q(qtok, 3)
                        fm_stage += 1
                inc_last("tensor", (s_pe, 1))
                m_gqmm[hg] = s_pe.n
                m_fmk = 0
                m_fmq = s_dv.n
                m_sig_last = s_ac.n

                if lvl == 2:
                    raise _Stop
                if lvl == 3:
                    raise _Stop

                wt("tensor", (s_dv, m_fmq))
                m_qT_ev = [0, 0]
                for tt in range(NT):
                    tp = tp_bf if tt % 2 == 0 else tp_bf2
                    if m_qT_ev[tt % 2]:
                        wt("tensor", (s_dv, m_qT_ev[tt % 2]))
                    for hh in range(HPG):
                        do("tensor", "transpose",
                           tp[:, hh * 128:(hh + 1) * 128],
                           qtok[:, tt * 512 + hh * 128:tt * 512 + (hh + 1) * 128],
                           t_id[:, :])
                    inc_last("tensor", (s_pe, 1))
                    wt("vector", (s_pe, s_pe.n))
                    do("vector", "tensor_copy",
                       q_fm.rearrange("p (h t) -> p h t", h=HPG)[
                           :, :, tt * 128:(tt + 1) * 128],
                       tp.rearrange("p (h t) -> p h t", h=HPG),
                       inc=((s_dv, 1),))
                    m_qT_ev[tt % 2] = s_dv.n
                m_qT = s_dv.n

                if hg + 1 < NHG:
                    wt("gpsimd", (s_pe, m_gqmm[hg]))
                    do("gpsimd", "dma_start",
                       out=t_bias4[0:1, 0:512],
                       in_=d_b["bk"][0:1, (hg + 1) * 512:(hg + 2) * 512],
                       inc=((s_dw, 16),))
                    m_bk = s_dw.n
                    do("gpsimd", "dma_start",
                       out=wsA.rearrange("p (k j) -> p k j", k=NK),
                       in_=wslab_ap("wv", hg + 1), inc=((s_dw, 16),))
                    m_wv = s_dw.n

                m_snap = {}
                m_numev = {}
                szh_buf = [t_szh, szh2]

                def emit_num(j):
                    jt, jhalf = j // 2, j % 2
                    jpb = jhalf * 64
                    wt("tensor", (s_dv, m_snap[j]))
                    if j == 0:
                        wt("tensor", (s_dv, m_qT))
                    if j == 1:
                        wt("tensor", (s_ac, m_sig_last))
                    if j >= 2:
                        wt("tensor", (s_dv, m_numev[j - 2]))
                    npair = (psw1[:, 1024:2048] if jhalf == 0
                             else psw0[:, 1024:2048])
                    szh = szh_buf[j % 2]
                    for hh in range(HPG):
                        hb, hl = hh // 2, hh % 2
                        do("tensor", "matmul",
                           npair[jpb:jpb + 64,
                                 hb * 512 + hl * 128:hb * 512 + hl * 128 + 128],
                           q_fm[:, hh * 1024 + j * 64:hh * 1024 + (j + 1) * 64],
                           szh[:, hh * 129:hh * 129 + 128],
                           start=(hl == 0), stop=False, skip_group_check=True,
                           tile_position=(0, jpb) if jpb else None)
                    for hh in range(HPG):
                        hb, hl = hh // 2, hh % 2
                        do("tensor", "matmul",
                           npair[jpb:jpb + 64,
                                 hb * 512 + 256 + hl:hb * 512 + 257 + hl],
                           q_fm[:, hh * 1024 + j * 64:hh * 1024 + (j + 1) * 64],
                           szh[:, hh * 129 + 128:hh * 129 + 129],
                           start=False, stop=(hl == 1), skip_group_check=True,
                           tile_position=(0, jpb) if jpb else None)
                    inc_last("tensor", (s_pe, 1))
                    wt("vector", (s_pe, s_pe.n))
                    n4 = npair.rearrange("p (b f) -> p b f", b=2)
                    dst = num_sb[jpb:jpb + 64,
                                 jt * 516:(jt + 1) * 516].rearrange(
                        "p (bh f) -> p bh f", f=129).rearrange(
                        "p (b h) f -> p b h f", b=2)
                    do("vector", "tensor_copy", dst[:, :, :, 0:128],
                       n4[jpb:jpb + 64, :, 0:256].rearrange(
                           "p b (h f) -> p b h f", h=2))
                    do("vector", "tensor_copy", dst[:, :, :, 128:129],
                       n4[jpb:jpb + 64, :, 256:258].rearrange(
                           "p b (h f) -> p b h f", h=2), inc=((s_dv, 1),))
                    m_numev[j] = s_dv.n

                for i in range(NCH):
                    tt, half = i // 2, i % 2
                    pb = half * 64
                    if i > 0:
                        wt("tensor", (s_dv, m_snap[i - 1]))
                    else:
                        wt("tensor", (s_dv, m_fmq))
                    for hh in range(HPG):
                        sbank = sA if hh < 2 else sB
                        hl = hh % 2
                        do("tensor", "matmul",
                           sbank[:, hl * 128:hl * 128 + 128],
                           k_tok[pb:pb + 64,
                                 tt * 512 + hh * 128:tt * 512 + (hh + 1) * 128],
                           v_tok[pb:pb + 64,
                                 tt * 512 + hh * 128:tt * 512 + (hh + 1) * 128],
                           start=(i == 0 and hl == 0), stop=False,
                           skip_group_check=True)
                    for hh in range(HPG):
                        sbank = sA if hh < 2 else sB
                        hl = hh % 2
                        do("tensor", "matmul", sbank[:, 256 + hl:257 + hl],
                           k_tok[pb:pb + 64,
                                 tt * 512 + hh * 128:tt * 512 + (hh + 1) * 128],
                           t_onescol[pb:pb + 64, 0:1],
                           start=False, stop=(i == NCH - 1 and hl == 1),
                           skip_group_check=True)
                    inc_last("tensor", (s_pe, 1))
                    m_mm1 = s_pe.n
                    wt("vector", (s_pe, m_mm1))
                    spair = psw1[:, 0:1024].rearrange("p (b f) -> p b f", b=2)
                    zdst = szh_buf[i % 2][:, 0:516].rearrange(
                        "p (bh f) -> p bh f", f=129).rearrange(
                        "p (b h) f -> p b h f", b=2)
                    do("vector", "tensor_copy", zdst[:, :, :, 0:128],
                       spair[:, :, 0:256].rearrange("p b (h f) -> p b h f", h=2))
                    do("vector", "tensor_copy", zdst[:, :, :, 128:129],
                       spair[:, :, 256:258].rearrange("p b (h f) -> p b h f", h=2),
                       inc=((s_dv, 1),))
                    m_snap[i] = s_dv.n
                    if i == NCH - 1:
                        t_sf16 = gath[:, 0:516]
                        fdst = t_sf16.rearrange(
                            "p (bh f) -> p bh f", f=129).rearrange(
                            "p (b h) f -> p b h f", b=2)
                        do("vector", "tensor_copy", fdst[:, :, :, 0:128],
                           spair[:, :, 0:256].rearrange(
                               "p b (h f) -> p b h f", h=2))
                        do("vector", "tensor_copy", fdst[:, :, :, 128:129],
                           spair[:, :, 256:258].rearrange(
                               "p b (h f) -> p b h f", h=2), inc=((s_dv, 1),))
                        m_sfin = s_dv.n
                        wt("sync", (s_dv, m_sfin))
                        do("sync", "dma_start", out=d_ccs[hg][:, :],
                           in_=t_sf16, inc=((s_dx, 16),))
                        m_stg = s_dx.n
                        wt("gpsimd", (s_dx, m_stg))
                        do("gpsimd", "collective_compute", "AllGather",
                           ALU.bypass, replica_groups=[list(range(NCORE))],
                           ins=[d_ccs[hg][:, :]], outs=[d_ccd[hg][:, :]],
                           inc=((s_cc, 1),))
                    if hg + 1 < NHG and 1 <= i <= 8:
                        emit_k_group(hg + 1, i - 1, m_wkb, m_bk)
                        if i == 8:
                            inc_last("tensor", (s_pe, 1))
                            m_kmm = s_pe.n
                            m_ktok[hg + 1] = ev_k[hg + 1][NT - 1]
                            wt("gpsimd", (s_pe, m_kmm))
                            do("gpsimd", "dma_start",
                               out=wsB.rearrange("p (k j) -> p k j", k=NK),
                               in_=wslab_ap("wq", hg + 1), inc=((s_dw, 16),))
                            do("gpsimd", "dma_start",
                               out=t_bias4[0:1, 0:512],
                               in_=d_b["bg"][0:1,
                                             (hg + 1) * 512:(hg + 2) * 512],
                               inc=((s_dw, 16),))
                            m_wq = s_dw.n
                    if hg + 1 < NHG and 9 <= i <= 15:
                        emit_v_group(hg + 1, i - 9, m_wv)
                    if i > 0:
                        emit_num(i - 1)
                if hg + 1 < NHG:
                    emit_v_group(hg + 1, 7, m_wv)
                    inc_last("tensor", (s_pe, 1))
                    m_vmm = s_pe.n
                emit_num(NCH - 1)
                m_scan = s_pe.n
                if hg + 1 < NHG:
                    wt("gpsimd", (s_pe, m_vmm))
                    do("gpsimd", "dma_start",
                       out=t_bias4[0:1, 512:1024],
                       in_=d_b["bq"][0:1, (hg + 1) * 512:(hg + 2) * 512],
                       inc=((s_dw, 16),))
                    m_bq = s_dw.n

                if lvl == 4:
                    raise _Stop

                if hg + 1 < NHG:
                    m_qmm, m_qev = emit_q_proj(hg + 1, m_wq, m_bq,
                                               m_numev[NCH - 1])
                wt("sync", (s_cc, s_cc.n), (s_dv, m_fmq))
                do("sync", "dma_start",
                   out=gath[:, 0:2064].rearrange("p (s f) -> p s f", s=4),
                   in_=d_ccd[hg].rearrange("(u v p) f -> p u v f", v=2, p=128)[
                       :, :, 0, :],
                   inc=((s_dx, 16),))
                m_gath = s_dx.n
                wt("vector", (s_dx, m_gath), (s_pe, m_scan))
                do("vector", "tensor_tensor",
                   gath[:, 0:2064].rearrange("p (s f) -> p s f", s=4),
                   gath[:, 0:2064].rearrange("p (s f) -> p s f", s=4),
                   t_msk[:, :].rearrange("p (u v) -> p u v", v=2)[
                       :, :, 0].unsqueeze(2).broadcast_to([128, 4, 516]),
                   ALU.mult)
                do("vector", "tensor_tensor", gath[:, 0:1032], gath[:, 0:1032],
                   gath[:, 1032:2064], ALU.add)
                do("vector", "tensor_tensor", t_szh[:, 0:516], gath[:, 0:516],
                   gath[:, 516:1032], ALU.add, inc=((s_dv, 1),))
                m_szh = s_dv.n
                if hg + 1 < NHG:
                    wt("gpsimd", (s_dv, m_szh), (s_pe, m_vmm))
                    do("gpsimd", "dma_start",
                       out=wsA.rearrange("p (k j) -> p k j", k=NK),
                       in_=wslab_ap("wg", hg + 1), inc=((s_dw, 16),))
                    m_wg = s_dw.n

                if lvl == 5:
                    raise _Stop

                gtp = [psw1[:, 0:256].bitcast(BF16),
                       psw1[:, 512:768].bitcast(BF16)]
                atok2 = [t_atok[:, 0:512], view(OFF_GATH + 4128, 512, BF16)]
                c129_b = [t_c129[:, 0:516], view(OFF_GATH + 5152, 516, F32)]
                m_c1war = m_qev if hg + 1 < NHG else m_numev[NCH - 1]
                m_corr = {}
                m_cfree = {}
                m_atok = {}
                m_attcopy = {}

                def emit_corr(tt):
                    if tt == 0:
                        wt("tensor", (s_dv, m_szh))
                    elif tt == 1:
                        wt("tensor", (s_dv, m_c1war))
                    else:
                        wt("tensor", (s_ac, m_cfree[tt - 2]))
                    cpair = (psw1[:, 1024:2048] if tt % 2 == 0
                             else psw0[:, 1024:2048])
                    for hh in range(HPG):
                        hb, hl = hh // 2, hh % 2
                        do("tensor", "matmul",
                           cpair[:, hb * 512 + hl * 128:hb * 512 + hl * 128 + 128],
                           q_fm[:, hh * 1024 + tt * 128:hh * 1024 + (tt + 1) * 128],
                           t_szh[:, hh * 129:hh * 129 + 128],
                           start=(hl == 0), stop=False, skip_group_check=True)
                    for hh in range(HPG):
                        hb, hl = hh // 2, hh % 2
                        do("tensor", "matmul",
                           cpair[:, hb * 512 + 256 + hl:hb * 512 + 257 + hl],
                           q_fm[:, hh * 1024 + tt * 128:hh * 1024 + (tt + 1) * 128],
                           t_szh[:, hh * 129 + 128:hh * 129 + 129],
                           start=False, stop=False, skip_group_check=True)
                    for hb in range(2):
                        do("tensor", "matmul",
                           cpair[:, hb * 512 + 256:hb * 512 + 258],
                           t_ones[0:1, :], t_eps[0:1, :],
                           start=False, stop=True, skip_group_check=True)
                    inc_last("tensor", (s_pe, 1))
                    m_corr[tt] = s_pe.n
                    return cpair

                def emit_gate_dve(tt, cpair):
                    c129 = c129_b[tt % 2].rearrange(
                        "p (bh f) -> p bh f", f=129).rearrange(
                        "p (b h) f -> p b h f", b=2)
                    c4 = cpair.rearrange("p (b f) -> p b f", b=2)
                    wt("scalar", (s_pe, m_corr[tt]))
                    do("scalar", "copy", c129[:, :, :, 128:129],
                       c4[:, :, 256:258].rearrange("p b (h f) -> p b h f", h=2))
                    do("scalar", "copy", c129[:, :, :, 0:128],
                       c4[:, :, 0:256].rearrange("p b (h f) -> p b h f", h=2),
                       inc=((s_ac, 1),))
                    m_cfree[tt] = s_ac.n
                    t129 = t_sfin[:, 0:516].rearrange(
                        "p (bh f) -> p bh f", f=129).rearrange(
                        "p (b h) f -> p b h f", b=2)
                    wt("vector", (s_ac, m_cfree[tt]))
                    do("vector", "tensor_tensor", t129,
                       num_sb[:, tt * 516:(tt + 1) * 516].rearrange(
                           "p (bh f) -> p bh f", f=129).rearrange(
                           "p (b h) f -> p b h f", b=2),
                       c129, ALU.add, inc=((s_dv, 1),))
                    wt("scalar", (s_dv, s_dv.n))
                    do("scalar", "copy", t_dtmp[:, :],
                       t129[:, :, :, 128:129].rearrange("p b h f -> p (b h f)"),
                       inc=((s_ac, 1),))
                    m_den = s_ac.n
                    a4 = atok2[tt % 2].rearrange(
                        "p (b h f) -> p b h f", b=2, h=2)
                    g4 = g_tok[:, tt * 512:(tt + 1) * 512].rearrange(
                        "p (b h f) -> p b h f", b=2, h=2)
                    do("vector", "tensor_tensor", a4[:, 0:1],
                       t129[:, 0:1, :, 0:128], g4[:, 0:1], ALU.mult)
                    wt("vector", (s_ac, m_den))
                    do("vector", "reciprocal", t_rden[:, :], t_dtmp[:, :])
                    do("vector", "tensor_tensor", a4[:, 1:2],
                       t129[:, 1:2, :, 0:128], g4[:, 1:2], ALU.mult)
                    do("vector", "tensor_tensor", a4, a4,
                       t_rden[:, :].rearrange("p (b h) -> p b h", b=2).unsqueeze(
                           3).broadcast_to([128, 2, 2, 128]), ALU.mult,
                       inc=((s_dv, 1),))
                    m_atok[tt] = s_dv.n

                def emit_attnT(tt):
                    wts = [(s_dv, m_atok[tt])]
                    if tt >= 2:
                        wts.append((s_dv, m_attcopy[tt - 2]))
                    wt("tensor", *wts)
                    tp = gtp[tt % 2]
                    src = atok2[tt % 2]
                    for hh in range(HPG):
                        do("tensor", "transpose",
                           tp[:, hh * 128:(hh + 1) * 128],
                           src[:, hh * 128:(hh + 1) * 128], t_id[:, :])
                    inc_last("tensor", (s_pe, 1))
                    wt("vector", (s_pe, s_pe.n))
                    do("vector", "tensor_copy",
                       attn_fm.rearrange("p (a t) -> p a t", a=NK)[
                           :, hg * 4:(hg + 1) * 4, tt * 128:(tt + 1) * 128],
                       tp.rearrange("p (a t) -> p a t", a=HPG),
                       inc=((s_dv, 1),))
                    m_attcopy[tt] = s_dv.n

                emit_gate_dve(0, emit_corr(0))
                for tt in range(1, NT):
                    emit_gate_dve(tt, emit_corr(tt))
                    emit_attnT(tt - 1)
                emit_attnT(NT - 1)
                m_gate[hg] = m_attcopy[NT - 1]
                m_gatepsw0 = m_cfree[NT - 1]

            if lvl < 7:
                raise _Stop


            wo_h = [view(OFF_WO, NK * 1024, BF16), view(OFF_XN, NK * 1024, BF16)]
            xld = [view(OFF_P3X0, 1024, F32), view(OFF_P3X1, 1024, F32)]
            ost = [view(OFF_P3O0, 1024, F32), view(OFF_P3O1, 1024, F32)]

            def wohalf_ap(hf):
                return d_w["wo"][:, hf * 1024:(hf + 1) * 1024].rearrange(
                    "(k p) j -> p k j", p=128)

            wt("gpsimd", (s_pe, m_gqmm[NHG - 1]))
            do("gpsimd", "dma_start", out=t_bias4[0:1, :],
               in_=d_b["bo"][0:1, 0:1024], inc=((s_dw, 16),))
            m_bo = [s_dw.n]
            wt("gpsimd", (s_cc, s_cc.n))
            do("gpsimd", "dma_start",
               out=wo_h[0].rearrange("p (k j) -> p k j", k=NK), in_=wohalf_ap(0),
               inc=((s_dw, 16),))
            m_wo = [s_dw.n]
            do("gpsimd", "dma_start",
               out=wo_h[1].rearrange("p (k j) -> p k j", k=NK), in_=wohalf_ap(1),
               inc=((s_dw, 16),))
            m_wo.append(s_dw.n)

            m_comb = {}
            m_p3mm = {}
            for hf in range(2):
                wo_half = wo_h[hf]
                wt("tensor", (s_dw, m_wo[hf]))
                if hf == 0:
                    wt("tensor", (s_dv, m_gate[NHG - 1]))
                for tt in range(NT):
                    ps = psw0 if tt % 2 == 0 else psw1
                    pshalf = ps[:, 0:1024]
                    prev = None
                    if tt >= 2:
                        prev = m_comb[(hf, tt - 2)]
                    elif hf == 1:
                        prev = m_comb[(0, NT - 2 + tt)]
                    if prev is not None:
                        wt("tensor", (s_dv, prev))
                    for k in range(NK):
                        lhsT = attn_fm[:, k * 1024 + tt * 128:k * 1024 + (tt + 1) * 128]
                        for jc in range(2):
                            do("tensor", "matmul",
                               pshalf[:, jc * 512:(jc + 1) * 512], lhsT,
                               wo_half[:, k * 1024 + jc * 512:k * 1024 + (jc + 1) * 512],
                               start=(k == 0), stop=False,
                               skip_group_check=True)
                    if tt == 0:
                        wt("tensor", (s_dw, m_bo[hf]))
                    for jc in range(2):
                        do("tensor", "matmul", pshalf[:, jc * 512:(jc + 1) * 512],
                           t_ones[0:1, :],
                           t_bias4[0:1, jc * 512:(jc + 1) * 512],
                           start=False, stop=True, skip_group_check=True)
                    inc_last("tensor", (s_pe, 1))
                    m_smm = s_pe.n
                    m_p3mm[(hf, tt)] = m_smm
                    if prev is not None:
                        wt("sync", (s_dv, prev))
                    elif hf == 0 and tt == 0:
                        wt("sync", (s_dv, m_gate[NHG - 1]))
                    do("sync", "dma_start", out=xld[tt % 2],
                       in_=d_x[tt * 128:(tt + 1) * 128, hf * 1024:(hf + 1) * 1024],
                       inc=((s_dx, 16),))
                    m_xr = s_dx.n
                    wt("vector", (s_pe, m_smm), (s_dx, m_xr))
                    if hf == 0 and tt == 0:
                        wt("vector", (s_ac, m_gffn))
                    do("vector", "tensor_tensor", ost[tt % 2], pshalf,
                       gffn[:, tt * 2048 + hf * 1024:tt * 2048 + (hf + 1) * 1024],
                       ALU.mult)
                    do("vector", "tensor_tensor", ost[tt % 2], ost[tt % 2],
                       xld[tt % 2], ALU.add, inc=((s_dv, 1),))
                    m_comb[(hf, tt)] = s_dv.n
                    wt("sync", (s_dv, m_comb[(hf, tt)]))
                    do("sync", "dma_start",
                       out=d_y[tt * 128:(tt + 1) * 128, hf * 1024:(hf + 1) * 1024],
                       in_=ost[tt % 2], inc=((s_out, 16),))
                if hf == 0:
                    wt("gpsimd", (s_pe, m_p3mm[(0, NT - 1)]))
                    do("gpsimd", "dma_start", out=t_bias4[0:1, :],
                       in_=d_b["bo"][0:1, 1024:2048], inc=((s_dw, 16),))
                    m_bo.append(s_dw.n)


        except _Stop:
            pass

        dbg_srcs = {
            "xn_fm": xn_fm, "gffn": gffn, "ga": ga, "k_tok": k_tok,
            "v_tok": v_tok, "g_tok": g_tok, "q_fm": q_fm, "num_sb": num_sb,
            "den": t_den[:, :], "attn_fm": attn_fm, "szh": t_szh[:, :],
            "stats": t_stats[:, :], "xns1": xns[1], "xb1": xb[1],
        }
        if debug:
            dchunk = view(OFF_BIG, 2048, F32)
            for (nm, shape) in debug:
                src_ap = dbg_srcs[nm]
                n = src_ap.shape[1]
                for c0 in range(0, n, 2048):
                    c1 = min(c0 + 2048, n)
                    wt("vector", (s_dv, s_dv.n), (s_ac, s_ac.n),
                       (s_pe, s_pe.n), (s_dx, s_dx.n), (s_dw, s_dw.n))
                    do("vector", "tensor_copy", dchunk[:, 0:c1 - c0],
                       src_ap[:, c0:c1], inc=((s_dv, 1),))
                    wt("gpsimd", (s_dv, s_dv.n))
                    do("gpsimd", "dma_start", out=d_dbg[nm][:, c0:c1],
                       in_=dchunk[:, 0:c1 - c0], inc=((s_dw, 16),))

        wt("sync", (s_out, s_out.n))
        wt("gpsimd", (s_dw, s_dw.n))

        with nc.Block() as block:
            @block.tensor
            def _(t):
                for f in P["tensor"]:
                    f(t)

            @block.vector
            def _(v):
                for f in P["vector"]:
                    f(v)

            @block.scalar
            def _(sc):
                for f in P["scalar"]:
                    f(sc)

            @block.gpsimd
            def _(g):
                for f in P["gpsimd"]:
                    f(g)

            @block.sync
            def _(sy):
                for f in P["sync"]:
                    f(sy)

    return nc


def _perm_deint():
    p = np.zeros(D, np.int64)
    for h in range(H):
        b = h * DK
        for i in range(DK // 2):
            p[b + i] = b + 2 * i
            p[b + 64 + i] = b + 2 * i + 1
    return p


def _rope_tables():
    inv = (1.0 / 10000.0 ** (np.arange(0, DK, 2, dtype=np.float32)
                             / np.float32(DK))).astype(np.float32)
    pos = np.arange(2048, dtype=np.float32)
    ang = pos[:, None] * inv[None, :]
    return np.sin(ang).astype(np.float32), np.cos(ang).astype(np.float32)


_DEBUG = ()


def kernel(L0_attn_x, L0_attn_Wq, L0_attn_bq, L0_attn_Wk, L0_attn_bk,
           L0_attn_Wv, L0_attn_bv, L0_attn_Wg, L0_attn_bg, L0_attn_Wo,
           L0_attn_bo, L0_ffn_up_weight, L0_ffn_down_weight, L0_ffn_up_scale,
           L0_ffn_down_scale, L0_ffn_output_scale):
    x = np.ascontiguousarray(
        np.asarray(L0_attn_x, np.float32).reshape(4 * 2048, D))
    p = _perm_deint()
    wq = np.ascontiguousarray(np.asarray(L0_attn_Wq, np.float32).T[:, p])
    wk = np.ascontiguousarray(np.asarray(L0_attn_Wk, np.float32).T[:, p])
    wv = np.ascontiguousarray(np.asarray(L0_attn_Wv, np.float32).T)
    wg = np.ascontiguousarray(np.asarray(L0_attn_Wg, np.float32).T)
    wo = np.ascontiguousarray(np.asarray(L0_attn_Wo, np.float32).T)
    upw = np.ascontiguousarray(np.asarray(L0_ffn_up_weight, np.float32).T)
    dnw = np.ascontiguousarray(np.asarray(L0_ffn_down_weight, np.float32).T)
    bq = np.ascontiguousarray(np.asarray(L0_attn_bq, np.float32)[p][None, :])
    bk = np.ascontiguousarray(np.asarray(L0_attn_bk, np.float32)[p][None, :])
    bv = np.ascontiguousarray(np.asarray(L0_attn_bv, np.float32)[None, :])
    bg = np.ascontiguousarray(np.asarray(L0_attn_bg, np.float32)[None, :])
    bo = np.ascontiguousarray(np.asarray(L0_attn_bo, np.float32)[None, :])
    usc = np.ascontiguousarray(
        np.asarray(L0_ffn_up_scale, np.float32).reshape(64, 128).T)
    dsc = np.ascontiguousarray(
        np.asarray(L0_ffn_down_scale, np.float32)[None, :])
    osc = np.ascontiguousarray(
        np.asarray(L0_ffn_output_scale, np.float32).reshape(1, 1))
    sin_f, cos_f = _rope_tables()
    ident = np.eye(128, dtype=np.float32)

    nc = build_nc(debug=_DEBUG)

    in_maps = []
    for c in range(NCORE):
        half = c % 2
        t0 = half * 1024
        mask = np.zeros((128, 8), np.float32)
        if half == 1:
            mask[:, c - 1] = 1.0
        in_maps.append({
            "x": np.ascontiguousarray(x[c * TPC:(c + 1) * TPC]),
            "wq": wq, "wk": wk, "wv": wv, "wg": wg, "wo": wo,
            "upw": upw, "dnw": dnw,
            "bq": bq, "bk": bk, "bv": bv, "bg": bg, "bo": bo,
            "usc": usc, "dsc": dsc, "osc": osc,
            "rsin": np.ascontiguousarray(sin_f[t0:t0 + 1024]),
            "rcos": np.ascontiguousarray(cos_f[t0:t0 + 1024]),
            "ident": ident,
            "mask8": mask,
        })

    res = run_bass_kernel_spmd(nc, in_maps, core_ids=list(range(NCORE)))
    out = np.concatenate([res.results[c]["y"] for c in range(NCORE)], axis=0)
    if _DEBUG:
        kernel._dbg = [{nm: res.results[c]["dbg_" + nm] for (nm, _s) in _DEBUG}
                       for c in range(NCORE)]
    return out.reshape(4, 2048, D)


# version 15
# speedup vs baseline: 1.3554x; 1.0025x over previous
import numpy as np
from contextlib import ExitStack

import concourse.bass as bass
import concourse.mybir as mybir
import concourse.bass_utils as _bu
from concourse.bass_utils import run_bass_kernel_spmd

if not getattr(_bu, "_ldw_patched", False):
    _orig_run_command = _bu.run_command

    def _run_command_ldw(cmd, *a, **kw):
        if isinstance(cmd, list):
            cmd = ["--enable-ldw-opt=true" if c == "--enable-ldw-opt=false"
                   else c for c in cmd]
        return _orig_run_command(cmd, *a, **kw)

    _bu.run_command = _run_command_ldw
    _bu._ldw_patched = True

F32 = mybir.dt.float32
BF16 = mybir.dt.bfloat16
U8 = mybir.dt.uint8
AF = mybir.ActivationFunctionType
ALU = mybir.AluOpType
AX = mybir.AxisListType

D = 2048
H = 16
DK = 128
CS = 64
TPC = 1024
NT = 8
NK = 16
NCH = 16
NSLAB = 8
DH = 8192
NHG = 4
HPG = 4
NCORE = 8

OFF_XN = 0
OFF_GA = 32768
OFF_ATTN = 32768
OFF_K = (65536, 65536 + 8192)
OFF_V = (65536 + 16384, 65536 + 24576)
OFF_G = 65536 + 32768
OFF_Q = 65536 + 40960
OFF_BIG = 65536 + 49152
BIG_SZ = 82304
ARENA_BYTES = OFF_BIG + BIG_SZ

OFF_XB0 = OFF_BIG + 32768
OFF_XB1 = OFF_BIG + 40960
OFF_X2 = OFF_BIG + 49152
OFF_XNS0 = OFF_BIG + 57344
OFF_XNS1 = OFF_BIG + 61440
OFF_UPS = OFF_BIG
OFF_DNS = OFF_BIG + 32768
OFF_HID = OFF_BIG + 65536
OFF_DSREP = OFF_BIG + 65536
OFF_DSROW = OFF_BIG + 73856
OFF_WSA = OFF_BIG
OFF_WSB = OFF_BIG + 16384
OFF_GFFN = OFF_BIG + 32768
OFF_QTMP = OFF_BIG + 65536
OFF_NUMSB = OFF_BIG + 65536
OFF_RT = OFF_BIG + 73856
OFF_GATH = OFF_BIG + 73856
OFF_WO = OFF_BIG
OFF_P3X0 = OFF_BIG + 65536
OFF_P3X1 = OFF_BIG + 69632
OFF_P3O0 = OFF_BIG + 73856
OFF_P3O1 = OFF_BIG + 77952

ENGS = ("tensor", "vector", "scalar", "gpsimd", "sync")


class _Stop(Exception):
    pass


class Sem:
    __slots__ = ("h", "n")

    def __init__(self, h):
        self.h = h
        self.n = 0


def build_nc(debug=(), upto="full"):
    order = ("p0", "p1", "p2proj", "p2fm", "p2scan", "p2cc", "p2", "full")
    lvl = order.index(upto)
    nc = bass.Bass("TRN2", target_bir_lowering=False,
                   detect_race_conditions=False)

    d_x = nc.dram_tensor("x", [TPC, D], F32, kind="ExternalInput").ap()
    d_w = {nm: nc.dram_tensor(nm, [D, D], F32, kind="ExternalInput").ap()
           for nm in ("wq", "wk", "wv", "wg", "wo")}
    d_up = nc.dram_tensor("upw", [D, DH], F32, kind="ExternalInput").ap()
    d_dn = nc.dram_tensor("dnw", [DH, D], F32, kind="ExternalInput").ap()
    d_b = {nm: nc.dram_tensor(nm, [1, D], F32, kind="ExternalInput").ap()
           for nm in ("bq", "bk", "bv", "bg", "bo")}
    d_usc = nc.dram_tensor("usc", [128, 64], F32, kind="ExternalInput").ap()
    d_dsc = nc.dram_tensor("dsc", [1, D], F32, kind="ExternalInput").ap()
    d_osc = nc.dram_tensor("osc", [1, 1], F32, kind="ExternalInput").ap()
    d_sin = nc.dram_tensor("rsin", [TPC, 64], F32, kind="ExternalInput").ap()
    d_cos = nc.dram_tensor("rcos", [TPC, 64], F32, kind="ExternalInput").ap()
    d_id = nc.dram_tensor("ident", [128, 128], F32, kind="ExternalInput").ap()
    d_msk = nc.dram_tensor("mask8", [128, 8], F32, kind="ExternalInput").ap()
    d_y = nc.dram_tensor("y", [TPC, D], F32, kind="ExternalOutput").ap()
    if lvl >= 5:
        d_ccs = [nc.dram_tensor(f"cc_src{h}", [128, 516], BF16)
                 for h in range(NHG)]
        d_ccd = [nc.dram_tensor(f"cc_dst{h}", [128 * NCORE, 516], BF16,
                                addr_space="Shared") for h in range(NHG)]
        d_wus = nc.dram_tensor("cc_wsrc", [1, 16], BF16)
        d_wud = nc.dram_tensor("cc_wdst", [NCORE, 16], BF16,
                               addr_space="Shared")
    d_dbg = {nm: nc.dram_tensor("dbg_" + nm, list(shape), F32,
                                kind="ExternalOutput").ap()
             for (nm, shape) in debug}

    P = {e: [] for e in ENGS}

    def do(eng, method, *a, inc=(), **kw):
        if method == "dma_start":
            for s, k in inc:
                if k == 16 and s.n:
                    wt(eng, (s, s.n))
        incs = [(s.h, k) for (s, k) in inc]

        def f(E, m=method, a=a, kw=kw, incs=incs):
            ins = getattr(E, m)(*a, **kw)
            for hh, kk in incs:
                ins = ins.then_inc(hh, kk)
        f._incs = incs
        P[eng].append(f)
        for s, k in inc:
            s.n += k

    def inc_last(eng, *pairs):
        f = P[eng][-1]
        for s, k in pairs:
            for idx, (hh, kk) in enumerate(f._incs):
                if hh is s.h:
                    f._incs[idx] = (hh, kk + k)
                    break
            else:
                f._incs.append((s.h, k))
            s.n += k

    def wt(eng, *pairs):
        ps = tuple((s.h, v) for (s, v) in pairs)

        def f(E, ps=ps):
            for hh, vv in ps:
                E.wait_ge(hh, vv)
        P[eng].append(f)

    with ExitStack() as stk:
        arena = stk.enter_context(nc.sbuf_tensor("arena", [128, ARENA_BYTES], U8))
        t_sin = stk.enter_context(nc.sbuf_tensor("t_sin", [128, NT * 64], F32))
        t_cos = stk.enter_context(nc.sbuf_tensor("t_cos", [128, NT * 64], F32))
        t_usc = stk.enter_context(nc.sbuf_tensor("t_usc", [128, 64], F32))
        t_id = stk.enter_context(nc.sbuf_tensor("t_id", [128, 128], BF16))
        t_ones = stk.enter_context(nc.sbuf_tensor("t_ones", [1, 128], BF16))
        t_ones32 = stk.enter_context(nc.sbuf_tensor("t_ones32", [1, 128], F32))
        t_onescol = stk.enter_context(nc.sbuf_tensor("t_onescol", [128, 1], BF16))
        t_bias4 = stk.enter_context(nc.sbuf_tensor("t_bias4", [1, 1024], BF16))
        t_cst = stk.enter_context(nc.sbuf_tensor("t_cst", [128, 6], F32))
        t_stats = stk.enter_context(nc.sbuf_tensor("t_stats", [128, 8], F32))
        t_msk = stk.enter_context(nc.sbuf_tensor("t_msk", [128, 8], F32))
        t_den = stk.enter_context(nc.sbuf_tensor("t_den", [128, NT * HPG], F32))
        t_dtmp = stk.enter_context(nc.sbuf_tensor("t_dtmp", [128, HPG], F32))
        t_rden = stk.enter_context(nc.sbuf_tensor("t_rden", [128, HPG], F32))
        t_szh = stk.enter_context(nc.sbuf_tensor("t_szh", [128, 516], BF16))
        t_sfin = stk.enter_context(nc.sbuf_tensor("t_sfin", [128, 516], F32))
        t_atok = stk.enter_context(nc.sbuf_tensor("t_atok", [128, 512], BF16))
        t_c129 = stk.enter_context(nc.sbuf_tensor("t_c129", [128, 520], F32))
        t_os = stk.enter_context(nc.sbuf_tensor("t_os", [1, 1], F32))
        t_os128 = stk.enter_context(nc.sbuf_tensor("t_os128", [1, 128], F32))
        t_eps = stk.enter_context(nc.sbuf_tensor("t_eps", [1, 2], BF16))

        psw0 = stk.enter_context(nc.psum_tensor("psw0", [128, 2048], F32))
        psw1 = stk.enter_context(nc.psum_tensor("psw1", [128, 2048], F32))

        s_dx = Sem(stk.enter_context(nc.semaphore("s_dx")))
        s_dw = Sem(stk.enter_context(nc.semaphore("s_dw")))
        s_out = Sem(stk.enter_context(nc.semaphore("s_out")))
        s_pe = Sem(stk.enter_context(nc.semaphore("s_pe")))
        s_dv = Sem(stk.enter_context(nc.semaphore("s_dv")))
        s_ac = Sem(stk.enter_context(nc.semaphore("s_ac")))
        s_gp = Sem(stk.enter_context(nc.semaphore("s_gp")))
        s_cc = Sem(stk.enter_context(nc.semaphore("s_cc")))

        def view(off, nelem, dt):
            sz = 4 if dt == F32 else 2
            return arena[:, off:off + nelem * sz].bitcast(dt)

        xn_fm = view(OFF_XN, NK * 1024, BF16)
        ga = view(OFF_GA, NT * 2048, F32)
        attn_fm = view(OFF_ATTN, NK * 1024, BF16)
        k_tok2 = [view(OFF_K[0], NT * 512, BF16), view(OFF_K[1], NT * 512, BF16)]
        v_tok2 = [view(OFF_V[0], NT * 512, BF16), view(OFF_V[1], NT * 512, BF16)]
        g_tok = view(OFF_G, NT * 512, BF16)
        q_fm = view(OFF_Q, HPG * 1024, BF16)
        gffn = view(OFF_GFFN, NT * 2048, BF16)
        up_sgn = view(OFF_UPS, NK * 1024, BF16)
        dn_sgn = view(OFF_DNS, 8 * 2048, BF16)
        hid = view(OFF_HID, 8 * 1024, BF16)
        wsA = view(OFF_WSA, NK * 512, BF16)
        wsB = view(OFF_WSB, NK * 512, BF16)
        qtmp = view(OFF_QTMP, NT * 512, BF16)
        ropeT = [view(OFF_RT + i * 2048, 1024, BF16) for i in range(4)]
        num_sb = view(OFF_NUMSB, NT * 516, BF16)
        gath = view(OFF_GATH, 8 * 516, BF16)
        szh2 = t_c129[:, 0:258].bitcast(BF16)

        try:
            do("gpsimd", "memset", t_ones[:, :], 1.0)
            do("gpsimd", "memset", t_ones32[:, :], 1.0)
            do("gpsimd", "memset", t_onescol[:, :], 1.0)
            do("gpsimd", "memset", t_cst[:, 0:1], 1e-5)
            do("gpsimd", "memset", t_cst[:, 1:2], -0.5)
            do("gpsimd", "memset", t_cst[:, 2:3], 1.0 / D)
            do("gpsimd", "memset", t_cst[:, 3:4], -0.34657359027997264)
            do("gpsimd", "memset", t_cst[:, 4:5], 0.015625)
            do("gpsimd", "memset", t_eps[:, :], 1e-6, inc=((s_gp, 1),))
            m_pre = s_gp.n
            do("gpsimd", "dma_start",
               out=t_sin[:, :].rearrange("p (t f) -> p t f", t=NT),
               in_=d_sin.rearrange("(t p) f -> p t f", p=128), inc=((s_dw, 16),))
            do("gpsimd", "dma_start",
               out=t_cos[:, :].rearrange("p (t f) -> p t f", t=NT),
               in_=d_cos.rearrange("(t p) f -> p t f", p=128), inc=((s_dw, 16),))
            do("gpsimd", "dma_start", out=t_usc[:, :], in_=d_usc, inc=((s_dw, 16),))
            do("gpsimd", "dma_start", out=t_id[:, :], in_=d_id, inc=((s_dw, 16),))
            do("gpsimd", "dma_start", out=t_msk[:, :], in_=d_msk, inc=((s_dw, 16),))
            do("gpsimd", "dma_start", out=t_os[:, :], in_=d_osc, inc=((s_dw, 16),))
            m_tables = s_dw.n
            if lvl >= 5:
                do("gpsimd", "dma_start", out=d_wus[:, :],
                   in_=t_ones[0:1, 0:16], inc=((s_dw, 16),))
                wt("gpsimd", (s_dw, s_dw.n))
                do("gpsimd", "collective_compute", "AllGather", ALU.bypass,
                   replica_groups=[list(range(NCORE))],
                   ins=[d_wus[:, :]], outs=[d_wud[:, :]], inc=((s_cc, 1),))

            def upslab_ap(s):
                return d_up[:, s * 1024:(s + 1) * 1024].rearrange(
                    "(k p) h -> p k h", p=128)

            def dnslab_ap(s):
                return d_dn[s * 1024:(s + 1) * 1024, :].rearrange(
                    "(t p) j -> p t j", p=128)

            do("gpsimd", "dma_start",
               out=up_sgn.rearrange("p (k h) -> p k h", k=NK), in_=upslab_ap(0),
               inc=((s_dw, 16),))
            m_updma = [s_dw.n]

            xb = [view(OFF_XB0, 2048, F32), view(OFF_XB1, 2048, F32)]
            x2 = view(OFF_GA, 2049, F32)
            xns = [view(OFF_XNS0, 2048, BF16), view(OFF_XNS1, 2048, BF16)]
            do("gpsimd", "memset", x2[:, 2048:2049], 5e-6, inc=((s_gp, 1),))
            m_x2e = s_gp.n
            m_ts = [0] * NT
            m_tp4 = [0] * NT
            m_tiny = [0] * NT
            m_tp_ev = [0, 0]

            def emit_ln_tail(j):
                st = t_stats[:, 4 * (j % 2):4 * (j % 2) + 4]
                wts = [(s_ac, m_tiny[j])]
                if j >= 2:
                    wts.append((s_pe, m_tp4[j - 2]))
                wt("vector", *wts)
                do("vector", "tensor_scalar", xns[j % 2], xb[j % 2],
                   st[:, 1:2], st[:, 3:4], ALU.mult, ALU.subtract,
                   inc=((s_dv, 1),))
                m_ts[j] = s_dv.n
                wt("tensor", (s_dv, m_ts[j]), (s_dw, m_tables))
                for kq in range(4):
                    pbank = psw0[:, (kq % 2) * 512:(kq % 2) * 512 + 256].bitcast(BF16)
                    if m_tp_ev[kq % 2]:
                        wt("tensor", (s_dv, m_tp_ev[kq % 2]))
                    for kk in range(4):
                        k = kq * 4 + kk
                        do("tensor", "transpose", pbank[:, kk * 128:(kk + 1) * 128],
                           xns[j % 2][:, k * 128:(k + 1) * 128], t_id[:, :])
                    inc_last("tensor", (s_pe, 1))
                    wt("vector", (s_pe, s_pe.n))
                    do("vector", "tensor_copy",
                       xn_fm.rearrange("p (k t) -> p k t", k=NK)[
                           :, kq * 4:(kq + 1) * 4, j * 128:(j + 1) * 128],
                       pbank.rearrange("p (k t) -> p k t", k=4), inc=((s_dv, 1),))
                    m_tp_ev[kq % 2] = s_dv.n
                m_tp4[j] = s_pe.n

            for j in range(NT):
                if j >= 2:
                    wt("sync", (s_dv, m_ts[j - 2]))
                do("sync", "dma_start", out=xb[j % 2],
                   in_=d_x[j * 128:(j + 1) * 128, :], inc=((s_dx, 16),))
                m_xdma = s_dx.n
                st = t_stats[:, 4 * (j % 2):4 * (j % 2) + 4]
                if j == 0:
                    wt("scalar", (s_dx, m_xdma), (s_gp, m_pre))
                else:
                    wt("scalar", (s_dx, m_xdma))
                do("scalar", "activation", out=x2[:, 0:2048], in_=xb[j % 2],
                   func=AF.Square, scale=t_cst[:, 4:5], inc=((s_ac, 1),))
                m_sq = s_ac.n
                wt("vector", (s_dx, m_xdma))
                do("vector", "reduce_sum", st[:, 3:4], xb[j % 2], axis=AX.X,
                   inc=((s_dv, 1),))
                if j == 0:
                    wt("vector", (s_ac, m_sq), (s_gp, m_x2e))
                else:
                    wt("vector", (s_ac, m_sq))
                do("vector", "reduce_sum", st[:, 1:2], x2, axis=AX.X,
                   inc=((s_dv, 1),))
                m_r2 = s_dv.n
                wt("scalar", (s_dv, m_r2))
                for _ in range(2):
                    do("scalar", "activation", out=st[:, 2:3], in_=st[:, 3:4],
                       func=AF.Copy, scale=t_cst[:, 2:3])
                for _ in range(2):
                    do("scalar", "activation", out=st[:, 3:4], in_=st[:, 2:3],
                       func=AF.Square)
                for _ in range(2):
                    do("scalar", "activation", out=st[:, 0:1], in_=st[:, 3:4],
                       func=AF.Ln, scale=t_cst[:, 1:2], bias=st[:, 1:2])
                for _ in range(2):
                    do("scalar", "activation", out=st[:, 1:2], in_=st[:, 0:1],
                       func=AF.Exp, scale=t_cst[:, 1:2], bias=t_cst[:, 3:4])
                do("scalar", "activation", out=st[:, 3:4], in_=st[:, 2:3],
                   func=AF.Copy, scale=st[:, 1:2], inc=((s_ac, 1),))
                m_tiny[j] = s_ac.n
                if j > 0:
                    emit_ln_tail(j - 1)
            emit_ln_tail(NT - 1)
            m_xn_all = s_dv.n

            wt("gpsimd", (s_dv, m_xn_all))
            do("gpsimd", "dma_start",
               out=dn_sgn.rearrange("p (t j) -> p t j", t=8), in_=dnslab_ap(0),
               inc=((s_dw, 16),))
            m_dndma = [s_dw.n]

            if lvl < 1:
                raise _Stop

            m_upmm = [0] * NSLAB
            m_dnmm = [0] * NSLAB
            m_gadd = {}

            for s in range(NSLAB):
                wt("scalar", (s_dw, m_updma[s]))
                do("scalar", "activation", out=up_sgn, in_=up_sgn, func=AF.Sign,
                   inc=((s_ac, 1),))
                m_upsgn = s_ac.n
                wt("scalar", (s_dw, m_dndma[s]))
                do("scalar", "activation", out=dn_sgn, in_=dn_sgn, func=AF.Sign,
                   inc=((s_ac, 1),))
                m_dnsgn = s_ac.n

                if s == 0:
                    wt("tensor", (s_ac, m_upsgn), (s_dv, m_xn_all))
                else:
                    wt("tensor", (s_ac, m_upsgn), (s_dv, m_gadd[(s - 1, NT - 1)]))
                ev_ms = {}
                for ht in range(8):
                    pA = psw0[:, (ht % 2) * 1024:(ht % 2) * 1024 + 512]
                    pB = psw0[:, (ht % 2) * 1024 + 512:(ht % 2) * 1024 + 1024]
                    if ht >= 2:
                        wt("tensor", (s_dv, ev_ms[ht - 2]))
                    for k in range(NK):
                        lhsT = up_sgn[:, k * 1024 + ht * 128:k * 1024 + (ht + 1) * 128]
                        do("tensor", "matmul", pA, lhsT,
                           xn_fm[:, k * 1024:k * 1024 + 512],
                           start=(k == 0), stop=(k == NK - 1), skip_group_check=True)
                        do("tensor", "matmul", pB, lhsT,
                           xn_fm[:, k * 1024 + 512:k * 1024 + 1024],
                           start=(k == 0), stop=(k == NK - 1), skip_group_check=True)
                    inc_last("tensor", (s_pe, 1))
                    wt("vector", (s_pe, s_pe.n))
                    do("vector", "tensor_scalar", hid[:, ht * 1024:ht * 1024 + 512],
                       pA, t_usc[:, s * 8 + ht:s * 8 + ht + 1], 0.0, ALU.mult, ALU.max)
                    do("vector", "tensor_scalar",
                       hid[:, ht * 1024 + 512:(ht + 1) * 1024],
                       pB, t_usc[:, s * 8 + ht:s * 8 + ht + 1], 0.0, ALU.mult, ALU.max,
                       inc=((s_dv, 1),))
                    ev_ms[ht] = s_dv.n
                inc_last("tensor", (s_pe, 1))
                m_upmm[s] = s_pe.n
                m_hid_done = ev_ms[7]

                if s + 1 < NSLAB:
                    wt("gpsimd", (s_pe, m_upmm[s]))
                    do("gpsimd", "dma_start",
                       out=up_sgn.rearrange("p (k h) -> p k h", k=NK),
                       in_=upslab_ap(s + 1), inc=((s_dw, 16),))
                    m_updma.append(s_dw.n)

                wt("tensor", (s_dv, m_hid_done), (s_ac, m_dnsgn))
                for tt in range(NT):
                    ps = psw0 if tt % 2 == 0 else psw1
                    if tt >= 2:
                        wt("tensor", (s_dv, m_gadd[(s, tt - 2)]))
                    for ht in range(8):
                        lhsT = hid[:, ht * 1024 + tt * 128:ht * 1024 + (tt + 1) * 128]
                        for jc in range(4):
                            do("tensor", "matmul", ps[:, jc * 512:(jc + 1) * 512],
                               lhsT,
                               dn_sgn[:, ht * 2048 + jc * 512:ht * 2048 + (jc + 1) * 512],
                               start=(ht == 0), stop=(ht == 7), skip_group_check=True)
                    inc_last("tensor", (s_pe, 1))
                    wt("vector", (s_pe, s_pe.n))
                    if s == 0:
                        do("vector", "tensor_copy", ga[:, tt * 2048:(tt + 1) * 2048],
                           ps[:, :], inc=((s_dv, 1),))
                    else:
                        do("vector", "tensor_tensor", ga[:, tt * 2048:(tt + 1) * 2048],
                           ga[:, tt * 2048:(tt + 1) * 2048], ps[:, :], ALU.add,
                           inc=((s_dv, 1),))
                    m_gadd[(s, tt)] = s_dv.n
                inc_last("tensor", (s_pe, 1))
                m_dnmm[s] = s_pe.n

                if s + 1 < NSLAB:
                    wt("gpsimd", (s_pe, m_dnmm[s]))
                    do("gpsimd", "dma_start",
                       out=dn_sgn.rearrange("p (t j) -> p t j", t=8),
                       in_=dnslab_ap(s + 1), inc=((s_dw, 16),))
                    m_dndma.append(s_dw.n)

            ds_rep = view(OFF_DSREP, 2048, F32)
            ds_tmp = view(OFF_DSROW, 2048, F32)
            wt("sync", (s_pe, m_dnmm[NSLAB - 1]))
            do("sync", "dma_start", out=ds_tmp[0:1, :], in_=d_dsc, inc=((s_dx, 16),))
            m_dsrow = s_dx.n
            wt("vector", (s_dw, m_tables))
            do("vector", "tensor_scalar", t_os128[0:1, :], t_ones32[0:1, :],
               t_os[0:1, 0:1], None, ALU.mult, inc=((s_dv, 1),))
            m_os128 = s_dv.n
            wt("tensor", (s_dv, m_os128), (s_dx, m_dsrow))
            for jc in range(4):
                do("tensor", "matmul", psw0[:, jc * 512:(jc + 1) * 512],
                   t_os128[0:1, :], ds_tmp[0:1, jc * 512:(jc + 1) * 512],
                   start=True, stop=True, skip_group_check=True)
            inc_last("tensor", (s_pe, 1))
            wt("vector", (s_pe, s_pe.n), (s_dv, m_gadd[(NSLAB - 1, NT - 1)]))
            do("vector", "tensor_copy", ds_rep, psw0[:, :], inc=((s_dv, 1),))
            m_silu = 0
            for tt in range(NT):
                if m_silu:
                    wt("vector", (s_ac, m_silu))
                do("vector", "tensor_tensor", ds_tmp, ga[:, tt * 2048:(tt + 1) * 2048],
                   ds_rep, ALU.mult, inc=((s_dv, 1),))
                wt("scalar", (s_dv, s_dv.n))
                do("scalar", "activation", out=gffn[:, tt * 2048:(tt + 1) * 2048],
                   in_=ds_tmp, func=AF.Silu, inc=((s_ac, 1),))
                m_silu = s_ac.n
            m_gffn = s_ac.n
            m_dsrep_read = s_dv.n

            if lvl < 2:
                raise _Stop


            sA = psw1[:, 0:512]
            sB = psw1[:, 512:1024]
            tp_bf = psw1[:, 1024:1280].bitcast(BF16)
            tp_bf2 = psw1[:, 1536:1792].bitcast(BF16)

            def wslab_ap(wname, hg):
                return d_w[wname][:, hg * 512:(hg + 1) * 512].rearrange(
                    "(k p) j -> p k j", p=128)

            m_gate = [0] * NHG
            m_kvmm = [0] * (NHG + 1)
            ev_kv = [dict() for _ in range(NHG + 1)]
            m_b4 = [0] * NHG
            m_gqmm = [0] * NHG

            wt("gpsimd", (s_pe, m_upmm[NSLAB - 1]))
            do("gpsimd", "dma_start", out=wsA.rearrange("p (k j) -> p k j", k=NK),
               in_=wslab_ap("wk", 0), inc=((s_dw, 16),))
            do("gpsimd", "dma_start", out=wsB.rearrange("p (k j) -> p k j", k=NK),
               in_=wslab_ap("wv", 0), inc=((s_dw, 16),))
            m_wkv = s_dw.n
            for bi, bn in enumerate(("bk", "bv")):
                do("gpsimd", "dma_start",
                   out=t_bias4[0:1, bi * 512:(bi + 1) * 512],
                   in_=d_b[bn][0:1, 0:512], inc=((s_dw, 16),))
            m_b4[0] = s_dw.n

            def emit_kv_group(hg, tt, m_w):
                ks = k_tok2[hg % 2]
                vs = v_tok2[hg % 2]
                kps = psw0[:, 0:512]
                vps = psw0[:, 512:1024]
                if tt == 0:
                    wt("tensor", (s_dw, m_w), (s_dw, m_b4[hg]),
                       (s_dv, m_dsrep_read))
                else:
                    wt("tensor", (s_dv, ev_kv[hg][tt - 1]))
                do("tensor", "matmul", kps, t_ones[0:1, :],
                   t_bias4[0:1, 0:512], start=True, stop=False,
                   skip_group_check=True)
                do("tensor", "matmul", vps, t_ones[0:1, :],
                   t_bias4[0:1, 512:1024], start=True, stop=False,
                   skip_group_check=True)
                for k in range(NK):
                    lhsT = xn_fm[:, k * 1024 + tt * 128:k * 1024 + (tt + 1) * 128]
                    do("tensor", "matmul", kps, lhsT,
                       wsA[:, k * 512:(k + 1) * 512],
                       start=False, stop=(k == NK - 1), skip_group_check=True)
                    do("tensor", "matmul", vps, lhsT,
                       wsB[:, k * 512:(k + 1) * 512],
                       start=False, stop=(k == NK - 1), skip_group_check=True)
                inc_last("tensor", (s_pe, 1))
                wt("vector", (s_pe, s_pe.n))
                do("vector", "tensor_copy", ks[:, tt * 512:(tt + 1) * 512], kps)
                do("vector", "tensor_copy", vs[:, tt * 512:(tt + 1) * 512], vps,
                   inc=((s_dv, 1),))
                ev_kv[hg][tt] = s_dv.n

            def emit_q_proj(hg, m_wq, m_bq, m_qdst_free):
                qdst = v_tok2[(hg + 1) % 2]
                ev_prev = [0, 0]
                for tt in range(NT):
                    qps = psw0[:, 1024 + (tt % 2) * 512:1536 + (tt % 2) * 512]
                    if tt == 0:
                        wt("tensor", (s_dw, m_wq), (s_dv, m_qdst_free))
                    elif tt >= 2:
                        wt("tensor", (s_dv, ev_prev[tt % 2]))
                    for k in range(NK):
                        lhsT = xn_fm[:, k * 1024 + tt * 128:k * 1024 + (tt + 1) * 128]
                        do("tensor", "matmul", qps, lhsT,
                           wsB[:, k * 512:(k + 1) * 512],
                           start=(k == 0), stop=False, skip_group_check=True)
                    if tt == 0:
                        wt("tensor", (s_dw, m_bq))
                    do("tensor", "matmul", qps, t_ones[0:1, :],
                       t_bias4[0:1, 512:1024], start=False, stop=True,
                       skip_group_check=True)
                    inc_last("tensor", (s_pe, 1))
                    wt("vector", (s_pe, s_pe.n))
                    do("vector", "tensor_copy", qdst[:, tt * 512:(tt + 1) * 512],
                       qps, inc=((s_dv, 1),))
                    ev_prev[tt % 2] = s_dv.n
                inc_last("tensor", (s_pe, 1))
                return s_pe.n, s_dv.n

            ev_k = [dict() for _ in range(NHG + 1)]
            ev_v = [dict() for _ in range(NHG + 1)]
            m_ktok = [0] * (NHG + 1)

            def emit_k_group(hg, tt, m_w, m_b):
                ks = k_tok2[hg % 2]
                kps = psw0[:, 0:512]
                if tt == 0:
                    wt("tensor", (s_dw, m_w), (s_dw, m_b))
                else:
                    wt("tensor", (s_dv, ev_k[hg][tt - 1]))
                for k in range(NK):
                    lhsT = xn_fm[:, k * 1024 + tt * 128:k * 1024 + (tt + 1) * 128]
                    do("tensor", "matmul", kps, lhsT,
                       wsB[:, k * 512:(k + 1) * 512],
                       start=(k == 0), stop=False, skip_group_check=True)
                do("tensor", "matmul", kps, t_ones[0:1, :],
                   t_bias4[0:1, 0:512], start=False, stop=True,
                   skip_group_check=True)
                inc_last("tensor", (s_pe, 1))
                wt("vector", (s_pe, s_pe.n))
                do("vector", "tensor_copy", ks[:, tt * 512:(tt + 1) * 512], kps,
                   inc=((s_dv, 1),))
                ev_k[hg][tt] = s_dv.n

            def emit_v_group(hg, tt, m_w):
                vs = v_tok2[hg % 2]
                vps = psw0[:, 512:1024]
                if tt == 0:
                    wt("tensor", (s_dw, m_w))
                else:
                    wt("tensor", (s_dv, ev_v[hg][tt - 1]))
                for k in range(NK):
                    lhsT = xn_fm[:, k * 1024 + tt * 128:k * 1024 + (tt + 1) * 128]
                    do("tensor", "matmul", vps, lhsT,
                       wsA[:, k * 512:(k + 1) * 512],
                       start=(k == 0), stop=False, skip_group_check=True)
                do("tensor", "matmul", vps, t_ones[0:1, :],
                   t_bias4[0:1, 512:1024], start=False, stop=True,
                   skip_group_check=True)
                inc_last("tensor", (s_pe, 1))
                wt("vector", (s_pe, s_pe.n))
                do("vector", "tensor_copy", vs[:, tt * 512:(tt + 1) * 512], vps,
                   inc=((s_dv, 1),))
                ev_v[hg][tt] = s_dv.n

            for tt in range(NT):
                emit_kv_group(0, tt, m_wkv)
            inc_last("tensor", (s_pe, 1))
            m_kvmm[0] = s_pe.n
            m_ktok[0] = ev_kv[0][NT - 1]

            wt("gpsimd", (s_pe, m_kvmm[0]))
            do("gpsimd", "dma_start", out=wsB.rearrange("p (k j) -> p k j", k=NK),
               in_=wslab_ap("wq", 0), inc=((s_dw, 16),))
            do("gpsimd", "dma_start",
               out=t_bias4[0:1, 512:1024], in_=d_b["bq"][0:1, 0:512],
               inc=((s_dw, 16),))
            m_wq = s_dw.n
            do("gpsimd", "dma_start", out=wsA.rearrange("p (k j) -> p k j", k=NK),
               in_=wslab_ap("wg", 0), inc=((s_dw, 16),))
            do("gpsimd", "dma_start",
               out=t_bias4[0:1, 0:512], in_=d_b["bg"][0:1, 0:512],
               inc=((s_dw, 16),))
            m_wg = s_dw.n
            m_qmm, m_qev = emit_q_proj(0, m_wq, m_wq, m_dsrep_read)

            def rope_hb(buf, hb):
                b4 = buf.rearrange("p (t h f) -> p t h f", t=NT, h=HPG)
                ropeT = [view(OFF_RT + i * 2048, 1024, BF16) for i in range(4)]
                x1 = b4[:, hb * 4:(hb + 1) * 4, :, 0:64]
                x2 = b4[:, hb * 4:(hb + 1) * 4, :, 64:128]
                cosb = t_cos[:, :].rearrange("p (t f) -> p t f", t=NT)[
                    :, hb * 4:(hb + 1) * 4, :].unsqueeze(2).broadcast_to(
                    [128, 4, HPG, 64])
                sinb = t_sin[:, :].rearrange("p (t f) -> p t f", t=NT)[
                    :, hb * 4:(hb + 1) * 4, :].unsqueeze(2).broadcast_to(
                    [128, 4, HPG, 64])
                r4 = [r.rearrange("p (t h f) -> p t h f", t=4, h=HPG)
                      for r in ropeT]
                do("vector", "tensor_tensor", r4[0], x1, cosb, ALU.mult)
                do("vector", "tensor_tensor", r4[1], x1, sinb, ALU.mult)
                do("vector", "tensor_tensor", r4[2], x2, sinb, ALU.mult)
                do("vector", "tensor_tensor", r4[3], x2, cosb, ALU.mult)
                do("vector", "tensor_tensor", x1, r4[0], r4[2], ALU.subtract)
                do("vector", "tensor_tensor", x2, r4[1], r4[3], ALU.add,
                   inc=((s_dv, 1),))

            def elu_q(buf, q4):
                ropeT = [view(OFF_RT + i * 2048, 1024, BF16) for i in range(4)]
                tmin = ropeT[0]
                texp = ropeT[2]
                sl = buf[:, q4 * 1024:(q4 + 1) * 1024]
                do("vector", "tensor_single_scalar", tmin, sl, 0.0, ALU.min,
                   inc=((s_dv, 1),))
                wt("scalar", (s_dv, s_dv.n))
                do("scalar", "activation", out=texp, in_=tmin, func=AF.Exp,
                   inc=((s_ac, 1),))
                wt("vector", (s_ac, s_ac.n))
                do("vector", "scalar_tensor_tensor", sl, sl, 0.0, texp,
                   ALU.max, ALU.add, inc=((s_dv, 1),))

            for hg in range(NHG):
                slot = hg % 2
                k_tok = k_tok2[slot]
                v_tok = v_tok2[slot]
                qtok = v_tok2[(hg + 1) % 2]

                if hg + 1 < NHG:
                    wt("gpsimd", (s_pe, m_qmm))
                    do("gpsimd", "dma_start",
                       out=wsB.rearrange("p (k j) -> p k j", k=NK),
                       in_=wslab_ap("wk", hg + 1), inc=((s_dw, 16),))
                    do("gpsimd", "dma_start",
                       out=t_bias4[0:1, 512:1024],
                       in_=d_b["bv"][0:1, (hg + 1) * 512:(hg + 2) * 512],
                       inc=((s_dw, 16),))
                    m_wkb = s_dw.n

                fm_stage = 0
                wt("tensor", (s_dw, m_wg))
                if hg > 0:
                    wt("tensor", (s_dv, m_gate[hg - 1]),
                       (s_ac, m_gatepsw0))
                else:
                    wt("tensor", (s_dv, ev_kv[0][NT - 1]), (s_dv, m_qev))
                ev_g = [0, 0]
                for tt in range(NT):
                    gps = psw0[:, (tt % 4) * 512:(tt % 4) * 512 + 512]
                    if tt >= 4:
                        wt("tensor", (s_ac, ev_g[(tt // 2) % 2]))
                    do("tensor", "matmul", gps, t_ones[0:1, :],
                       t_bias4[0:1, 0:512], start=True, stop=False,
                       skip_group_check=True)
                    for k in range(NK):
                        lhsT = xn_fm[:, k * 1024 + tt * 128:k * 1024 + (tt + 1) * 128]
                        do("tensor", "matmul", gps, lhsT,
                           wsA[:, k * 512:(k + 1) * 512],
                           start=False, stop=(k == NK - 1), skip_group_check=True)
                    inc_last("tensor", (s_pe, 1))
                    if tt % 2 == 1:
                        if tt == 1 and hg > 0:
                            wt("scalar", (s_pe, s_pe.n), (s_dv, m_gate[hg - 1]))
                        else:
                            wt("scalar", (s_pe, s_pe.n))
                        for t2 in (tt - 1, tt):
                            do("scalar", "activation",
                               out=g_tok[:, t2 * 512:(t2 + 1) * 512],
                               in_=psw0[:, (t2 % 4) * 512:(t2 % 4) * 512 + 512],
                               func=AF.Sigmoid,
                               inc=((s_ac, 1),) if t2 == tt else ())
                        ev_g[(tt // 2) % 2] = s_ac.n
                    if fm_stage < 8:
                        if fm_stage == 0:
                            wts = [(s_dv, m_ktok[hg]), (s_dw, m_tables)]
                            if hg == 0:
                                wts.append((s_ac, m_gffn))
                            wt("vector", *wts)
                            rope_hb(k_tok, 0)
                        elif fm_stage == 1:
                            rope_hb(k_tok, 1)
                        elif fm_stage == 2:
                            wt("vector", (s_dv, m_qev))
                            rope_hb(qtok, 0)
                        elif fm_stage == 3:
                            rope_hb(qtok, 1)
                        elif fm_stage == 4:
                            elu_q(k_tok, 0)
                            elu_q(k_tok, 1)
                        elif fm_stage == 5:
                            elu_q(k_tok, 2)
                            elu_q(k_tok, 3)
                        elif fm_stage == 6:
                            elu_q(qtok, 0)
                            elu_q(qtok, 1)
                        else:
                            elu_q(qtok, 2)
                            elu_q(qtok, 3)
                        fm_stage += 1
                inc_last("tensor", (s_pe, 1))
                m_gqmm[hg] = s_pe.n
                m_fmk = 0
                m_fmq = s_dv.n
                m_sig_last = s_ac.n

                if lvl == 2:
                    raise _Stop
                if lvl == 3:
                    raise _Stop

                wt("tensor", (s_dv, m_fmq))
                m_qT_ev = [0, 0]
                for tt in range(NT):
                    tp = tp_bf if tt % 2 == 0 else tp_bf2
                    if m_qT_ev[tt % 2]:
                        wt("tensor", (s_dv, m_qT_ev[tt % 2]))
                    for hh in range(HPG):
                        do("tensor", "transpose",
                           tp[:, hh * 128:(hh + 1) * 128],
                           qtok[:, tt * 512 + hh * 128:tt * 512 + (hh + 1) * 128],
                           t_id[:, :])
                    inc_last("tensor", (s_pe, 1))
                    wt("vector", (s_pe, s_pe.n))
                    do("vector", "tensor_copy",
                       q_fm.rearrange("p (h t) -> p h t", h=HPG)[
                           :, :, tt * 128:(tt + 1) * 128],
                       tp.rearrange("p (h t) -> p h t", h=HPG),
                       inc=((s_dv, 1),))
                    m_qT_ev[tt % 2] = s_dv.n
                m_qT = s_dv.n

                if hg + 1 < NHG:
                    wt("gpsimd", (s_pe, m_gqmm[hg]))
                    do("gpsimd", "dma_start",
                       out=t_bias4[0:1, 0:512],
                       in_=d_b["bk"][0:1, (hg + 1) * 512:(hg + 2) * 512],
                       inc=((s_dw, 16),))
                    m_bk = s_dw.n
                    do("gpsimd", "dma_start",
                       out=wsA.rearrange("p (k j) -> p k j", k=NK),
                       in_=wslab_ap("wv", hg + 1), inc=((s_dw, 16),))
                    m_wv = s_dw.n

                m_snap = {}
                m_numev = {}
                szh_buf = [t_szh, szh2]

                def emit_num(j):
                    jt, jhalf = j // 2, j % 2
                    jpb = jhalf * 64
                    wt("tensor", (s_dv, m_snap[j]))
                    if j == 0:
                        wt("tensor", (s_dv, m_qT))
                    if j == 1:
                        wt("tensor", (s_ac, m_sig_last))
                    if j >= 2:
                        wt("tensor", (s_dv, m_numev[j - 2]))
                    npair = (psw1[:, 1024:2048] if jhalf == 0
                             else psw0[:, 1024:2048])
                    szh = szh_buf[j % 2]
                    for hh in range(HPG):
                        hb, hl = hh // 2, hh % 2
                        do("tensor", "matmul",
                           npair[jpb:jpb + 64,
                                 hb * 512 + hl * 128:hb * 512 + hl * 128 + 128],
                           q_fm[:, hh * 1024 + j * 64:hh * 1024 + (j + 1) * 64],
                           szh[:, hh * 129:hh * 129 + 128],
                           start=(hl == 0), stop=False, skip_group_check=True,
                           tile_position=(0, jpb) if jpb else None)
                    for hh in range(HPG):
                        hb, hl = hh // 2, hh % 2
                        do("tensor", "matmul",
                           npair[jpb:jpb + 64,
                                 hb * 512 + 256 + hl:hb * 512 + 257 + hl],
                           q_fm[:, hh * 1024 + j * 64:hh * 1024 + (j + 1) * 64],
                           szh[:, hh * 129 + 128:hh * 129 + 129],
                           start=False, stop=(hl == 1), skip_group_check=True,
                           tile_position=(0, jpb) if jpb else None)
                    inc_last("tensor", (s_pe, 1))
                    wt("vector", (s_pe, s_pe.n))
                    n4 = npair.rearrange("p (b f) -> p b f", b=2)
                    dst = num_sb[jpb:jpb + 64,
                                 jt * 516:(jt + 1) * 516].rearrange(
                        "p (bh f) -> p bh f", f=129).rearrange(
                        "p (b h) f -> p b h f", b=2)
                    do("vector", "tensor_copy", dst[:, :, :, 0:128],
                       n4[jpb:jpb + 64, :, 0:256].rearrange(
                           "p b (h f) -> p b h f", h=2))
                    do("vector", "tensor_copy", dst[:, :, :, 128:129],
                       n4[jpb:jpb + 64, :, 256:258].rearrange(
                           "p b (h f) -> p b h f", h=2), inc=((s_dv, 1),))
                    m_numev[j] = s_dv.n

                for i in range(NCH):
                    tt, half = i // 2, i % 2
                    pb = half * 64
                    if i > 0:
                        wt("tensor", (s_dv, m_snap[i - 1]))
                    else:
                        wt("tensor", (s_dv, m_fmq))
                    for hh in range(HPG):
                        sbank = sA if hh < 2 else sB
                        hl = hh % 2
                        do("tensor", "matmul",
                           sbank[:, hl * 128:hl * 128 + 128],
                           k_tok[pb:pb + 64,
                                 tt * 512 + hh * 128:tt * 512 + (hh + 1) * 128],
                           v_tok[pb:pb + 64,
                                 tt * 512 + hh * 128:tt * 512 + (hh + 1) * 128],
                           start=(i == 0 and hl == 0), stop=False,
                           skip_group_check=True)
                    for hh in range(HPG):
                        sbank = sA if hh < 2 else sB
                        hl = hh % 2
                        do("tensor", "matmul", sbank[:, 256 + hl:257 + hl],
                           k_tok[pb:pb + 64,
                                 tt * 512 + hh * 128:tt * 512 + (hh + 1) * 128],
                           t_onescol[pb:pb + 64, 0:1],
                           start=False, stop=(i == NCH - 1 and hl == 1),
                           skip_group_check=True)
                    inc_last("tensor", (s_pe, 1))
                    m_mm1 = s_pe.n
                    wt("vector", (s_pe, m_mm1))
                    spair = psw1[:, 0:1024].rearrange("p (b f) -> p b f", b=2)
                    zdst = szh_buf[i % 2][:, 0:516].rearrange(
                        "p (bh f) -> p bh f", f=129).rearrange(
                        "p (b h) f -> p b h f", b=2)
                    do("vector", "tensor_copy", zdst[:, :, :, 0:128],
                       spair[:, :, 0:256].rearrange("p b (h f) -> p b h f", h=2))
                    do("vector", "tensor_copy", zdst[:, :, :, 128:129],
                       spair[:, :, 256:258].rearrange("p b (h f) -> p b h f", h=2),
                       inc=((s_dv, 1),))
                    m_snap[i] = s_dv.n
                    if i == NCH - 1:
                        t_sf16 = gath[:, 0:516]
                        fdst = t_sf16.rearrange(
                            "p (bh f) -> p bh f", f=129).rearrange(
                            "p (b h) f -> p b h f", b=2)
                        do("vector", "tensor_copy", fdst[:, :, :, 0:128],
                           spair[:, :, 0:256].rearrange(
                               "p b (h f) -> p b h f", h=2))
                        do("vector", "tensor_copy", fdst[:, :, :, 128:129],
                           spair[:, :, 256:258].rearrange(
                               "p b (h f) -> p b h f", h=2), inc=((s_dv, 1),))
                        m_sfin = s_dv.n
                        wt("sync", (s_dv, m_sfin))
                        do("sync", "dma_start", out=d_ccs[hg][:, :],
                           in_=t_sf16, inc=((s_dx, 16),))
                        m_stg = s_dx.n
                        wt("gpsimd", (s_dx, m_stg))
                        do("gpsimd", "collective_compute", "AllGather",
                           ALU.bypass, replica_groups=[list(range(NCORE))],
                           ins=[d_ccs[hg][:, :]], outs=[d_ccd[hg][:, :]],
                           inc=((s_cc, 1),))
                    if hg + 1 < NHG and 1 <= i <= 8:
                        emit_k_group(hg + 1, i - 1, m_wkb, m_bk)
                        if i == 8:
                            inc_last("tensor", (s_pe, 1))
                            m_kmm = s_pe.n
                            m_ktok[hg + 1] = ev_k[hg + 1][NT - 1]
                            wt("gpsimd", (s_pe, m_kmm))
                            do("gpsimd", "dma_start",
                               out=wsB.rearrange("p (k j) -> p k j", k=NK),
                               in_=wslab_ap("wq", hg + 1), inc=((s_dw, 16),))
                            do("gpsimd", "dma_start",
                               out=t_bias4[0:1, 0:512],
                               in_=d_b["bg"][0:1,
                                             (hg + 1) * 512:(hg + 2) * 512],
                               inc=((s_dw, 16),))
                            m_wq = s_dw.n
                    if hg + 1 < NHG and 9 <= i <= 15:
                        emit_v_group(hg + 1, i - 9, m_wv)
                    if i > 0:
                        emit_num(i - 1)
                if hg + 1 < NHG:
                    emit_v_group(hg + 1, 7, m_wv)
                    inc_last("tensor", (s_pe, 1))
                    m_vmm = s_pe.n
                emit_num(NCH - 1)
                m_scan = s_pe.n
                if hg + 1 < NHG:
                    wt("gpsimd", (s_pe, m_vmm))
                    do("gpsimd", "dma_start",
                       out=t_bias4[0:1, 512:1024],
                       in_=d_b["bq"][0:1, (hg + 1) * 512:(hg + 2) * 512],
                       inc=((s_dw, 16),))
                    m_bq = s_dw.n

                if lvl == 4:
                    raise _Stop

                if hg + 1 < NHG:
                    m_qmm, m_qev = emit_q_proj(hg + 1, m_wq, m_bq,
                                               m_numev[NCH - 1])
                wt("sync", (s_cc, s_cc.n), (s_dv, m_fmq))
                do("sync", "dma_start",
                   out=gath[:, 0:2064].rearrange("p (s f) -> p s f", s=4),
                   in_=d_ccd[hg].rearrange("(u v p) f -> p u v f", v=2, p=128)[
                       :, :, 0, :],
                   inc=((s_dx, 16),))
                m_gath = s_dx.n
                wt("vector", (s_dx, m_gath), (s_pe, m_scan))
                do("vector", "tensor_tensor",
                   gath[:, 0:2064].rearrange("p (s f) -> p s f", s=4),
                   gath[:, 0:2064].rearrange("p (s f) -> p s f", s=4),
                   t_msk[:, :].rearrange("p (u v) -> p u v", v=2)[
                       :, :, 0].unsqueeze(2).broadcast_to([128, 4, 516]),
                   ALU.mult)
                do("vector", "tensor_tensor", gath[:, 0:1032], gath[:, 0:1032],
                   gath[:, 1032:2064], ALU.add)
                do("vector", "tensor_tensor", t_szh[:, 0:516], gath[:, 0:516],
                   gath[:, 516:1032], ALU.add, inc=((s_dv, 1),))
                m_szh = s_dv.n
                if hg + 1 < NHG:
                    wt("gpsimd", (s_dv, m_szh), (s_pe, m_vmm))
                    do("gpsimd", "dma_start",
                       out=wsA.rearrange("p (k j) -> p k j", k=NK),
                       in_=wslab_ap("wg", hg + 1), inc=((s_dw, 16),))
                    m_wg = s_dw.n

                if lvl == 5:
                    raise _Stop

                gtp = [psw1[:, 0:256].bitcast(BF16),
                       psw1[:, 512:768].bitcast(BF16)]
                atok2 = [t_atok[:, 0:512], view(OFF_GATH + 4128, 512, BF16)]
                c129_b = [t_c129[:, 0:516], view(OFF_GATH + 5152, 516, F32)]
                m_c1war = m_qev if hg + 1 < NHG else m_numev[NCH - 1]
                m_corr = {}
                m_cfree = {}
                m_atok = {}
                m_attcopy = {}

                def emit_corr(tt):
                    if tt == 0:
                        wt("tensor", (s_dv, m_szh))
                    elif tt == 1:
                        wt("tensor", (s_dv, m_c1war))
                    else:
                        wt("tensor", (s_ac, m_cfree[tt - 2]))
                    cpair = (psw1[:, 1024:2048] if tt % 2 == 0
                             else psw0[:, 1024:2048])
                    for hh in range(HPG):
                        hb, hl = hh // 2, hh % 2
                        do("tensor", "matmul",
                           cpair[:, hb * 512 + hl * 128:hb * 512 + hl * 128 + 128],
                           q_fm[:, hh * 1024 + tt * 128:hh * 1024 + (tt + 1) * 128],
                           t_szh[:, hh * 129:hh * 129 + 128],
                           start=(hl == 0), stop=False, skip_group_check=True)
                    for hh in range(HPG):
                        hb, hl = hh // 2, hh % 2
                        do("tensor", "matmul",
                           cpair[:, hb * 512 + 256 + hl:hb * 512 + 257 + hl],
                           q_fm[:, hh * 1024 + tt * 128:hh * 1024 + (tt + 1) * 128],
                           t_szh[:, hh * 129 + 128:hh * 129 + 129],
                           start=False, stop=False, skip_group_check=True)
                    for hb in range(2):
                        do("tensor", "matmul",
                           cpair[:, hb * 512 + 256:hb * 512 + 258],
                           t_ones[0:1, :], t_eps[0:1, :],
                           start=False, stop=True, skip_group_check=True)
                    inc_last("tensor", (s_pe, 1))
                    m_corr[tt] = s_pe.n
                    return cpair

                def emit_gate_dve(tt, cpair):
                    c129 = c129_b[tt % 2].rearrange(
                        "p (bh f) -> p bh f", f=129).rearrange(
                        "p (b h) f -> p b h f", b=2)
                    c4 = cpair.rearrange("p (b f) -> p b f", b=2)
                    wt("scalar", (s_pe, m_corr[tt]))
                    do("scalar", "copy", c129[:, :, :, 128:129],
                       c4[:, :, 256:258].rearrange("p b (h f) -> p b h f", h=2))
                    do("scalar", "copy", c129[:, :, :, 0:128],
                       c4[:, :, 0:256].rearrange("p b (h f) -> p b h f", h=2),
                       inc=((s_ac, 1),))
                    m_cfree[tt] = s_ac.n
                    t129 = t_sfin[:, 0:516].rearrange(
                        "p (bh f) -> p bh f", f=129).rearrange(
                        "p (b h) f -> p b h f", b=2)
                    wt("vector", (s_ac, m_cfree[tt]))
                    do("vector", "tensor_tensor", t129,
                       num_sb[:, tt * 516:(tt + 1) * 516].rearrange(
                           "p (bh f) -> p bh f", f=129).rearrange(
                           "p (b h) f -> p b h f", b=2),
                       c129, ALU.add, inc=((s_dv, 1),))
                    wt("scalar", (s_dv, s_dv.n))
                    do("scalar", "copy", t_dtmp[:, :],
                       t129[:, :, :, 128:129].rearrange("p b h f -> p (b h f)"),
                       inc=((s_ac, 1),))
                    m_den = s_ac.n
                    a4 = atok2[tt % 2].rearrange(
                        "p (b h f) -> p b h f", b=2, h=2)
                    g4 = g_tok[:, tt * 512:(tt + 1) * 512].rearrange(
                        "p (b h f) -> p b h f", b=2, h=2)
                    do("vector", "tensor_tensor", a4[:, 0:1],
                       t129[:, 0:1, :, 0:128], g4[:, 0:1], ALU.mult)
                    wt("vector", (s_ac, m_den))
                    do("vector", "reciprocal", t_rden[:, :], t_dtmp[:, :])
                    do("vector", "tensor_tensor", a4[:, 1:2],
                       t129[:, 1:2, :, 0:128], g4[:, 1:2], ALU.mult)
                    do("vector", "tensor_tensor", a4, a4,
                       t_rden[:, :].rearrange("p (b h) -> p b h", b=2).unsqueeze(
                           3).broadcast_to([128, 2, 2, 128]), ALU.mult,
                       inc=((s_dv, 1),))
                    m_atok[tt] = s_dv.n

                def emit_attnT(tt):
                    wts = [(s_dv, m_atok[tt])]
                    if tt >= 2:
                        wts.append((s_dv, m_attcopy[tt - 2]))
                    wt("tensor", *wts)
                    tp = gtp[tt % 2]
                    src = atok2[tt % 2]
                    for hh in range(HPG):
                        do("tensor", "transpose",
                           tp[:, hh * 128:(hh + 1) * 128],
                           src[:, hh * 128:(hh + 1) * 128], t_id[:, :])
                    inc_last("tensor", (s_pe, 1))
                    wt("vector", (s_pe, s_pe.n))
                    do("vector", "tensor_copy",
                       attn_fm.rearrange("p (a t) -> p a t", a=NK)[
                           :, hg * 4:(hg + 1) * 4, tt * 128:(tt + 1) * 128],
                       tp.rearrange("p (a t) -> p a t", a=HPG),
                       inc=((s_dv, 1),))
                    m_attcopy[tt] = s_dv.n

                emit_gate_dve(0, emit_corr(0))
                for tt in range(1, NT):
                    emit_gate_dve(tt, emit_corr(tt))
                    emit_attnT(tt - 1)
                emit_attnT(NT - 1)
                m_gate[hg] = m_attcopy[NT - 1]
                m_gatepsw0 = m_cfree[NT - 1]

            if lvl < 7:
                raise _Stop


            wo_h = [view(OFF_WO, NK * 1024, BF16), view(OFF_XN, NK * 1024, BF16)]
            xld = [view(OFF_P3X0, 1024, F32), view(OFF_P3X1, 1024, F32)]
            ost = [view(OFF_P3O0, 1024, F32), view(OFF_P3O1, 1024, F32)]

            def wohalf_ap(hf):
                return d_w["wo"][:, hf * 1024:(hf + 1) * 1024].rearrange(
                    "(k p) j -> p k j", p=128)

            wt("gpsimd", (s_pe, m_gqmm[NHG - 1]))
            do("gpsimd", "dma_start", out=t_bias4[0:1, :],
               in_=d_b["bo"][0:1, 0:1024], inc=((s_dw, 16),))
            m_bo = [s_dw.n]
            wt("gpsimd", (s_cc, s_cc.n))
            do("gpsimd", "dma_start",
               out=wo_h[0].rearrange("p (k j) -> p k j", k=NK), in_=wohalf_ap(0),
               inc=((s_dw, 16),))
            m_wo = [s_dw.n]
            do("gpsimd", "dma_start",
               out=wo_h[1].rearrange("p (k j) -> p k j", k=NK), in_=wohalf_ap(1),
               inc=((s_dw, 16),))
            m_wo.append(s_dw.n)

            m_comb = {}
            m_p3mm = {}
            for hf in range(2):
                wo_half = wo_h[hf]
                wt("tensor", (s_dw, m_wo[hf]))
                if hf == 0:
                    wt("tensor", (s_dv, m_gate[NHG - 1]))
                for tt in range(NT):
                    ps = psw0 if tt % 2 == 0 else psw1
                    pshalf = ps[:, 0:1024]
                    prev = None
                    if tt >= 2:
                        prev = m_comb[(hf, tt - 2)]
                    elif hf == 1:
                        prev = m_comb[(0, NT - 2 + tt)]
                    if prev is not None:
                        wt("tensor", (s_dv, prev))
                    for k in range(NK):
                        lhsT = attn_fm[:, k * 1024 + tt * 128:k * 1024 + (tt + 1) * 128]
                        for jc in range(2):
                            do("tensor", "matmul",
                               pshalf[:, jc * 512:(jc + 1) * 512], lhsT,
                               wo_half[:, k * 1024 + jc * 512:k * 1024 + (jc + 1) * 512],
                               start=(k == 0), stop=False,
                               skip_group_check=True)
                    if tt == 0:
                        wt("tensor", (s_dw, m_bo[hf]))
                    for jc in range(2):
                        do("tensor", "matmul", pshalf[:, jc * 512:(jc + 1) * 512],
                           t_ones[0:1, :],
                           t_bias4[0:1, jc * 512:(jc + 1) * 512],
                           start=False, stop=True, skip_group_check=True)
                    inc_last("tensor", (s_pe, 1))
                    m_smm = s_pe.n
                    m_p3mm[(hf, tt)] = m_smm
                    if prev is not None:
                        wt("sync", (s_dv, prev))
                    elif hf == 0 and tt == 0:
                        wt("sync", (s_dv, m_gate[NHG - 1]))
                    do("sync", "dma_start", out=xld[tt % 2],
                       in_=d_x[tt * 128:(tt + 1) * 128, hf * 1024:(hf + 1) * 1024],
                       inc=((s_dx, 16),))
                    m_xr = s_dx.n
                    wt("vector", (s_pe, m_smm), (s_dx, m_xr))
                    if hf == 0 and tt == 0:
                        wt("vector", (s_ac, m_gffn))
                    do("vector", "tensor_tensor", ost[tt % 2], pshalf,
                       gffn[:, tt * 2048 + hf * 1024:tt * 2048 + (hf + 1) * 1024],
                       ALU.mult)
                    do("vector", "tensor_tensor", ost[tt % 2], ost[tt % 2],
                       xld[tt % 2], ALU.add, inc=((s_dv, 1),))
                    m_comb[(hf, tt)] = s_dv.n
                    wt("sync", (s_dv, m_comb[(hf, tt)]))
                    do("sync", "dma_start",
                       out=d_y[tt * 128:(tt + 1) * 128, hf * 1024:(hf + 1) * 1024],
                       in_=ost[tt % 2], inc=((s_out, 16),))
                if hf == 0:
                    wt("gpsimd", (s_pe, m_p3mm[(0, NT - 1)]))
                    do("gpsimd", "dma_start", out=t_bias4[0:1, :],
                       in_=d_b["bo"][0:1, 1024:2048], inc=((s_dw, 16),))
                    m_bo.append(s_dw.n)


        except _Stop:
            pass

        dbg_srcs = {
            "xn_fm": xn_fm, "gffn": gffn, "ga": ga, "k_tok": k_tok2[0],
            "v_tok": v_tok2[0], "g_tok": g_tok, "q_fm": q_fm, "num_sb": num_sb,
            "den": t_den[:, :], "attn_fm": attn_fm, "szh": t_szh[:, :],
            "stats": t_stats[:, :], "xns1": xns[1], "xb1": xb[1],
        }
        if debug:
            dchunk = view(OFF_BIG, 2048, F32)
            for (nm, shape) in debug:
                src_ap = dbg_srcs[nm]
                n = src_ap.shape[1]
                for c0 in range(0, n, 2048):
                    c1 = min(c0 + 2048, n)
                    wt("vector", (s_dv, s_dv.n), (s_ac, s_ac.n),
                       (s_pe, s_pe.n), (s_dx, s_dx.n), (s_dw, s_dw.n))
                    do("vector", "tensor_copy", dchunk[:, 0:c1 - c0],
                       src_ap[:, c0:c1], inc=((s_dv, 1),))
                    wt("gpsimd", (s_dv, s_dv.n))
                    do("gpsimd", "dma_start", out=d_dbg[nm][:, c0:c1],
                       in_=dchunk[:, 0:c1 - c0], inc=((s_dw, 16),))

        wt("sync", (s_out, s_out.n))
        wt("gpsimd", (s_dw, s_dw.n))

        with nc.Block() as block:
            @block.tensor
            def _(t):
                for f in P["tensor"]:
                    f(t)

            @block.vector
            def _(v):
                for f in P["vector"]:
                    f(v)

            @block.scalar
            def _(sc):
                for f in P["scalar"]:
                    f(sc)

            @block.gpsimd
            def _(g):
                for f in P["gpsimd"]:
                    f(g)

            @block.sync
            def _(sy):
                for f in P["sync"]:
                    f(sy)

    return nc


def _perm_deint():
    p = np.zeros(D, np.int64)
    for h in range(H):
        b = h * DK
        for i in range(DK // 2):
            p[b + i] = b + 2 * i
            p[b + 64 + i] = b + 2 * i + 1
    return p


def _rope_tables():
    inv = (1.0 / 10000.0 ** (np.arange(0, DK, 2, dtype=np.float32)
                             / np.float32(DK))).astype(np.float32)
    pos = np.arange(2048, dtype=np.float32)
    ang = pos[:, None] * inv[None, :]
    return np.sin(ang).astype(np.float32), np.cos(ang).astype(np.float32)


_DEBUG = ()


def kernel(L0_attn_x, L0_attn_Wq, L0_attn_bq, L0_attn_Wk, L0_attn_bk,
           L0_attn_Wv, L0_attn_bv, L0_attn_Wg, L0_attn_bg, L0_attn_Wo,
           L0_attn_bo, L0_ffn_up_weight, L0_ffn_down_weight, L0_ffn_up_scale,
           L0_ffn_down_scale, L0_ffn_output_scale):
    x = np.ascontiguousarray(
        np.asarray(L0_attn_x, np.float32).reshape(4 * 2048, D))
    p = _perm_deint()
    wq = np.ascontiguousarray(np.asarray(L0_attn_Wq, np.float32).T[:, p])
    wk = np.ascontiguousarray(np.asarray(L0_attn_Wk, np.float32).T[:, p])
    wv = np.ascontiguousarray(np.asarray(L0_attn_Wv, np.float32).T)
    wg = np.ascontiguousarray(np.asarray(L0_attn_Wg, np.float32).T)
    wo = np.ascontiguousarray(np.asarray(L0_attn_Wo, np.float32).T)
    upw = np.ascontiguousarray(np.asarray(L0_ffn_up_weight, np.float32).T)
    dnw = np.ascontiguousarray(np.asarray(L0_ffn_down_weight, np.float32).T)
    bq = np.ascontiguousarray(np.asarray(L0_attn_bq, np.float32)[p][None, :])
    bk = np.ascontiguousarray(np.asarray(L0_attn_bk, np.float32)[p][None, :])
    bv = np.ascontiguousarray(np.asarray(L0_attn_bv, np.float32)[None, :])
    bg = np.ascontiguousarray(np.asarray(L0_attn_bg, np.float32)[None, :])
    bo = np.ascontiguousarray(np.asarray(L0_attn_bo, np.float32)[None, :])
    usc = np.ascontiguousarray(
        np.asarray(L0_ffn_up_scale, np.float32).reshape(64, 128).T)
    dsc = np.ascontiguousarray(
        np.asarray(L0_ffn_down_scale, np.float32)[None, :])
    osc = np.ascontiguousarray(
        np.asarray(L0_ffn_output_scale, np.float32).reshape(1, 1))
    sin_f, cos_f = _rope_tables()
    ident = np.eye(128, dtype=np.float32)

    nc = build_nc(debug=_DEBUG)

    in_maps = []
    for c in range(NCORE):
        half = c % 2
        t0 = half * 1024
        mask = np.zeros((128, 8), np.float32)
        if half == 1:
            mask[:, c - 1] = 1.0
        in_maps.append({
            "x": np.ascontiguousarray(x[c * TPC:(c + 1) * TPC]),
            "wq": wq, "wk": wk, "wv": wv, "wg": wg, "wo": wo,
            "upw": upw, "dnw": dnw,
            "bq": bq, "bk": bk, "bv": bv, "bg": bg, "bo": bo,
            "usc": usc, "dsc": dsc, "osc": osc,
            "rsin": np.ascontiguousarray(sin_f[t0:t0 + 1024]),
            "rcos": np.ascontiguousarray(cos_f[t0:t0 + 1024]),
            "ident": ident,
            "mask8": mask,
        })

    res = run_bass_kernel_spmd(nc, in_maps, core_ids=list(range(NCORE)))
    out = np.concatenate([res.results[c]["y"] for c in range(NCORE)], axis=0)
    if _DEBUG:
        kernel._dbg = [{nm: res.results[c]["dbg_" + nm] for (nm, _s) in _DEBUG}
                       for c in range(NCORE)]
    return out.reshape(4, 2048, D)


# version 16 (final)
# speedup vs baseline: 1.3630x; 1.0056x over previous
import numpy as np
from contextlib import ExitStack

import concourse.bass as bass
import concourse.mybir as mybir
import concourse.bass_utils as _bu
from concourse.bass_utils import run_bass_kernel_spmd

if not getattr(_bu, "_ldw_patched", False):
    _orig_run_command = _bu.run_command

    def _run_command_ldw(cmd, *a, **kw):
        if isinstance(cmd, list):
            cmd = ["--enable-ldw-opt=true" if c == "--enable-ldw-opt=false"
                   else c for c in cmd]
        return _orig_run_command(cmd, *a, **kw)

    _bu.run_command = _run_command_ldw
    _bu._ldw_patched = True

F32 = mybir.dt.float32
BF16 = mybir.dt.bfloat16
U8 = mybir.dt.uint8
AF = mybir.ActivationFunctionType
ALU = mybir.AluOpType
AX = mybir.AxisListType

D = 2048
H = 16
DK = 128
CS = 64
TPC = 1024
NT = 8
NK = 16
NCH = 16
NSLAB = 8
DH = 8192
NHG = 4
HPG = 4
NCORE = 8

OFF_XN = 0
OFF_GA = 32768
OFF_ATTN = 32768
OFF_K = (65536, 65536 + 8192)
OFF_V = (65536 + 16384, 65536 + 24576)
OFF_G = 65536 + 32768
OFF_Q = 65536 + 40960
OFF_BIG = 65536 + 49152
BIG_SZ = 82304
ARENA_BYTES = OFF_BIG + BIG_SZ

OFF_XB0 = OFF_BIG + 32768
OFF_XB1 = OFF_BIG + 40960
OFF_X2 = OFF_BIG + 49152
OFF_XNS0 = OFF_BIG + 57344
OFF_XNS1 = OFF_BIG + 61440
OFF_UPS = OFF_BIG
OFF_DNS = OFF_BIG + 32768
OFF_HID = OFF_BIG + 65536
OFF_DSREP = OFF_BIG + 65536
OFF_DSROW = OFF_BIG + 73856
OFF_WSA = OFF_BIG
OFF_WSB = OFF_BIG + 16384
OFF_GFFN = OFF_BIG + 32768
OFF_QTMP = OFF_BIG + 65536
OFF_NUMSB = OFF_BIG + 65536
OFF_RT = OFF_BIG + 73856
OFF_GATH = OFF_BIG + 73856
OFF_WO = OFF_BIG
OFF_P3X0 = OFF_BIG + 65536
OFF_P3X1 = OFF_BIG + 69632
OFF_P3O0 = OFF_BIG + 73856
OFF_P3O1 = OFF_BIG + 77952

ENGS = ("tensor", "vector", "scalar", "gpsimd", "sync")


class _Stop(Exception):
    pass


class Sem:
    __slots__ = ("h", "n")

    def __init__(self, h):
        self.h = h
        self.n = 0


def build_nc(debug=(), upto="full"):
    order = ("p0", "p1", "p2proj", "p2fm", "p2scan", "p2cc", "p2", "full")
    lvl = order.index(upto)
    nc = bass.Bass("TRN2", target_bir_lowering=False,
                   detect_race_conditions=False)

    d_x = nc.dram_tensor("x", [TPC, D], F32, kind="ExternalInput").ap()
    d_w = {nm: nc.dram_tensor(nm, [D, D], F32, kind="ExternalInput").ap()
           for nm in ("wq", "wk", "wv", "wg", "wo")}
    d_up = nc.dram_tensor("upw", [D, DH], F32, kind="ExternalInput").ap()
    d_dn = nc.dram_tensor("dnw", [DH, D], F32, kind="ExternalInput").ap()
    d_b = {nm: nc.dram_tensor(nm, [1, D], F32, kind="ExternalInput").ap()
           for nm in ("bq", "bk", "bv", "bg", "bo")}
    d_usc = nc.dram_tensor("usc", [128, 64], F32, kind="ExternalInput").ap()
    d_dsc = nc.dram_tensor("dsc", [1, D], F32, kind="ExternalInput").ap()
    d_osc = nc.dram_tensor("osc", [1, 1], F32, kind="ExternalInput").ap()
    d_sin = nc.dram_tensor("rsin", [TPC, 64], F32, kind="ExternalInput").ap()
    d_cos = nc.dram_tensor("rcos", [TPC, 64], F32, kind="ExternalInput").ap()
    d_id = nc.dram_tensor("ident", [128, 128], F32, kind="ExternalInput").ap()
    d_msk = nc.dram_tensor("mask8", [128, 8], F32, kind="ExternalInput").ap()
    d_y = nc.dram_tensor("y", [TPC, D], F32, kind="ExternalOutput").ap()
    if lvl >= 5:
        d_ccs = [nc.dram_tensor(f"cc_src{h}", [128, 516], BF16)
                 for h in range(NHG)]
        d_ccd = [nc.dram_tensor(f"cc_dst{h}", [128 * NCORE, 516], BF16,
                                addr_space="Shared") for h in range(NHG)]
        d_wus = nc.dram_tensor("cc_wsrc", [1, 16], BF16)
        d_wud = nc.dram_tensor("cc_wdst", [NCORE, 16], BF16,
                               addr_space="Shared")
    d_dbg = {nm: nc.dram_tensor("dbg_" + nm, list(shape), F32,
                                kind="ExternalOutput").ap()
             for (nm, shape) in debug}

    P = {e: [] for e in ENGS}

    def do(eng, method, *a, inc=(), **kw):
        if method == "dma_start":
            for s, k in inc:
                if k == 16 and s.n:
                    wt(eng, (s, s.n))
        incs = [(s.h, k) for (s, k) in inc]

        def f(E, m=method, a=a, kw=kw, incs=incs):
            ins = getattr(E, m)(*a, **kw)
            for hh, kk in incs:
                ins = ins.then_inc(hh, kk)
        f._incs = incs
        P[eng].append(f)
        for s, k in inc:
            s.n += k

    def inc_last(eng, *pairs):
        f = P[eng][-1]
        for s, k in pairs:
            for idx, (hh, kk) in enumerate(f._incs):
                if hh is s.h:
                    f._incs[idx] = (hh, kk + k)
                    break
            else:
                f._incs.append((s.h, k))
            s.n += k

    def wt(eng, *pairs):
        ps = tuple((s.h, v) for (s, v) in pairs)

        def f(E, ps=ps):
            for hh, vv in ps:
                E.wait_ge(hh, vv)
        P[eng].append(f)

    with ExitStack() as stk:
        arena = stk.enter_context(nc.sbuf_tensor("arena", [128, ARENA_BYTES], U8))
        t_sin = stk.enter_context(nc.sbuf_tensor("t_sin", [128, NT * 64], F32))
        t_cos = stk.enter_context(nc.sbuf_tensor("t_cos", [128, NT * 64], F32))
        t_usc = stk.enter_context(nc.sbuf_tensor("t_usc", [128, 64], F32))
        t_id = stk.enter_context(nc.sbuf_tensor("t_id", [128, 128], BF16))
        t_ones = stk.enter_context(nc.sbuf_tensor("t_ones", [1, 128], BF16))
        t_ones32 = stk.enter_context(nc.sbuf_tensor("t_ones32", [1, 128], F32))
        t_onescol = stk.enter_context(nc.sbuf_tensor("t_onescol", [128, 1], BF16))
        t_bias4 = stk.enter_context(nc.sbuf_tensor("t_bias4", [1, 1024], BF16))
        t_cst = stk.enter_context(nc.sbuf_tensor("t_cst", [128, 6], F32))
        t_stats = stk.enter_context(nc.sbuf_tensor("t_stats", [128, 8], F32))
        t_msk = stk.enter_context(nc.sbuf_tensor("t_msk", [128, 8], F32))
        t_den = stk.enter_context(nc.sbuf_tensor("t_den", [128, NT * HPG], F32))
        t_dtmp = stk.enter_context(nc.sbuf_tensor("t_dtmp", [128, HPG], F32))
        t_rden = stk.enter_context(nc.sbuf_tensor("t_rden", [128, HPG], F32))
        t_szh = stk.enter_context(nc.sbuf_tensor("t_szh", [128, 516], BF16))
        t_sfin = stk.enter_context(nc.sbuf_tensor("t_sfin", [128, 516], F32))
        t_atok = stk.enter_context(nc.sbuf_tensor("t_atok", [128, 512], BF16))
        t_c129 = stk.enter_context(nc.sbuf_tensor("t_c129", [128, 520], F32))
        t_os = stk.enter_context(nc.sbuf_tensor("t_os", [1, 1], F32))
        t_os128 = stk.enter_context(nc.sbuf_tensor("t_os128", [1, 128], F32))
        t_eps = stk.enter_context(nc.sbuf_tensor("t_eps", [1, 2], BF16))

        psw0 = stk.enter_context(nc.psum_tensor("psw0", [128, 2048], F32))
        psw1 = stk.enter_context(nc.psum_tensor("psw1", [128, 2048], F32))

        s_dx = Sem(stk.enter_context(nc.semaphore("s_dx")))
        s_dw = Sem(stk.enter_context(nc.semaphore("s_dw")))
        s_out = Sem(stk.enter_context(nc.semaphore("s_out")))
        s_pe = Sem(stk.enter_context(nc.semaphore("s_pe")))
        s_dv = Sem(stk.enter_context(nc.semaphore("s_dv")))
        s_ac = Sem(stk.enter_context(nc.semaphore("s_ac")))
        s_gp = Sem(stk.enter_context(nc.semaphore("s_gp")))
        s_cc = Sem(stk.enter_context(nc.semaphore("s_cc")))

        def view(off, nelem, dt):
            sz = 4 if dt == F32 else 2
            return arena[:, off:off + nelem * sz].bitcast(dt)

        xn_fm = view(OFF_XN, NK * 1024, BF16)
        ga = view(OFF_GA, NT * 2048, F32)
        attn_fm = view(OFF_ATTN, NK * 1024, BF16)
        k_tok2 = [view(OFF_K[0], NT * 512, BF16), view(OFF_K[1], NT * 512, BF16)]
        v_tok2 = [view(OFF_V[0], NT * 512, BF16), view(OFF_V[1], NT * 512, BF16)]
        g_tok = view(OFF_G, NT * 512, BF16)
        q_fm = view(OFF_Q, HPG * 1024, BF16)
        gffn = view(OFF_GFFN, NT * 2048, BF16)
        up_sgn = view(OFF_UPS, NK * 1024, BF16)
        dn_sgn = view(OFF_DNS, 8 * 2048, BF16)
        hid = view(OFF_HID, 8 * 1024, BF16)
        wsA = view(OFF_WSA, NK * 512, BF16)
        wsB = view(OFF_WSB, NK * 512, BF16)
        qtmp = view(OFF_QTMP, NT * 512, BF16)
        ropeT = [view(OFF_RT + i * 2048, 1024, BF16) for i in range(4)]
        num_sb = view(OFF_NUMSB, NT * 516, BF16)
        gath = view(OFF_GATH, 8 * 516, BF16)
        szh2 = t_c129[:, 0:258].bitcast(BF16)

        try:
            do("gpsimd", "memset", t_ones[:, :], 1.0)
            do("gpsimd", "memset", t_ones32[:, :], 1.0)
            do("gpsimd", "memset", t_onescol[:, :], 1.0)
            do("gpsimd", "memset", t_cst[:, 0:1], 1e-5)
            do("gpsimd", "memset", t_cst[:, 1:2], -0.5)
            do("gpsimd", "memset", t_cst[:, 2:3], 1.0 / D)
            do("gpsimd", "memset", t_cst[:, 3:4], -0.34657359027997264)
            do("gpsimd", "memset", t_cst[:, 4:5], 0.015625)
            do("gpsimd", "memset", t_eps[:, :], 1e-6, inc=((s_gp, 1),))
            m_pre = s_gp.n
            do("gpsimd", "dma_start",
               out=t_sin[:, :].rearrange("p (t f) -> p t f", t=NT),
               in_=d_sin.rearrange("(t p) f -> p t f", p=128), inc=((s_dw, 16),))
            do("gpsimd", "dma_start",
               out=t_cos[:, :].rearrange("p (t f) -> p t f", t=NT),
               in_=d_cos.rearrange("(t p) f -> p t f", p=128), inc=((s_dw, 16),))
            do("gpsimd", "dma_start", out=t_usc[:, :], in_=d_usc, inc=((s_dw, 16),))
            do("gpsimd", "dma_start", out=t_id[:, :], in_=d_id, inc=((s_dw, 16),))
            do("gpsimd", "dma_start", out=t_msk[:, :], in_=d_msk, inc=((s_dw, 16),))
            do("gpsimd", "dma_start", out=t_os[:, :], in_=d_osc, inc=((s_dw, 16),))
            m_tables = s_dw.n
            if lvl >= 5:
                do("gpsimd", "dma_start", out=d_wus[:, :],
                   in_=t_ones[0:1, 0:16], inc=((s_dw, 16),))
                wt("gpsimd", (s_dw, s_dw.n))
                do("gpsimd", "collective_compute", "AllGather", ALU.bypass,
                   replica_groups=[list(range(NCORE))],
                   ins=[d_wus[:, :]], outs=[d_wud[:, :]], inc=((s_cc, 1),))

            def upslab_ap(s):
                return d_up[:, s * 1024:(s + 1) * 1024].rearrange(
                    "(k p) h -> p k h", p=128)

            def dnslab_ap(s):
                return d_dn[s * 1024:(s + 1) * 1024, :].rearrange(
                    "(t p) j -> p t j", p=128)

            do("gpsimd", "dma_start",
               out=up_sgn.rearrange("p (k h) -> p k h", k=NK), in_=upslab_ap(0),
               inc=((s_dw, 16),))
            m_updma = [s_dw.n]

            xb = [view(OFF_XB0, 2048, F32), view(OFF_XB1, 2048, F32)]
            x2 = view(OFF_GA, 2049, F32)
            xns = [view(OFF_XNS0, 2048, BF16), view(OFF_XNS1, 2048, BF16)]
            do("gpsimd", "memset", x2[:, 2048:2049], 5e-6, inc=((s_gp, 1),))
            m_x2e = s_gp.n
            m_ts = [0] * NT
            m_tp4 = [0] * NT
            m_tiny = [0] * NT
            m_tp_ev = [0, 0]

            def emit_ln_tail(j):
                st = t_stats[:, 4 * (j % 2):4 * (j % 2) + 4]
                wts = [(s_ac, m_tiny[j])]
                if j >= 2:
                    wts.append((s_pe, m_tp4[j - 2]))
                wt("vector", *wts)
                do("vector", "tensor_scalar", xns[j % 2], xb[j % 2],
                   st[:, 1:2], st[:, 3:4], ALU.mult, ALU.subtract,
                   inc=((s_dv, 1),))
                m_ts[j] = s_dv.n
                wt("tensor", (s_dv, m_ts[j]), (s_dw, m_tables))
                for kq in range(4):
                    pbank = psw0[:, (kq % 2) * 512:(kq % 2) * 512 + 256].bitcast(BF16)
                    if m_tp_ev[kq % 2]:
                        wt("tensor", (s_dv, m_tp_ev[kq % 2]))
                    for kk in range(4):
                        k = kq * 4 + kk
                        do("tensor", "transpose", pbank[:, kk * 128:(kk + 1) * 128],
                           xns[j % 2][:, k * 128:(k + 1) * 128], t_id[:, :])
                    inc_last("tensor", (s_pe, 1))
                    wt("vector", (s_pe, s_pe.n))
                    do("vector", "tensor_copy",
                       xn_fm.rearrange("p (k t) -> p k t", k=NK)[
                           :, kq * 4:(kq + 1) * 4, j * 128:(j + 1) * 128],
                       pbank.rearrange("p (k t) -> p k t", k=4), inc=((s_dv, 1),))
                    m_tp_ev[kq % 2] = s_dv.n
                m_tp4[j] = s_pe.n

            for j in range(NT):
                if j >= 2:
                    wt("sync", (s_dv, m_ts[j - 2]))
                do("sync", "dma_start", out=xb[j % 2],
                   in_=d_x[j * 128:(j + 1) * 128, :], inc=((s_dx, 16),))
                m_xdma = s_dx.n
                st = t_stats[:, 4 * (j % 2):4 * (j % 2) + 4]
                if j == 0:
                    wt("scalar", (s_dx, m_xdma), (s_gp, m_pre))
                else:
                    wt("scalar", (s_dx, m_xdma))
                do("scalar", "activation", out=x2[:, 0:2048], in_=xb[j % 2],
                   func=AF.Square, scale=t_cst[:, 4:5], inc=((s_ac, 1),))
                m_sq = s_ac.n
                wt("vector", (s_dx, m_xdma))
                do("vector", "reduce_sum", st[:, 3:4], xb[j % 2], axis=AX.X,
                   inc=((s_dv, 1),))
                if j == 0:
                    wt("vector", (s_ac, m_sq), (s_gp, m_x2e))
                else:
                    wt("vector", (s_ac, m_sq))
                do("vector", "reduce_sum", st[:, 1:2], x2, axis=AX.X,
                   inc=((s_dv, 1),))
                m_r2 = s_dv.n
                wt("scalar", (s_dv, m_r2))
                for _ in range(2):
                    do("scalar", "activation", out=st[:, 2:3], in_=st[:, 3:4],
                       func=AF.Copy, scale=t_cst[:, 2:3])
                for _ in range(2):
                    do("scalar", "activation", out=st[:, 3:4], in_=st[:, 2:3],
                       func=AF.Square)
                for _ in range(2):
                    do("scalar", "activation", out=st[:, 0:1], in_=st[:, 3:4],
                       func=AF.Ln, scale=t_cst[:, 1:2], bias=st[:, 1:2])
                for _ in range(2):
                    do("scalar", "activation", out=st[:, 1:2], in_=st[:, 0:1],
                       func=AF.Exp, scale=t_cst[:, 1:2], bias=t_cst[:, 3:4])
                do("scalar", "activation", out=st[:, 3:4], in_=st[:, 2:3],
                   func=AF.Copy, scale=st[:, 1:2], inc=((s_ac, 1),))
                m_tiny[j] = s_ac.n
                if j > 0:
                    emit_ln_tail(j - 1)
            emit_ln_tail(NT - 1)
            m_xn_all = s_dv.n

            wt("gpsimd", (s_dv, m_xn_all))
            do("gpsimd", "dma_start",
               out=dn_sgn.rearrange("p (t j) -> p t j", t=8), in_=dnslab_ap(0),
               inc=((s_dw, 16),))
            m_dndma = [s_dw.n]

            if lvl < 1:
                raise _Stop

            m_upmm = [0] * NSLAB
            m_dnmm = [0] * NSLAB
            m_gadd = {}

            for s in range(NSLAB):
                wt("scalar", (s_dw, m_updma[s]))
                do("scalar", "activation", out=up_sgn, in_=up_sgn, func=AF.Sign,
                   inc=((s_ac, 1),))
                m_upsgn = s_ac.n
                wt("scalar", (s_dw, m_dndma[s]))
                do("scalar", "activation", out=dn_sgn, in_=dn_sgn, func=AF.Sign,
                   inc=((s_ac, 1),))
                m_dnsgn = s_ac.n

                if s == 0:
                    wt("tensor", (s_ac, m_upsgn), (s_dv, m_xn_all))
                else:
                    wt("tensor", (s_ac, m_upsgn), (s_dv, m_gadd[(s - 1, NT - 1)]))
                ev_ms = {}
                for ht in range(8):
                    pA = psw0[:, (ht % 2) * 1024:(ht % 2) * 1024 + 512]
                    pB = psw0[:, (ht % 2) * 1024 + 512:(ht % 2) * 1024 + 1024]
                    if ht >= 2:
                        wt("tensor", (s_dv, ev_ms[ht - 2]))
                    for k in range(NK):
                        lhsT = up_sgn[:, k * 1024 + ht * 128:k * 1024 + (ht + 1) * 128]
                        do("tensor", "matmul", pA, lhsT,
                           xn_fm[:, k * 1024:k * 1024 + 512],
                           start=(k == 0), stop=(k == NK - 1), skip_group_check=True)
                        do("tensor", "matmul", pB, lhsT,
                           xn_fm[:, k * 1024 + 512:k * 1024 + 1024],
                           start=(k == 0), stop=(k == NK - 1), skip_group_check=True)
                    inc_last("tensor", (s_pe, 1))
                    wt("vector", (s_pe, s_pe.n))
                    do("vector", "tensor_scalar", hid[:, ht * 1024:ht * 1024 + 512],
                       pA, t_usc[:, s * 8 + ht:s * 8 + ht + 1], 0.0, ALU.mult, ALU.max)
                    do("vector", "tensor_scalar",
                       hid[:, ht * 1024 + 512:(ht + 1) * 1024],
                       pB, t_usc[:, s * 8 + ht:s * 8 + ht + 1], 0.0, ALU.mult, ALU.max,
                       inc=((s_dv, 1),))
                    ev_ms[ht] = s_dv.n
                inc_last("tensor", (s_pe, 1))
                m_upmm[s] = s_pe.n
                m_hid_done = ev_ms[7]

                if s + 1 < NSLAB:
                    wt("gpsimd", (s_pe, m_upmm[s]))
                    do("gpsimd", "dma_start",
                       out=up_sgn.rearrange("p (k h) -> p k h", k=NK),
                       in_=upslab_ap(s + 1), inc=((s_dw, 16),))
                    m_updma.append(s_dw.n)

                wt("tensor", (s_dv, m_hid_done), (s_ac, m_dnsgn))
                for tt in range(NT):
                    ps = psw0 if tt % 2 == 0 else psw1
                    if tt >= 2:
                        wt("tensor", (s_dv, m_gadd[(s, tt - 2)]))
                    for ht in range(8):
                        lhsT = hid[:, ht * 1024 + tt * 128:ht * 1024 + (tt + 1) * 128]
                        for jc in range(4):
                            do("tensor", "matmul", ps[:, jc * 512:(jc + 1) * 512],
                               lhsT,
                               dn_sgn[:, ht * 2048 + jc * 512:ht * 2048 + (jc + 1) * 512],
                               start=(ht == 0), stop=(ht == 7), skip_group_check=True)
                    inc_last("tensor", (s_pe, 1))
                    wt("vector", (s_pe, s_pe.n))
                    if s == 0:
                        do("vector", "tensor_copy", ga[:, tt * 2048:(tt + 1) * 2048],
                           ps[:, :], inc=((s_dv, 1),))
                    else:
                        do("vector", "tensor_tensor", ga[:, tt * 2048:(tt + 1) * 2048],
                           ga[:, tt * 2048:(tt + 1) * 2048], ps[:, :], ALU.add,
                           inc=((s_dv, 1),))
                    m_gadd[(s, tt)] = s_dv.n
                inc_last("tensor", (s_pe, 1))
                m_dnmm[s] = s_pe.n

                if s + 1 < NSLAB:
                    wt("gpsimd", (s_pe, m_dnmm[s]))
                    do("gpsimd", "dma_start",
                       out=dn_sgn.rearrange("p (t j) -> p t j", t=8),
                       in_=dnslab_ap(s + 1), inc=((s_dw, 16),))
                    m_dndma.append(s_dw.n)

            ds_rep = view(OFF_DSREP, 2048, F32)
            ds_tmp = view(OFF_DSROW, 2048, F32)
            wt("sync", (s_pe, m_dnmm[NSLAB - 1]))
            do("sync", "dma_start", out=ds_tmp[0:1, :], in_=d_dsc, inc=((s_dx, 16),))
            m_dsrow = s_dx.n
            wt("vector", (s_dw, m_tables))
            do("vector", "tensor_scalar", t_os128[0:1, :], t_ones32[0:1, :],
               t_os[0:1, 0:1], None, ALU.mult, inc=((s_dv, 1),))
            m_os128 = s_dv.n
            wt("tensor", (s_dv, m_os128), (s_dx, m_dsrow))
            for jc in range(4):
                do("tensor", "matmul", psw1[:, jc * 512:(jc + 1) * 512],
                   t_os128[0:1, :], ds_tmp[0:1, jc * 512:(jc + 1) * 512],
                   start=True, stop=True, skip_group_check=True)
            inc_last("tensor", (s_pe, 1))
            wt("vector", (s_pe, s_pe.n), (s_dv, m_gadd[(NSLAB - 1, NT - 1)]))
            do("vector", "tensor_copy", ds_rep, psw1[:, :], inc=((s_dv, 1),))
            dst_b = [ds_tmp, view(OFF_Q, 2048, F32)]
            m_silu = {}
            for tt in range(NT):
                if tt >= 2:
                    wt("vector", (s_ac, m_silu[tt - 2]))
                do("vector", "tensor_tensor", dst_b[tt % 2],
                   ga[:, tt * 2048:(tt + 1) * 2048],
                   ds_rep, ALU.mult, inc=((s_dv, 1),))
                wt("scalar", (s_dv, s_dv.n))
                do("scalar", "activation", out=gffn[:, tt * 2048:(tt + 1) * 2048],
                   in_=dst_b[tt % 2], func=AF.Silu, inc=((s_ac, 1),))
                m_silu[tt] = s_ac.n
            m_gffn = s_ac.n
            m_ga_last = m_gadd[(NSLAB - 1, NT - 1)]

            if lvl < 2:
                raise _Stop


            sA = psw1[:, 0:512]
            sB = psw1[:, 512:1024]
            tp_bf = psw1[:, 1024:1280].bitcast(BF16)
            tp_bf2 = psw1[:, 1536:1792].bitcast(BF16)

            def wslab_ap(wname, hg):
                return d_w[wname][:, hg * 512:(hg + 1) * 512].rearrange(
                    "(k p) j -> p k j", p=128)

            m_gate = [0] * NHG
            m_kvmm = [0] * (NHG + 1)
            ev_kv = [dict() for _ in range(NHG + 1)]
            m_b4 = [0] * NHG
            m_gqmm = [0] * NHG

            wt("gpsimd", (s_pe, m_upmm[NSLAB - 1]))
            do("gpsimd", "dma_start", out=wsA.rearrange("p (k j) -> p k j", k=NK),
               in_=wslab_ap("wk", 0), inc=((s_dw, 16),))
            do("gpsimd", "dma_start", out=wsB.rearrange("p (k j) -> p k j", k=NK),
               in_=wslab_ap("wv", 0), inc=((s_dw, 16),))
            m_wkv = s_dw.n
            for bi, bn in enumerate(("bk", "bv")):
                do("gpsimd", "dma_start",
                   out=t_bias4[0:1, bi * 512:(bi + 1) * 512],
                   in_=d_b[bn][0:1, 0:512], inc=((s_dw, 16),))
            m_b4[0] = s_dw.n

            def emit_kv_group(hg, tt, m_w):
                ks = k_tok2[hg % 2]
                vs = v_tok2[hg % 2]
                kps = psw0[:, 0:512]
                vps = psw0[:, 512:1024]
                if tt == 0:
                    wt("tensor", (s_dw, m_w), (s_dw, m_b4[hg]),
                       (s_dv, m_ga_last))
                else:
                    wt("tensor", (s_dv, ev_kv[hg][tt - 1]))
                do("tensor", "matmul", kps, t_ones[0:1, :],
                   t_bias4[0:1, 0:512], start=True, stop=False,
                   skip_group_check=True)
                do("tensor", "matmul", vps, t_ones[0:1, :],
                   t_bias4[0:1, 512:1024], start=True, stop=False,
                   skip_group_check=True)
                for k in range(NK):
                    lhsT = xn_fm[:, k * 1024 + tt * 128:k * 1024 + (tt + 1) * 128]
                    do("tensor", "matmul", kps, lhsT,
                       wsA[:, k * 512:(k + 1) * 512],
                       start=False, stop=(k == NK - 1), skip_group_check=True)
                    do("tensor", "matmul", vps, lhsT,
                       wsB[:, k * 512:(k + 1) * 512],
                       start=False, stop=(k == NK - 1), skip_group_check=True)
                inc_last("tensor", (s_pe, 1))
                wt("vector", (s_pe, s_pe.n))
                do("vector", "tensor_copy", ks[:, tt * 512:(tt + 1) * 512], kps)
                do("vector", "tensor_copy", vs[:, tt * 512:(tt + 1) * 512], vps,
                   inc=((s_dv, 1),))
                ev_kv[hg][tt] = s_dv.n

            def emit_q_proj(hg, m_wq, m_bq, m_qdst_free):
                qdst = v_tok2[(hg + 1) % 2]
                ev_prev = [0, 0]
                for tt in range(NT):
                    qps = psw0[:, 1024 + (tt % 2) * 512:1536 + (tt % 2) * 512]
                    if tt == 0:
                        wt("tensor", (s_dw, m_wq), (s_dv, m_qdst_free))
                    elif tt >= 2:
                        wt("tensor", (s_dv, ev_prev[tt % 2]))
                    for k in range(NK):
                        lhsT = xn_fm[:, k * 1024 + tt * 128:k * 1024 + (tt + 1) * 128]
                        do("tensor", "matmul", qps, lhsT,
                           wsB[:, k * 512:(k + 1) * 512],
                           start=(k == 0), stop=False, skip_group_check=True)
                    if tt == 0:
                        wt("tensor", (s_dw, m_bq))
                    do("tensor", "matmul", qps, t_ones[0:1, :],
                       t_bias4[0:1, 512:1024], start=False, stop=True,
                       skip_group_check=True)
                    inc_last("tensor", (s_pe, 1))
                    wt("vector", (s_pe, s_pe.n))
                    do("vector", "tensor_copy", qdst[:, tt * 512:(tt + 1) * 512],
                       qps, inc=((s_dv, 1),))
                    ev_prev[tt % 2] = s_dv.n
                inc_last("tensor", (s_pe, 1))
                return s_pe.n, s_dv.n

            ev_k = [dict() for _ in range(NHG + 1)]
            ev_v = [dict() for _ in range(NHG + 1)]
            m_ktok = [0] * (NHG + 1)

            def emit_k_group(hg, tt, m_w, m_b):
                ks = k_tok2[hg % 2]
                kps = psw0[:, 0:512]
                if tt == 0:
                    wt("tensor", (s_dw, m_w), (s_dw, m_b))
                else:
                    wt("tensor", (s_dv, ev_k[hg][tt - 1]))
                for k in range(NK):
                    lhsT = xn_fm[:, k * 1024 + tt * 128:k * 1024 + (tt + 1) * 128]
                    do("tensor", "matmul", kps, lhsT,
                       wsB[:, k * 512:(k + 1) * 512],
                       start=(k == 0), stop=False, skip_group_check=True)
                do("tensor", "matmul", kps, t_ones[0:1, :],
                   t_bias4[0:1, 0:512], start=False, stop=True,
                   skip_group_check=True)
                inc_last("tensor", (s_pe, 1))
                wt("vector", (s_pe, s_pe.n))
                do("vector", "tensor_copy", ks[:, tt * 512:(tt + 1) * 512], kps,
                   inc=((s_dv, 1),))
                ev_k[hg][tt] = s_dv.n

            def emit_v_group(hg, tt, m_w):
                vs = v_tok2[hg % 2]
                vps = psw0[:, 512:1024]
                if tt == 0:
                    wt("tensor", (s_dw, m_w))
                else:
                    wt("tensor", (s_dv, ev_v[hg][tt - 1]))
                for k in range(NK):
                    lhsT = xn_fm[:, k * 1024 + tt * 128:k * 1024 + (tt + 1) * 128]
                    do("tensor", "matmul", vps, lhsT,
                       wsA[:, k * 512:(k + 1) * 512],
                       start=(k == 0), stop=False, skip_group_check=True)
                do("tensor", "matmul", vps, t_ones[0:1, :],
                   t_bias4[0:1, 512:1024], start=False, stop=True,
                   skip_group_check=True)
                inc_last("tensor", (s_pe, 1))
                wt("vector", (s_pe, s_pe.n))
                do("vector", "tensor_copy", vs[:, tt * 512:(tt + 1) * 512], vps,
                   inc=((s_dv, 1),))
                ev_v[hg][tt] = s_dv.n

            for tt in range(NT):
                emit_kv_group(0, tt, m_wkv)
            inc_last("tensor", (s_pe, 1))
            m_kvmm[0] = s_pe.n
            m_ktok[0] = ev_kv[0][NT - 1]

            wt("gpsimd", (s_pe, m_kvmm[0]))
            do("gpsimd", "dma_start", out=wsB.rearrange("p (k j) -> p k j", k=NK),
               in_=wslab_ap("wq", 0), inc=((s_dw, 16),))
            do("gpsimd", "dma_start",
               out=t_bias4[0:1, 512:1024], in_=d_b["bq"][0:1, 0:512],
               inc=((s_dw, 16),))
            m_wq = s_dw.n
            do("gpsimd", "dma_start", out=wsA.rearrange("p (k j) -> p k j", k=NK),
               in_=wslab_ap("wg", 0), inc=((s_dw, 16),))
            do("gpsimd", "dma_start",
               out=t_bias4[0:1, 0:512], in_=d_b["bg"][0:1, 0:512],
               inc=((s_dw, 16),))
            m_wg = s_dw.n
            m_qmm, m_qev = emit_q_proj(0, m_wq, m_wq, m_ga_last)

            def rope_hb(buf, hb):
                b4 = buf.rearrange("p (t h f) -> p t h f", t=NT, h=HPG)
                ropeT = [view(OFF_RT + i * 2048, 1024, BF16) for i in range(4)]
                x1 = b4[:, hb * 4:(hb + 1) * 4, :, 0:64]
                x2 = b4[:, hb * 4:(hb + 1) * 4, :, 64:128]
                cosb = t_cos[:, :].rearrange("p (t f) -> p t f", t=NT)[
                    :, hb * 4:(hb + 1) * 4, :].unsqueeze(2).broadcast_to(
                    [128, 4, HPG, 64])
                sinb = t_sin[:, :].rearrange("p (t f) -> p t f", t=NT)[
                    :, hb * 4:(hb + 1) * 4, :].unsqueeze(2).broadcast_to(
                    [128, 4, HPG, 64])
                r4 = [r.rearrange("p (t h f) -> p t h f", t=4, h=HPG)
                      for r in ropeT]
                do("vector", "tensor_tensor", r4[0], x1, cosb, ALU.mult)
                do("vector", "tensor_tensor", r4[1], x1, sinb, ALU.mult)
                do("vector", "tensor_tensor", r4[2], x2, sinb, ALU.mult)
                do("vector", "tensor_tensor", r4[3], x2, cosb, ALU.mult)
                do("vector", "tensor_tensor", x1, r4[0], r4[2], ALU.subtract)
                do("vector", "tensor_tensor", x2, r4[1], r4[3], ALU.add,
                   inc=((s_dv, 1),))

            def elu_q(buf, q4):
                ropeT = [view(OFF_RT + i * 2048, 1024, BF16) for i in range(4)]
                tmin = ropeT[0]
                texp = ropeT[2]
                sl = buf[:, q4 * 1024:(q4 + 1) * 1024]
                do("vector", "tensor_single_scalar", tmin, sl, 0.0, ALU.min,
                   inc=((s_dv, 1),))
                wt("scalar", (s_dv, s_dv.n))
                do("scalar", "activation", out=texp, in_=tmin, func=AF.Exp,
                   inc=((s_ac, 1),))
                wt("vector", (s_ac, s_ac.n))
                do("vector", "scalar_tensor_tensor", sl, sl, 0.0, texp,
                   ALU.max, ALU.add, inc=((s_dv, 1),))

            for hg in range(NHG):
                slot = hg % 2
                k_tok = k_tok2[slot]
                v_tok = v_tok2[slot]
                qtok = v_tok2[(hg + 1) % 2]

                if hg + 1 < NHG:
                    wt("gpsimd", (s_pe, m_qmm))
                    do("gpsimd", "dma_start",
                       out=wsB.rearrange("p (k j) -> p k j", k=NK),
                       in_=wslab_ap("wk", hg + 1), inc=((s_dw, 16),))
                    do("gpsimd", "dma_start",
                       out=t_bias4[0:1, 512:1024],
                       in_=d_b["bv"][0:1, (hg + 1) * 512:(hg + 2) * 512],
                       inc=((s_dw, 16),))
                    m_wkb = s_dw.n

                fm_stage = 0
                wt("tensor", (s_dw, m_wg))
                if hg > 0:
                    wt("tensor", (s_dv, m_gate[hg - 1]),
                       (s_ac, m_gatepsw0))
                else:
                    wt("tensor", (s_dv, ev_kv[0][NT - 1]), (s_dv, m_qev))
                ev_g = [0, 0]
                for tt in range(NT):
                    gps = psw0[:, (tt % 4) * 512:(tt % 4) * 512 + 512]
                    if tt >= 4:
                        wt("tensor", (s_ac, ev_g[(tt // 2) % 2]))
                    do("tensor", "matmul", gps, t_ones[0:1, :],
                       t_bias4[0:1, 0:512], start=True, stop=False,
                       skip_group_check=True)
                    for k in range(NK):
                        lhsT = xn_fm[:, k * 1024 + tt * 128:k * 1024 + (tt + 1) * 128]
                        do("tensor", "matmul", gps, lhsT,
                           wsA[:, k * 512:(k + 1) * 512],
                           start=False, stop=(k == NK - 1), skip_group_check=True)
                    inc_last("tensor", (s_pe, 1))
                    if tt % 2 == 1:
                        if tt == 1 and hg > 0:
                            wt("scalar", (s_pe, s_pe.n), (s_dv, m_gate[hg - 1]))
                        else:
                            wt("scalar", (s_pe, s_pe.n))
                        for t2 in (tt - 1, tt):
                            do("scalar", "activation",
                               out=g_tok[:, t2 * 512:(t2 + 1) * 512],
                               in_=psw0[:, (t2 % 4) * 512:(t2 % 4) * 512 + 512],
                               func=AF.Sigmoid,
                               inc=((s_ac, 1),) if t2 == tt else ())
                        ev_g[(tt // 2) % 2] = s_ac.n
                    if fm_stage < 8:
                        if fm_stage == 0:
                            wts = [(s_dv, m_ktok[hg]), (s_dw, m_tables)]
                            if hg == 0:
                                wts.append((s_ac, m_gffn))
                            wt("vector", *wts)
                            rope_hb(k_tok, 0)
                        elif fm_stage == 1:
                            rope_hb(k_tok, 1)
                        elif fm_stage == 2:
                            wt("vector", (s_dv, m_qev))
                            rope_hb(qtok, 0)
                        elif fm_stage == 3:
                            rope_hb(qtok, 1)
                        elif fm_stage == 4:
                            elu_q(k_tok, 0)
                            elu_q(k_tok, 1)
                        elif fm_stage == 5:
                            elu_q(k_tok, 2)
                            elu_q(k_tok, 3)
                        elif fm_stage == 6:
                            elu_q(qtok, 0)
                            elu_q(qtok, 1)
                        else:
                            elu_q(qtok, 2)
                            elu_q(qtok, 3)
                        fm_stage += 1
                inc_last("tensor", (s_pe, 1))
                m_gqmm[hg] = s_pe.n
                m_fmk = 0
                m_fmq = s_dv.n
                m_sig_last = s_ac.n

                if lvl == 2:
                    raise _Stop
                if lvl == 3:
                    raise _Stop

                wt("tensor", (s_dv, m_fmq))
                m_qT_ev = [0, 0]
                for tt in range(NT):
                    tp = tp_bf if tt % 2 == 0 else tp_bf2
                    if m_qT_ev[tt % 2]:
                        wt("tensor", (s_dv, m_qT_ev[tt % 2]))
                    for hh in range(HPG):
                        do("tensor", "transpose",
                           tp[:, hh * 128:(hh + 1) * 128],
                           qtok[:, tt * 512 + hh * 128:tt * 512 + (hh + 1) * 128],
                           t_id[:, :])
                    inc_last("tensor", (s_pe, 1))
                    wt("vector", (s_pe, s_pe.n))
                    do("vector", "tensor_copy",
                       q_fm.rearrange("p (h t) -> p h t", h=HPG)[
                           :, :, tt * 128:(tt + 1) * 128],
                       tp.rearrange("p (h t) -> p h t", h=HPG),
                       inc=((s_dv, 1),))
                    m_qT_ev[tt % 2] = s_dv.n
                m_qT = s_dv.n

                if hg + 1 < NHG:
                    wt("gpsimd", (s_pe, m_gqmm[hg]))
                    do("gpsimd", "dma_start",
                       out=t_bias4[0:1, 0:512],
                       in_=d_b["bk"][0:1, (hg + 1) * 512:(hg + 2) * 512],
                       inc=((s_dw, 16),))
                    m_bk = s_dw.n
                    do("gpsimd", "dma_start",
                       out=wsA.rearrange("p (k j) -> p k j", k=NK),
                       in_=wslab_ap("wv", hg + 1), inc=((s_dw, 16),))
                    m_wv = s_dw.n

                m_snap = {}
                m_numev = {}
                szh_buf = [t_szh, szh2]

                def emit_num(j):
                    jt, jhalf = j // 2, j % 2
                    jpb = jhalf * 64
                    wt("tensor", (s_dv, m_snap[j]))
                    if j == 0:
                        wt("tensor", (s_dv, m_qT))
                    if j == 1:
                        wt("tensor", (s_ac, m_sig_last))
                    if j >= 2:
                        wt("tensor", (s_dv, m_numev[j - 2]))
                    npair = (psw1[:, 1024:2048] if jhalf == 0
                             else psw0[:, 1024:2048])
                    szh = szh_buf[j % 2]
                    for hh in range(HPG):
                        hb, hl = hh // 2, hh % 2
                        do("tensor", "matmul",
                           npair[jpb:jpb + 64,
                                 hb * 512 + hl * 128:hb * 512 + hl * 128 + 128],
                           q_fm[:, hh * 1024 + j * 64:hh * 1024 + (j + 1) * 64],
                           szh[:, hh * 129:hh * 129 + 128],
                           start=(hl == 0), stop=False, skip_group_check=True,
                           tile_position=(0, jpb) if jpb else None)
                    for hh in range(HPG):
                        hb, hl = hh // 2, hh % 2
                        do("tensor", "matmul",
                           npair[jpb:jpb + 64,
                                 hb * 512 + 256 + hl:hb * 512 + 257 + hl],
                           q_fm[:, hh * 1024 + j * 64:hh * 1024 + (j + 1) * 64],
                           szh[:, hh * 129 + 128:hh * 129 + 129],
                           start=False, stop=(hl == 1), skip_group_check=True,
                           tile_position=(0, jpb) if jpb else None)
                    inc_last("tensor", (s_pe, 1))
                    wt("vector", (s_pe, s_pe.n))
                    n4 = npair.rearrange("p (b f) -> p b f", b=2)
                    dst = num_sb[jpb:jpb + 64,
                                 jt * 516:(jt + 1) * 516].rearrange(
                        "p (bh f) -> p bh f", f=129).rearrange(
                        "p (b h) f -> p b h f", b=2)
                    do("vector", "tensor_copy", dst[:, :, :, 0:128],
                       n4[jpb:jpb + 64, :, 0:256].rearrange(
                           "p b (h f) -> p b h f", h=2))
                    do("vector", "tensor_copy", dst[:, :, :, 128:129],
                       n4[jpb:jpb + 64, :, 256:258].rearrange(
                           "p b (h f) -> p b h f", h=2), inc=((s_dv, 1),))
                    m_numev[j] = s_dv.n

                for i in range(NCH):
                    tt, half = i // 2, i % 2
                    pb = half * 64
                    if i > 0:
                        wt("tensor", (s_dv, m_snap[i - 1]))
                    else:
                        wt("tensor", (s_dv, m_fmq))
                    for hh in range(HPG):
                        sbank = sA if hh < 2 else sB
                        hl = hh % 2
                        do("tensor", "matmul",
                           sbank[:, hl * 128:hl * 128 + 128],
                           k_tok[pb:pb + 64,
                                 tt * 512 + hh * 128:tt * 512 + (hh + 1) * 128],
                           v_tok[pb:pb + 64,
                                 tt * 512 + hh * 128:tt * 512 + (hh + 1) * 128],
                           start=(i == 0 and hl == 0), stop=False,
                           skip_group_check=True)
                    for hh in range(HPG):
                        sbank = sA if hh < 2 else sB
                        hl = hh % 2
                        do("tensor", "matmul", sbank[:, 256 + hl:257 + hl],
                           k_tok[pb:pb + 64,
                                 tt * 512 + hh * 128:tt * 512 + (hh + 1) * 128],
                           t_onescol[pb:pb + 64, 0:1],
                           start=False, stop=(i == NCH - 1 and hl == 1),
                           skip_group_check=True)
                    inc_last("tensor", (s_pe, 1))
                    m_mm1 = s_pe.n
                    wt("vector", (s_pe, m_mm1))
                    spair = psw1[:, 0:1024].rearrange("p (b f) -> p b f", b=2)
                    zdst = szh_buf[i % 2][:, 0:516].rearrange(
                        "p (bh f) -> p bh f", f=129).rearrange(
                        "p (b h) f -> p b h f", b=2)
                    do("vector", "tensor_copy", zdst[:, :, :, 0:128],
                       spair[:, :, 0:256].rearrange("p b (h f) -> p b h f", h=2))
                    do("vector", "tensor_copy", zdst[:, :, :, 128:129],
                       spair[:, :, 256:258].rearrange("p b (h f) -> p b h f", h=2),
                       inc=((s_dv, 1),))
                    m_snap[i] = s_dv.n
                    if i == NCH - 1:
                        t_sf16 = gath[:, 0:516]
                        fdst = t_sf16.rearrange(
                            "p (bh f) -> p bh f", f=129).rearrange(
                            "p (b h) f -> p b h f", b=2)
                        do("vector", "tensor_copy", fdst[:, :, :, 0:128],
                           spair[:, :, 0:256].rearrange(
                               "p b (h f) -> p b h f", h=2))
                        do("vector", "tensor_copy", fdst[:, :, :, 128:129],
                           spair[:, :, 256:258].rearrange(
                               "p b (h f) -> p b h f", h=2), inc=((s_dv, 1),))
                        m_sfin = s_dv.n
                        wt("sync", (s_dv, m_sfin))
                        do("sync", "dma_start", out=d_ccs[hg][:, :],
                           in_=t_sf16, inc=((s_dx, 16),))
                        m_stg = s_dx.n
                        wt("gpsimd", (s_dx, m_stg))
                        do("gpsimd", "collective_compute", "AllGather",
                           ALU.bypass, replica_groups=[list(range(NCORE))],
                           ins=[d_ccs[hg][:, :]], outs=[d_ccd[hg][:, :]],
                           inc=((s_cc, 1),))
                    if hg + 1 < NHG and 1 <= i <= 8:
                        emit_k_group(hg + 1, i - 1, m_wkb, m_bk)
                        if i == 8:
                            inc_last("tensor", (s_pe, 1))
                            m_kmm = s_pe.n
                            m_ktok[hg + 1] = ev_k[hg + 1][NT - 1]
                            wt("gpsimd", (s_pe, m_kmm))
                            do("gpsimd", "dma_start",
                               out=wsB.rearrange("p (k j) -> p k j", k=NK),
                               in_=wslab_ap("wq", hg + 1), inc=((s_dw, 16),))
                            do("gpsimd", "dma_start",
                               out=t_bias4[0:1, 0:512],
                               in_=d_b["bg"][0:1,
                                             (hg + 1) * 512:(hg + 2) * 512],
                               inc=((s_dw, 16),))
                            m_wq = s_dw.n
                    if hg + 1 < NHG and 9 <= i <= 15:
                        emit_v_group(hg + 1, i - 9, m_wv)
                    if i > 0:
                        emit_num(i - 1)
                if hg + 1 < NHG:
                    emit_v_group(hg + 1, 7, m_wv)
                    inc_last("tensor", (s_pe, 1))
                    m_vmm = s_pe.n
                emit_num(NCH - 1)
                m_scan = s_pe.n
                if hg + 1 < NHG:
                    wt("gpsimd", (s_pe, m_vmm))
                    do("gpsimd", "dma_start",
                       out=t_bias4[0:1, 512:1024],
                       in_=d_b["bq"][0:1, (hg + 1) * 512:(hg + 2) * 512],
                       inc=((s_dw, 16),))
                    m_bq = s_dw.n

                if lvl == 4:
                    raise _Stop

                if hg + 1 < NHG:
                    m_qmm, m_qev = emit_q_proj(hg + 1, m_wq, m_bq,
                                               m_numev[NCH - 1])
                wt("sync", (s_cc, s_cc.n), (s_dv, m_fmq))
                do("sync", "dma_start",
                   out=gath[:, 0:2064].rearrange("p (s f) -> p s f", s=4),
                   in_=d_ccd[hg].rearrange("(u v p) f -> p u v f", v=2, p=128)[
                       :, :, 0, :],
                   inc=((s_dx, 16),))
                m_gath = s_dx.n
                wt("vector", (s_dx, m_gath), (s_pe, m_scan))
                do("vector", "tensor_tensor",
                   gath[:, 0:2064].rearrange("p (s f) -> p s f", s=4),
                   gath[:, 0:2064].rearrange("p (s f) -> p s f", s=4),
                   t_msk[:, :].rearrange("p (u v) -> p u v", v=2)[
                       :, :, 0].unsqueeze(2).broadcast_to([128, 4, 516]),
                   ALU.mult)
                do("vector", "tensor_tensor", gath[:, 0:1032], gath[:, 0:1032],
                   gath[:, 1032:2064], ALU.add)
                do("vector", "tensor_tensor", t_szh[:, 0:516], gath[:, 0:516],
                   gath[:, 516:1032], ALU.add, inc=((s_dv, 1),))
                m_szh = s_dv.n
                if hg + 1 < NHG:
                    wt("gpsimd", (s_dv, m_szh), (s_pe, m_vmm))
                    do("gpsimd", "dma_start",
                       out=wsA.rearrange("p (k j) -> p k j", k=NK),
                       in_=wslab_ap("wg", hg + 1), inc=((s_dw, 16),))
                    m_wg = s_dw.n

                if lvl == 5:
                    raise _Stop

                gtp = [psw1[:, 0:256].bitcast(BF16),
                       psw1[:, 512:768].bitcast(BF16)]
                atok2 = [t_atok[:, 0:512], view(OFF_GATH + 4128, 512, BF16)]
                c129_b = [t_c129[:, 0:516], view(OFF_GATH + 5152, 516, F32)]
                m_c1war = m_qev if hg + 1 < NHG else m_numev[NCH - 1]
                m_corr = {}
                m_cfree = {}
                m_atok = {}
                m_attcopy = {}

                def emit_corr(tt):
                    if tt == 0:
                        wt("tensor", (s_dv, m_szh))
                    elif tt == 1:
                        wt("tensor", (s_dv, m_c1war))
                    else:
                        wt("tensor", (s_ac, m_cfree[tt - 2]))
                    cpair = (psw1[:, 1024:2048] if tt % 2 == 0
                             else psw0[:, 1024:2048])
                    for hh in range(HPG):
                        hb, hl = hh // 2, hh % 2
                        do("tensor", "matmul",
                           cpair[:, hb * 512 + hl * 128:hb * 512 + hl * 128 + 128],
                           q_fm[:, hh * 1024 + tt * 128:hh * 1024 + (tt + 1) * 128],
                           t_szh[:, hh * 129:hh * 129 + 128],
                           start=(hl == 0), stop=False, skip_group_check=True)
                    for hh in range(HPG):
                        hb, hl = hh // 2, hh % 2
                        do("tensor", "matmul",
                           cpair[:, hb * 512 + 256 + hl:hb * 512 + 257 + hl],
                           q_fm[:, hh * 1024 + tt * 128:hh * 1024 + (tt + 1) * 128],
                           t_szh[:, hh * 129 + 128:hh * 129 + 129],
                           start=False, stop=False, skip_group_check=True)
                    for hb in range(2):
                        do("tensor", "matmul",
                           cpair[:, hb * 512 + 256:hb * 512 + 258],
                           t_ones[0:1, :], t_eps[0:1, :],
                           start=False, stop=True, skip_group_check=True)
                    inc_last("tensor", (s_pe, 1))
                    m_corr[tt] = s_pe.n
                    return cpair

                def emit_gate_dve(tt, cpair):
                    c129 = c129_b[tt % 2].rearrange(
                        "p (bh f) -> p bh f", f=129).rearrange(
                        "p (b h) f -> p b h f", b=2)
                    c4 = cpair.rearrange("p (b f) -> p b f", b=2)
                    wt("scalar", (s_pe, m_corr[tt]))
                    do("scalar", "copy", c129[:, :, :, 128:129],
                       c4[:, :, 256:258].rearrange("p b (h f) -> p b h f", h=2))
                    do("scalar", "copy", c129[:, :, :, 0:128],
                       c4[:, :, 0:256].rearrange("p b (h f) -> p b h f", h=2),
                       inc=((s_ac, 1),))
                    m_cfree[tt] = s_ac.n
                    t129 = t_sfin[:, 0:516].rearrange(
                        "p (bh f) -> p bh f", f=129).rearrange(
                        "p (b h) f -> p b h f", b=2)
                    wt("vector", (s_ac, m_cfree[tt]))
                    do("vector", "tensor_tensor", t129,
                       num_sb[:, tt * 516:(tt + 1) * 516].rearrange(
                           "p (bh f) -> p bh f", f=129).rearrange(
                           "p (b h) f -> p b h f", b=2),
                       c129, ALU.add, inc=((s_dv, 1),))
                    wt("scalar", (s_dv, s_dv.n))
                    do("scalar", "copy", t_dtmp[:, :],
                       t129[:, :, :, 128:129].rearrange("p b h f -> p (b h f)"),
                       inc=((s_ac, 1),))
                    m_den = s_ac.n
                    a4 = atok2[tt % 2].rearrange(
                        "p (b h f) -> p b h f", b=2, h=2)
                    g4 = g_tok[:, tt * 512:(tt + 1) * 512].rearrange(
                        "p (b h f) -> p b h f", b=2, h=2)
                    do("vector", "tensor_tensor", a4[:, 0:1],
                       t129[:, 0:1, :, 0:128], g4[:, 0:1], ALU.mult)
                    wt("vector", (s_ac, m_den))
                    do("vector", "reciprocal", t_rden[:, :], t_dtmp[:, :])
                    do("vector", "tensor_tensor", a4[:, 1:2],
                       t129[:, 1:2, :, 0:128], g4[:, 1:2], ALU.mult)
                    do("vector", "tensor_tensor", a4, a4,
                       t_rden[:, :].rearrange("p (b h) -> p b h", b=2).unsqueeze(
                           3).broadcast_to([128, 2, 2, 128]), ALU.mult,
                       inc=((s_dv, 1),))
                    m_atok[tt] = s_dv.n

                def emit_attnT(tt):
                    wts = [(s_dv, m_atok[tt])]
                    if tt >= 2:
                        wts.append((s_dv, m_attcopy[tt - 2]))
                    wt("tensor", *wts)
                    tp = gtp[tt % 2]
                    src = atok2[tt % 2]
                    for hh in range(HPG):
                        do("tensor", "transpose",
                           tp[:, hh * 128:(hh + 1) * 128],
                           src[:, hh * 128:(hh + 1) * 128], t_id[:, :])
                    inc_last("tensor", (s_pe, 1))
                    wt("vector", (s_pe, s_pe.n))
                    do("vector", "tensor_copy",
                       attn_fm.rearrange("p (a t) -> p a t", a=NK)[
                           :, hg * 4:(hg + 1) * 4, tt * 128:(tt + 1) * 128],
                       tp.rearrange("p (a t) -> p a t", a=HPG),
                       inc=((s_dv, 1),))
                    m_attcopy[tt] = s_dv.n

                emit_gate_dve(0, emit_corr(0))
                for tt in range(1, NT):
                    emit_gate_dve(tt, emit_corr(tt))
                    emit_attnT(tt - 1)
                emit_attnT(NT - 1)
                m_gate[hg] = m_attcopy[NT - 1]
                m_gatepsw0 = m_cfree[NT - 1]

            if lvl < 7:
                raise _Stop


            wo_h = [view(OFF_WO, NK * 1024, BF16), view(OFF_XN, NK * 1024, BF16)]
            xld = [view(OFF_P3X0, 1024, F32), view(OFF_P3X1, 1024, F32)]
            ost = [view(OFF_P3O0, 1024, F32), view(OFF_P3O1, 1024, F32)]

            def wohalf_ap(hf):
                return d_w["wo"][:, hf * 1024:(hf + 1) * 1024].rearrange(
                    "(k p) j -> p k j", p=128)

            wt("gpsimd", (s_pe, m_gqmm[NHG - 1]))
            do("gpsimd", "dma_start", out=t_bias4[0:1, :],
               in_=d_b["bo"][0:1, 0:1024], inc=((s_dw, 16),))
            m_bo = [s_dw.n]
            wt("gpsimd", (s_cc, s_cc.n))
            do("gpsimd", "dma_start",
               out=wo_h[0].rearrange("p (k j) -> p k j", k=NK), in_=wohalf_ap(0),
               inc=((s_dw, 16),))
            m_wo = [s_dw.n]
            do("gpsimd", "dma_start",
               out=wo_h[1].rearrange("p (k j) -> p k j", k=NK), in_=wohalf_ap(1),
               inc=((s_dw, 16),))
            m_wo.append(s_dw.n)

            m_comb = {}
            m_p3mm = {}
            for hf in range(2):
                wo_half = wo_h[hf]
                wt("tensor", (s_dw, m_wo[hf]))
                if hf == 0:
                    wt("tensor", (s_dv, m_gate[NHG - 1]))
                for tt in range(NT):
                    ps = psw0 if tt % 2 == 0 else psw1
                    pshalf = ps[:, 0:1024]
                    prev = None
                    if tt >= 2:
                        prev = m_comb[(hf, tt - 2)]
                    elif hf == 1:
                        prev = m_comb[(0, NT - 2 + tt)]
                    if prev is not None:
                        wt("tensor", (s_dv, prev))
                    for k in range(NK):
                        lhsT = attn_fm[:, k * 1024 + tt * 128:k * 1024 + (tt + 1) * 128]
                        for jc in range(2):
                            do("tensor", "matmul",
                               pshalf[:, jc * 512:(jc + 1) * 512], lhsT,
                               wo_half[:, k * 1024 + jc * 512:k * 1024 + (jc + 1) * 512],
                               start=(k == 0), stop=False,
                               skip_group_check=True)
                    if tt == 0:
                        wt("tensor", (s_dw, m_bo[hf]))
                    for jc in range(2):
                        do("tensor", "matmul", pshalf[:, jc * 512:(jc + 1) * 512],
                           t_ones[0:1, :],
                           t_bias4[0:1, jc * 512:(jc + 1) * 512],
                           start=False, stop=True, skip_group_check=True)
                    inc_last("tensor", (s_pe, 1))
                    m_smm = s_pe.n
                    m_p3mm[(hf, tt)] = m_smm
                    if prev is not None:
                        wt("sync", (s_dv, prev))
                    elif hf == 0 and tt == 0:
                        wt("sync", (s_dv, m_gate[NHG - 1]))
                    do("sync", "dma_start", out=xld[tt % 2],
                       in_=d_x[tt * 128:(tt + 1) * 128, hf * 1024:(hf + 1) * 1024],
                       inc=((s_dx, 16),))
                    m_xr = s_dx.n
                    wt("vector", (s_pe, m_smm), (s_dx, m_xr))
                    if hf == 0 and tt == 0:
                        wt("vector", (s_ac, m_gffn))
                    do("vector", "tensor_tensor", ost[tt % 2], pshalf,
                       gffn[:, tt * 2048 + hf * 1024:tt * 2048 + (hf + 1) * 1024],
                       ALU.mult)
                    do("vector", "tensor_tensor", ost[tt % 2], ost[tt % 2],
                       xld[tt % 2], ALU.add, inc=((s_dv, 1),))
                    m_comb[(hf, tt)] = s_dv.n
                    wt("sync", (s_dv, m_comb[(hf, tt)]))
                    do("sync", "dma_start",
                       out=d_y[tt * 128:(tt + 1) * 128, hf * 1024:(hf + 1) * 1024],
                       in_=ost[tt % 2], inc=((s_out, 16),))
                if hf == 0:
                    wt("gpsimd", (s_pe, m_p3mm[(0, NT - 1)]))
                    do("gpsimd", "dma_start", out=t_bias4[0:1, :],
                       in_=d_b["bo"][0:1, 1024:2048], inc=((s_dw, 16),))
                    m_bo.append(s_dw.n)


        except _Stop:
            pass

        dbg_srcs = {
            "xn_fm": xn_fm, "gffn": gffn, "ga": ga, "k_tok": k_tok2[0],
            "v_tok": v_tok2[0], "g_tok": g_tok, "q_fm": q_fm, "num_sb": num_sb,
            "den": t_den[:, :], "attn_fm": attn_fm, "szh": t_szh[:, :],
            "stats": t_stats[:, :], "xns1": xns[1], "xb1": xb[1],
        }
        if debug:
            dchunk = view(OFF_BIG, 2048, F32)
            for (nm, shape) in debug:
                src_ap = dbg_srcs[nm]
                n = src_ap.shape[1]
                for c0 in range(0, n, 2048):
                    c1 = min(c0 + 2048, n)
                    wt("vector", (s_dv, s_dv.n), (s_ac, s_ac.n),
                       (s_pe, s_pe.n), (s_dx, s_dx.n), (s_dw, s_dw.n))
                    do("vector", "tensor_copy", dchunk[:, 0:c1 - c0],
                       src_ap[:, c0:c1], inc=((s_dv, 1),))
                    wt("gpsimd", (s_dv, s_dv.n))
                    do("gpsimd", "dma_start", out=d_dbg[nm][:, c0:c1],
                       in_=dchunk[:, 0:c1 - c0], inc=((s_dw, 16),))

        wt("sync", (s_out, s_out.n))
        wt("gpsimd", (s_dw, s_dw.n))

        with nc.Block() as block:
            @block.tensor
            def _(t):
                for f in P["tensor"]:
                    f(t)

            @block.vector
            def _(v):
                for f in P["vector"]:
                    f(v)

            @block.scalar
            def _(sc):
                for f in P["scalar"]:
                    f(sc)

            @block.gpsimd
            def _(g):
                for f in P["gpsimd"]:
                    f(g)

            @block.sync
            def _(sy):
                for f in P["sync"]:
                    f(sy)

    return nc


def _perm_deint():
    p = np.zeros(D, np.int64)
    for h in range(H):
        b = h * DK
        for i in range(DK // 2):
            p[b + i] = b + 2 * i
            p[b + 64 + i] = b + 2 * i + 1
    return p


def _rope_tables():
    inv = (1.0 / 10000.0 ** (np.arange(0, DK, 2, dtype=np.float32)
                             / np.float32(DK))).astype(np.float32)
    pos = np.arange(2048, dtype=np.float32)
    ang = pos[:, None] * inv[None, :]
    return np.sin(ang).astype(np.float32), np.cos(ang).astype(np.float32)


_DEBUG = ()


def kernel(L0_attn_x, L0_attn_Wq, L0_attn_bq, L0_attn_Wk, L0_attn_bk,
           L0_attn_Wv, L0_attn_bv, L0_attn_Wg, L0_attn_bg, L0_attn_Wo,
           L0_attn_bo, L0_ffn_up_weight, L0_ffn_down_weight, L0_ffn_up_scale,
           L0_ffn_down_scale, L0_ffn_output_scale):
    x = np.ascontiguousarray(
        np.asarray(L0_attn_x, np.float32).reshape(4 * 2048, D))
    p = _perm_deint()
    wq = np.ascontiguousarray(np.asarray(L0_attn_Wq, np.float32).T[:, p])
    wk = np.ascontiguousarray(np.asarray(L0_attn_Wk, np.float32).T[:, p])
    wv = np.ascontiguousarray(np.asarray(L0_attn_Wv, np.float32).T)
    wg = np.ascontiguousarray(np.asarray(L0_attn_Wg, np.float32).T)
    wo = np.ascontiguousarray(np.asarray(L0_attn_Wo, np.float32).T)
    upw = np.ascontiguousarray(np.asarray(L0_ffn_up_weight, np.float32).T)
    dnw = np.ascontiguousarray(np.asarray(L0_ffn_down_weight, np.float32).T)
    bq = np.ascontiguousarray(np.asarray(L0_attn_bq, np.float32)[p][None, :])
    bk = np.ascontiguousarray(np.asarray(L0_attn_bk, np.float32)[p][None, :])
    bv = np.ascontiguousarray(np.asarray(L0_attn_bv, np.float32)[None, :])
    bg = np.ascontiguousarray(np.asarray(L0_attn_bg, np.float32)[None, :])
    bo = np.ascontiguousarray(np.asarray(L0_attn_bo, np.float32)[None, :])
    usc = np.ascontiguousarray(
        np.asarray(L0_ffn_up_scale, np.float32).reshape(64, 128).T)
    dsc = np.ascontiguousarray(
        np.asarray(L0_ffn_down_scale, np.float32)[None, :])
    osc = np.ascontiguousarray(
        np.asarray(L0_ffn_output_scale, np.float32).reshape(1, 1))
    sin_f, cos_f = _rope_tables()
    ident = np.eye(128, dtype=np.float32)

    nc = build_nc(debug=_DEBUG)

    in_maps = []
    for c in range(NCORE):
        half = c % 2
        t0 = half * 1024
        mask = np.zeros((128, 8), np.float32)
        if half == 1:
            mask[:, c - 1] = 1.0
        in_maps.append({
            "x": np.ascontiguousarray(x[c * TPC:(c + 1) * TPC]),
            "wq": wq, "wk": wk, "wv": wv, "wg": wg, "wo": wo,
            "upw": upw, "dnw": dnw,
            "bq": bq, "bk": bk, "bv": bv, "bg": bg, "bo": bo,
            "usc": usc, "dsc": dsc, "osc": osc,
            "rsin": np.ascontiguousarray(sin_f[t0:t0 + 1024]),
            "rcos": np.ascontiguousarray(cos_f[t0:t0 + 1024]),
            "ident": ident,
            "mask8": mask,
        })

    res = run_bass_kernel_spmd(nc, in_maps, core_ids=list(range(NCORE)))
    out = np.concatenate([res.results[c]["y"] for c in range(NCORE)], axis=0)
    if _DEBUG:
        kernel._dbg = [{nm: res.results[c]["dbg_" + nm] for (nm, _s) in _DEBUG}
                       for c in range(NCORE)]
    return out.reshape(4, 2048, D)


# version 17
# speedup vs baseline: 1.3901x; 1.0199x over previous
import numpy as np
from contextlib import ExitStack

import concourse.bass as bass
import concourse.mybir as mybir
import concourse.bass_utils as _bu
from concourse.bass_utils import run_bass_kernel_spmd

if not getattr(_bu, "_ldw_patched", False):
    _orig_run_command = _bu.run_command

    def _run_command_ldw(cmd, *a, **kw):
        if isinstance(cmd, list):
            cmd = ["--enable-ldw-opt=true" if c == "--enable-ldw-opt=false"
                   else c for c in cmd]
        return _orig_run_command(cmd, *a, **kw)

    _bu.run_command = _run_command_ldw
    _bu._ldw_patched = True

F32 = mybir.dt.float32
BF16 = mybir.dt.bfloat16
U8 = mybir.dt.uint8
AF = mybir.ActivationFunctionType
ALU = mybir.AluOpType
AX = mybir.AxisListType

D = 2048
H = 16
DK = 128
CS = 64
TPC = 1024
NT = 8
NK = 16
NCH = 16
NSLAB = 8
DH = 8192
NHG = 4
HPG = 4
NCORE = 8

OFF_XN = 0
OFF_GA = 32768
OFF_ATTN = 32768
OFF_K = (65536, 65536 + 8192)
OFF_V = (65536 + 16384, 65536 + 24576)
OFF_G = 65536 + 32768
OFF_Q = 65536 + 40960
OFF_BIG = 65536 + 49152
BIG_SZ = 82304
ARENA_BYTES = OFF_BIG + BIG_SZ

OFF_XB0 = OFF_BIG + 32768
OFF_XB1 = OFF_BIG + 40960
OFF_X2 = OFF_BIG + 49152
OFF_XNS0 = OFF_BIG + 57344
OFF_XNS1 = OFF_BIG + 61440
OFF_UPS = OFF_BIG
OFF_DNS = OFF_BIG + 32768
OFF_HID = OFF_BIG + 65536
OFF_DSREP = OFF_BIG + 65536
OFF_DSROW = OFF_BIG + 73856
OFF_WSA = OFF_BIG
OFF_WSB = OFF_BIG + 16384
OFF_GFFN = OFF_BIG + 32768
OFF_QTMP = OFF_BIG + 65536
OFF_NUMSB = OFF_BIG + 65536
OFF_RT = OFF_BIG + 73856
OFF_GATH = OFF_BIG + 73856
OFF_WO = OFF_BIG
OFF_P3X0 = OFF_BIG + 65536
OFF_P3X1 = OFF_BIG + 69632
OFF_P3O0 = OFF_BIG + 73856
OFF_P3O1 = OFF_BIG + 77952

ENGS = ("tensor", "vector", "scalar", "gpsimd", "sync")


class _Stop(Exception):
    pass


class Sem:
    __slots__ = ("h", "n")

    def __init__(self, h):
        self.h = h
        self.n = 0


def build_nc(debug=(), upto="full"):
    order = ("p0", "p1", "p2proj", "p2fm", "p2scan", "p2cc", "p2", "full")
    lvl = order.index(upto)
    nc = bass.Bass("TRN2", target_bir_lowering=False,
                   detect_race_conditions=False)

    d_x = nc.dram_tensor("x", [TPC, D], F32, kind="ExternalInput").ap()
    d_w = {nm: nc.dram_tensor(nm, [D, D], F32, kind="ExternalInput").ap()
           for nm in ("wq", "wk", "wv", "wg", "wo")}
    d_up = nc.dram_tensor("upw", [D, DH], F32, kind="ExternalInput").ap()
    d_dn = nc.dram_tensor("dnw", [DH, D], F32, kind="ExternalInput").ap()
    d_b = {nm: nc.dram_tensor(nm, [1, D], F32, kind="ExternalInput").ap()
           for nm in ("bq", "bk", "bv", "bg", "bo")}
    d_usc = nc.dram_tensor("usc", [128, 64], F32, kind="ExternalInput").ap()
    d_dsc = nc.dram_tensor("dsc", [1, D], F32, kind="ExternalInput").ap()
    d_osc = nc.dram_tensor("osc", [1, 1], F32, kind="ExternalInput").ap()
    d_sin = nc.dram_tensor("rsin", [TPC, 64], F32, kind="ExternalInput").ap()
    d_cos = nc.dram_tensor("rcos", [TPC, 64], F32, kind="ExternalInput").ap()
    d_id = nc.dram_tensor("ident", [128, 128], F32, kind="ExternalInput").ap()
    d_msk = nc.dram_tensor("mask8", [128, 8], F32, kind="ExternalInput").ap()
    d_y = nc.dram_tensor("y", [TPC, D], F32, kind="ExternalOutput").ap()
    if lvl >= 5:
        d_ccs = [nc.dram_tensor(f"cc_src{h}", [128, 516], BF16)
                 for h in range(NHG)]
        d_ccd = [nc.dram_tensor(f"cc_dst{h}", [128 * NCORE, 516], BF16,
                                addr_space="Shared") for h in range(NHG)]
        d_wus = nc.dram_tensor("cc_wsrc", [1, 16], BF16)
        d_wud = nc.dram_tensor("cc_wdst", [NCORE, 16], BF16,
                               addr_space="Shared")
    d_dbg = {nm: nc.dram_tensor("dbg_" + nm, list(shape), F32,
                                kind="ExternalOutput").ap()
             for (nm, shape) in debug}

    P = {e: [] for e in ENGS}

    def do(eng, method, *a, inc=(), **kw):
        if method == "dma_start":
            for s, k in inc:
                if k == 16 and s.n:
                    wt(eng, (s, s.n))
        incs = [(s.h, k) for (s, k) in inc]

        def f(E, m=method, a=a, kw=kw, incs=incs):
            ins = getattr(E, m)(*a, **kw)
            for hh, kk in incs:
                ins = ins.then_inc(hh, kk)
        f._incs = incs
        P[eng].append(f)
        for s, k in inc:
            s.n += k

    def inc_last(eng, *pairs):
        f = P[eng][-1]
        for s, k in pairs:
            for idx, (hh, kk) in enumerate(f._incs):
                if hh is s.h:
                    f._incs[idx] = (hh, kk + k)
                    break
            else:
                f._incs.append((s.h, k))
            s.n += k

    def wt(eng, *pairs):
        ps = tuple((s.h, v) for (s, v) in pairs)

        def f(E, ps=ps):
            for hh, vv in ps:
                E.wait_ge(hh, vv)
        P[eng].append(f)

    with ExitStack() as stk:
        arena = stk.enter_context(nc.sbuf_tensor("arena", [128, ARENA_BYTES], U8))
        t_sin = stk.enter_context(nc.sbuf_tensor("t_sin", [128, NT * 64], F32))
        t_cos = stk.enter_context(nc.sbuf_tensor("t_cos", [128, NT * 64], F32))
        t_usc = stk.enter_context(nc.sbuf_tensor("t_usc", [128, 64], F32))
        t_id = stk.enter_context(nc.sbuf_tensor("t_id", [128, 128], BF16))
        t_ones = stk.enter_context(nc.sbuf_tensor("t_ones", [1, 128], BF16))
        t_ones32 = stk.enter_context(nc.sbuf_tensor("t_ones32", [1, 128], F32))
        t_onescol = stk.enter_context(nc.sbuf_tensor("t_onescol", [128, 1], BF16))
        t_bias4 = stk.enter_context(nc.sbuf_tensor("t_bias4", [1, 1024], BF16))
        t_cst = stk.enter_context(nc.sbuf_tensor("t_cst", [128, 6], F32))
        t_stats = stk.enter_context(nc.sbuf_tensor("t_stats", [128, 8], F32))
        t_msk = stk.enter_context(nc.sbuf_tensor("t_msk", [128, 8], F32))
        t_den = stk.enter_context(nc.sbuf_tensor("t_den", [128, NT * HPG], F32))
        t_dtmp = stk.enter_context(nc.sbuf_tensor("t_dtmp", [128, HPG], F32))
        t_rden = stk.enter_context(nc.sbuf_tensor("t_rden", [128, HPG], F32))
        t_szh = stk.enter_context(nc.sbuf_tensor("t_szh", [128, 516], BF16))
        t_sfin = stk.enter_context(nc.sbuf_tensor("t_sfin", [128, 516], F32))
        t_atok = stk.enter_context(nc.sbuf_tensor("t_atok", [128, 512], BF16))
        t_c129 = stk.enter_context(nc.sbuf_tensor("t_c129", [128, 520], F32))
        t_os = stk.enter_context(nc.sbuf_tensor("t_os", [1, 1], F32))
        t_os128 = stk.enter_context(nc.sbuf_tensor("t_os128", [1, 128], F32))
        t_eps = stk.enter_context(nc.sbuf_tensor("t_eps", [1, 2], BF16))

        psw0 = stk.enter_context(nc.psum_tensor("psw0", [128, 2048], F32))
        psw1 = stk.enter_context(nc.psum_tensor("psw1", [128, 2048], F32))

        s_dx = Sem(stk.enter_context(nc.semaphore("s_dx")))
        s_dw = Sem(stk.enter_context(nc.semaphore("s_dw")))
        s_out = Sem(stk.enter_context(nc.semaphore("s_out")))
        s_pe = Sem(stk.enter_context(nc.semaphore("s_pe")))
        s_dv = Sem(stk.enter_context(nc.semaphore("s_dv")))
        s_ac = Sem(stk.enter_context(nc.semaphore("s_ac")))
        s_gp = Sem(stk.enter_context(nc.semaphore("s_gp")))
        s_cc = Sem(stk.enter_context(nc.semaphore("s_cc")))

        def view(off, nelem, dt):
            sz = 4 if dt == F32 else 2
            return arena[:, off:off + nelem * sz].bitcast(dt)

        xn_fm = view(OFF_XN, NK * 1024, BF16)
        ga = view(OFF_GA, NT * 2048, F32)
        attn_fm = view(OFF_ATTN, NK * 1024, BF16)
        k_tok2 = [view(OFF_K[0], NT * 512, BF16), view(OFF_K[1], NT * 512, BF16)]
        v_tok2 = [view(OFF_V[0], NT * 512, BF16), view(OFF_V[1], NT * 512, BF16)]
        g_tok = view(OFF_G, NT * 512, BF16)
        q_fm = view(OFF_Q, HPG * 1024, BF16)
        gffn = view(OFF_GFFN, NT * 2048, BF16)
        up_sgn = view(OFF_UPS, NK * 1024, BF16)
        dn_sgn = view(OFF_DNS, 8 * 2048, BF16)
        hid = view(OFF_HID, 8 * 1024, BF16)
        wsA = view(OFF_WSA, NK * 512, BF16)
        wsB = view(OFF_WSB, NK * 512, BF16)
        qtmp = view(OFF_QTMP, NT * 512, BF16)
        ropeT = [view(OFF_RT + i * 2048, 1024, BF16) for i in range(4)]
        num_sb = view(OFF_NUMSB, NT * 516, BF16)
        gath = view(OFF_GATH, 8 * 516, BF16)
        szh2 = t_c129[:, 0:258].bitcast(BF16)

        try:
            do("gpsimd", "memset", t_ones[:, :], 1.0)
            do("gpsimd", "memset", t_ones32[:, :], 1.0)
            do("gpsimd", "memset", t_onescol[:, :], 1.0)
            do("gpsimd", "memset", t_cst[:, 0:1], 1e-5)
            do("gpsimd", "memset", t_cst[:, 1:2], -0.5)
            do("gpsimd", "memset", t_cst[:, 2:3], 1.0 / D)
            do("gpsimd", "memset", t_cst[:, 3:4], -0.34657359027997264)
            do("gpsimd", "memset", t_cst[:, 4:5], 0.015625)
            do("gpsimd", "memset", t_eps[:, :], 1e-6, inc=((s_gp, 1),))
            m_pre = s_gp.n
            do("gpsimd", "dma_start",
               out=t_id[:, :], in_=d_id, inc=((s_dw, 16),))
            m_tid = s_dw.n
            do("gpsimd", "dma_start",
               out=t_sin[:, :].rearrange("p (t f) -> p t f", t=NT),
               in_=d_sin.rearrange("(t p) f -> p t f", p=128), inc=((s_dw, 16),))
            do("gpsimd", "dma_start",
               out=t_cos[:, :].rearrange("p (t f) -> p t f", t=NT),
               in_=d_cos.rearrange("(t p) f -> p t f", p=128), inc=((s_dw, 16),))
            do("gpsimd", "dma_start", out=t_usc[:, :], in_=d_usc, inc=((s_dw, 16),))
            do("gpsimd", "dma_start", out=t_msk[:, :], in_=d_msk, inc=((s_dw, 16),))
            do("gpsimd", "dma_start", out=t_os[:, :], in_=d_osc, inc=((s_dw, 16),))
            m_tables = s_dw.n
            if lvl >= 5:
                do("gpsimd", "dma_start", out=d_wus[:, :],
                   in_=t_ones[0:1, 0:16], inc=((s_dw, 16),))
                wt("gpsimd", (s_dw, s_dw.n))
                do("gpsimd", "collective_compute", "AllGather", ALU.bypass,
                   replica_groups=[list(range(NCORE))],
                   ins=[d_wus[:, :]], outs=[d_wud[:, :]], inc=((s_cc, 1),))

            def upslab_ap(s):
                return d_up[:, s * 1024:(s + 1) * 1024].rearrange(
                    "(k p) h -> p k h", p=128)

            def dnslab_ap(s):
                return d_dn[s * 1024:(s + 1) * 1024, :].rearrange(
                    "(t p) j -> p t j", p=128)

            do("gpsimd", "dma_start",
               out=up_sgn.rearrange("p (k h) -> p k h", k=NK), in_=upslab_ap(0),
               inc=((s_dw, 16),))
            m_updma = [s_dw.n]

            xb = [view(OFF_XB0, 2048, F32), view(OFF_XB1, 2048, F32)]
            x2 = view(OFF_GA, 2049, F32)
            xns = [view(OFF_XNS0, 2048, BF16), view(OFF_XNS1, 2048, BF16)]
            do("gpsimd", "memset", x2[:, 2048:2049], 5e-6, inc=((s_gp, 1),))
            m_x2e = s_gp.n
            m_ts = [0] * NT
            m_tp4 = [0] * NT
            m_tiny = [0] * NT
            m_tp_ev = [0, 0]

            def emit_ln_tail(j):
                st = t_stats[:, 4 * (j % 2):4 * (j % 2) + 4]
                wts = [(s_ac, m_tiny[j])]
                if j >= 2:
                    wts.append((s_pe, m_tp4[j - 2]))
                wt("vector", *wts)
                do("vector", "tensor_scalar", xns[j % 2], xb[j % 2],
                   st[:, 1:2], st[:, 3:4], ALU.mult, ALU.subtract,
                   inc=((s_dv, 1),))
                m_ts[j] = s_dv.n
                wt("tensor", (s_dv, m_ts[j]), (s_dw, m_tid))
                for kq in range(4):
                    pbank = psw0[:, (kq % 2) * 512:(kq % 2) * 512 + 256].bitcast(BF16)
                    if m_tp_ev[kq % 2]:
                        wt("tensor", m_tp_ev[kq % 2])
                    for kk in range(4):
                        k = kq * 4 + kk
                        do("tensor", "transpose", pbank[:, kk * 128:(kk + 1) * 128],
                           xns[j % 2][:, k * 128:(k + 1) * 128], t_id[:, :])
                    inc_last("tensor", (s_pe, 1))
                    ev_eng, ev_m, ev_sem = (("vector", "tensor_copy", s_dv)
                                            if kq < 2 else
                                            ("scalar", "copy", s_ac))
                    wt(ev_eng, (s_pe, s_pe.n))
                    do(ev_eng, ev_m,
                       xn_fm.rearrange("p (k t) -> p k t", k=NK)[
                           :, kq * 4:(kq + 1) * 4, j * 128:(j + 1) * 128],
                       pbank.rearrange("p (k t) -> p k t", k=4),
                       inc=((ev_sem, 1),))
                    m_tp_ev[kq % 2] = (ev_sem, ev_sem.n)
                m_tp4[j] = s_pe.n

            for j in range(NT):
                if j >= 2:
                    wt("sync", (s_dv, m_ts[j - 2]))
                do("sync", "dma_start", out=xb[j % 2],
                   in_=d_x[j * 128:(j + 1) * 128, :], inc=((s_dx, 16),))
                m_xdma = s_dx.n
                st = t_stats[:, 4 * (j % 2):4 * (j % 2) + 4]
                if j == 0:
                    wt("scalar", (s_dx, m_xdma), (s_gp, m_pre))
                else:
                    wt("scalar", (s_dx, m_xdma))
                do("scalar", "activation", out=x2[:, 0:2048], in_=xb[j % 2],
                   func=AF.Square, scale=t_cst[:, 4:5], inc=((s_ac, 1),))
                m_sq = s_ac.n
                wt("vector", (s_dx, m_xdma))
                do("vector", "reduce_sum", st[:, 3:4], xb[j % 2], axis=AX.X,
                   inc=((s_dv, 1),))
                if j == 0:
                    wt("vector", (s_ac, m_sq), (s_gp, m_x2e))
                else:
                    wt("vector", (s_ac, m_sq))
                do("vector", "reduce_sum", st[:, 1:2], x2, axis=AX.X,
                   inc=((s_dv, 1),))
                m_r2 = s_dv.n
                wt("scalar", (s_dv, m_r2))
                for _ in range(2):
                    do("scalar", "activation", out=st[:, 2:3], in_=st[:, 3:4],
                       func=AF.Copy, scale=t_cst[:, 2:3])
                for _ in range(2):
                    do("scalar", "activation", out=st[:, 3:4], in_=st[:, 2:3],
                       func=AF.Square)
                for _ in range(2):
                    do("scalar", "activation", out=st[:, 0:1], in_=st[:, 3:4],
                       func=AF.Ln, scale=t_cst[:, 1:2], bias=st[:, 1:2])
                for _ in range(2):
                    do("scalar", "activation", out=st[:, 1:2], in_=st[:, 0:1],
                       func=AF.Exp, scale=t_cst[:, 1:2], bias=t_cst[:, 3:4])
                do("scalar", "activation", out=st[:, 3:4], in_=st[:, 2:3],
                   func=AF.Copy, scale=st[:, 1:2], inc=((s_ac, 1),))
                m_tiny[j] = s_ac.n
                if j > 0:
                    emit_ln_tail(j - 1)
            emit_ln_tail(NT - 1)
            m_xn_all = s_dv.n

            wt("gpsimd", (s_dv, m_xn_all), (s_pe, m_tp4[NT - 1]))
            do("gpsimd", "dma_start",
               out=dn_sgn.rearrange("p (t j) -> p t j", t=8), in_=dnslab_ap(0),
               inc=((s_dw, 16),))
            m_dndma = [s_dw.n]

            if lvl < 1:
                raise _Stop

            m_upmm = [0] * NSLAB
            m_dnmm = [0] * NSLAB
            m_gadd = {}

            for s in range(NSLAB):
                wt("scalar", (s_dw, m_updma[s]))
                do("scalar", "activation", out=up_sgn, in_=up_sgn, func=AF.Sign,
                   inc=((s_ac, 1),))
                m_upsgn = s_ac.n
                wt("scalar", (s_dw, m_dndma[s]))
                do("scalar", "activation", out=dn_sgn, in_=dn_sgn, func=AF.Sign,
                   inc=((s_ac, 1),))
                m_dnsgn = s_ac.n

                if s == 0:
                    wt("tensor", (s_ac, m_upsgn), (s_dv, m_xn_all))
                else:
                    wt("tensor", (s_ac, m_upsgn), (s_dv, m_gadd[(s - 1, NT - 1)]))
                ev_ms = {}
                for ht in range(8):
                    pA = psw0[:, (ht % 2) * 1024:(ht % 2) * 1024 + 512]
                    pB = psw0[:, (ht % 2) * 1024 + 512:(ht % 2) * 1024 + 1024]
                    if ht >= 2:
                        wt("tensor", (s_dv, ev_ms[ht - 2]))
                    for k in range(NK):
                        lhsT = up_sgn[:, k * 1024 + ht * 128:k * 1024 + (ht + 1) * 128]
                        do("tensor", "matmul", pA, lhsT,
                           xn_fm[:, k * 1024:k * 1024 + 512],
                           start=(k == 0), stop=(k == NK - 1), skip_group_check=True)
                        do("tensor", "matmul", pB, lhsT,
                           xn_fm[:, k * 1024 + 512:k * 1024 + 1024],
                           start=(k == 0), stop=(k == NK - 1), skip_group_check=True)
                    inc_last("tensor", (s_pe, 1))
                    wt("vector", (s_pe, s_pe.n))
                    do("vector", "tensor_scalar", hid[:, ht * 1024:ht * 1024 + 512],
                       pA, t_usc[:, s * 8 + ht:s * 8 + ht + 1], 0.0, ALU.mult, ALU.max)
                    do("vector", "tensor_scalar",
                       hid[:, ht * 1024 + 512:(ht + 1) * 1024],
                       pB, t_usc[:, s * 8 + ht:s * 8 + ht + 1], 0.0, ALU.mult, ALU.max,
                       inc=((s_dv, 1),))
                    ev_ms[ht] = s_dv.n
                inc_last("tensor", (s_pe, 1))
                m_upmm[s] = s_pe.n
                m_hid_done = ev_ms[7]

                if s + 1 < NSLAB:
                    wt("gpsimd", (s_pe, m_upmm[s]))
                    do("gpsimd", "dma_start",
                       out=up_sgn.rearrange("p (k h) -> p k h", k=NK),
                       in_=upslab_ap(s + 1), inc=((s_dw, 16),))
                    m_updma.append(s_dw.n)

                wt("tensor", (s_dv, m_hid_done), (s_ac, m_dnsgn))
                for tt in range(NT):
                    ps = psw0 if tt % 2 == 0 else psw1
                    if tt >= 2:
                        wt("tensor", (s_dv, m_gadd[(s, tt - 2)]))
                    for ht in range(8):
                        lhsT = hid[:, ht * 1024 + tt * 128:ht * 1024 + (tt + 1) * 128]
                        for jc in range(4):
                            do("tensor", "matmul", ps[:, jc * 512:(jc + 1) * 512],
                               lhsT,
                               dn_sgn[:, ht * 2048 + jc * 512:ht * 2048 + (jc + 1) * 512],
                               start=(ht == 0), stop=(ht == 7), skip_group_check=True)
                    inc_last("tensor", (s_pe, 1))
                    wt("vector", (s_pe, s_pe.n))
                    if s == 0:
                        do("vector", "tensor_copy", ga[:, tt * 2048:(tt + 1) * 2048],
                           ps[:, :], inc=((s_dv, 1),))
                    else:
                        do("vector", "tensor_tensor", ga[:, tt * 2048:(tt + 1) * 2048],
                           ga[:, tt * 2048:(tt + 1) * 2048], ps[:, :], ALU.add,
                           inc=((s_dv, 1),))
                    m_gadd[(s, tt)] = s_dv.n
                inc_last("tensor", (s_pe, 1))
                m_dnmm[s] = s_pe.n

                if s + 1 < NSLAB:
                    wt("gpsimd", (s_pe, m_dnmm[s]))
                    do("gpsimd", "dma_start",
                       out=dn_sgn.rearrange("p (t j) -> p t j", t=8),
                       in_=dnslab_ap(s + 1), inc=((s_dw, 16),))
                    m_dndma.append(s_dw.n)

            ds_rep = view(OFF_DSREP, 2048, F32)
            ds_tmp = view(OFF_DSROW, 2048, F32)
            wt("sync", (s_pe, m_dnmm[NSLAB - 1]))
            do("sync", "dma_start", out=ds_tmp[0:1, :], in_=d_dsc, inc=((s_dx, 16),))
            m_dsrow = s_dx.n
            wt("vector", (s_dw, m_tables))
            do("vector", "tensor_scalar", t_os128[0:1, :], t_ones32[0:1, :],
               t_os[0:1, 0:1], None, ALU.mult, inc=((s_dv, 1),))
            m_os128 = s_dv.n
            wt("tensor", (s_dv, m_os128), (s_dx, m_dsrow))
            for jc in range(4):
                do("tensor", "matmul", psw1[:, jc * 512:(jc + 1) * 512],
                   t_os128[0:1, :], ds_tmp[0:1, jc * 512:(jc + 1) * 512],
                   start=True, stop=True, skip_group_check=True)
            inc_last("tensor", (s_pe, 1))
            wt("vector", (s_pe, s_pe.n), (s_dv, m_gadd[(NSLAB - 1, NT - 1)]))
            do("vector", "tensor_copy", ds_rep, psw1[:, :], inc=((s_dv, 1),))
            dst_b = [ds_tmp, view(OFF_Q, 2048, F32)]
            m_silu = {}
            for tt in range(NT):
                if tt >= 2:
                    wt("vector", (s_ac, m_silu[tt - 2]))
                do("vector", "tensor_tensor", dst_b[tt % 2],
                   ga[:, tt * 2048:(tt + 1) * 2048],
                   ds_rep, ALU.mult, inc=((s_dv, 1),))
                wt("scalar", (s_dv, s_dv.n))
                do("scalar", "activation", out=gffn[:, tt * 2048:(tt + 1) * 2048],
                   in_=dst_b[tt % 2], func=AF.Silu, inc=((s_ac, 1),))
                m_silu[tt] = s_ac.n
            m_gffn = s_ac.n
            m_ga_last = m_gadd[(NSLAB - 1, NT - 1)]

            if lvl < 2:
                raise _Stop


            sA = psw1[:, 0:512]
            sB = psw1[:, 512:1024]
            tp_bf = psw1[:, 1024:1280].bitcast(BF16)
            tp_bf2 = psw1[:, 1536:1792].bitcast(BF16)

            def wslab_ap(wname, hg):
                return d_w[wname][:, hg * 512:(hg + 1) * 512].rearrange(
                    "(k p) j -> p k j", p=128)

            m_gate = [0] * NHG
            m_kvmm = [0] * (NHG + 1)
            ev_kv = [dict() for _ in range(NHG + 1)]
            m_b4 = [0] * NHG
            m_gqmm = [0] * NHG

            wt("gpsimd", (s_pe, m_upmm[NSLAB - 1]))
            do("gpsimd", "dma_start", out=wsA.rearrange("p (k j) -> p k j", k=NK),
               in_=wslab_ap("wk", 0), inc=((s_dw, 16),))
            do("gpsimd", "dma_start", out=wsB.rearrange("p (k j) -> p k j", k=NK),
               in_=wslab_ap("wv", 0), inc=((s_dw, 16),))
            m_wkv = s_dw.n
            for bi, bn in enumerate(("bk", "bv")):
                do("gpsimd", "dma_start",
                   out=t_bias4[0:1, bi * 512:(bi + 1) * 512],
                   in_=d_b[bn][0:1, 0:512], inc=((s_dw, 16),))
            m_b4[0] = s_dw.n

            def emit_kv_group(hg, tt, m_w):
                ks = k_tok2[hg % 2]
                vs = v_tok2[hg % 2]
                kps = psw0[:, 0:512]
                vps = psw0[:, 512:1024]
                if tt == 0:
                    wt("tensor", (s_dw, m_w), (s_dw, m_b4[hg]),
                       (s_dv, m_ga_last))
                else:
                    wt("tensor", (s_dv, ev_kv[hg][tt - 1]))
                do("tensor", "matmul", kps, t_ones[0:1, :],
                   t_bias4[0:1, 0:512], start=True, stop=False,
                   skip_group_check=True)
                do("tensor", "matmul", vps, t_ones[0:1, :],
                   t_bias4[0:1, 512:1024], start=True, stop=False,
                   skip_group_check=True)
                for k in range(NK):
                    lhsT = xn_fm[:, k * 1024 + tt * 128:k * 1024 + (tt + 1) * 128]
                    do("tensor", "matmul", kps, lhsT,
                       wsA[:, k * 512:(k + 1) * 512],
                       start=False, stop=(k == NK - 1), skip_group_check=True)
                    do("tensor", "matmul", vps, lhsT,
                       wsB[:, k * 512:(k + 1) * 512],
                       start=False, stop=(k == NK - 1), skip_group_check=True)
                inc_last("tensor", (s_pe, 1))
                wt("vector", (s_pe, s_pe.n))
                do("vector", "tensor_copy", ks[:, tt * 512:(tt + 1) * 512], kps)
                do("vector", "tensor_copy", vs[:, tt * 512:(tt + 1) * 512], vps,
                   inc=((s_dv, 1),))
                ev_kv[hg][tt] = s_dv.n

            def emit_q_proj(hg, m_wq, m_bq, m_qdst_free):
                qdst = v_tok2[(hg + 1) % 2]
                ev_prev = [0, 0]
                for tt in range(NT):
                    qps = psw0[:, 1024 + (tt % 2) * 512:1536 + (tt % 2) * 512]
                    if tt == 0:
                        wt("tensor", (s_dw, m_wq), (s_dv, m_qdst_free))
                    elif tt >= 2:
                        wt("tensor", (s_dv, ev_prev[tt % 2]))
                    for k in range(NK):
                        lhsT = xn_fm[:, k * 1024 + tt * 128:k * 1024 + (tt + 1) * 128]
                        do("tensor", "matmul", qps, lhsT,
                           wsB[:, k * 512:(k + 1) * 512],
                           start=(k == 0), stop=False, skip_group_check=True)
                    if tt == 0:
                        wt("tensor", (s_dw, m_bq))
                    do("tensor", "matmul", qps, t_ones[0:1, :],
                       t_bias4[0:1, 512:1024], start=False, stop=True,
                       skip_group_check=True)
                    inc_last("tensor", (s_pe, 1))
                    wt("vector", (s_pe, s_pe.n))
                    do("vector", "tensor_copy", qdst[:, tt * 512:(tt + 1) * 512],
                       qps, inc=((s_dv, 1),))
                    ev_prev[tt % 2] = s_dv.n
                inc_last("tensor", (s_pe, 1))
                return s_pe.n, s_dv.n

            ev_k = [dict() for _ in range(NHG + 1)]
            ev_v = [dict() for _ in range(NHG + 1)]
            m_ktok = [0] * (NHG + 1)

            def emit_k_group(hg, tt, m_w, m_b):
                ks = k_tok2[hg % 2]
                kps = psw0[:, 0:512]
                if tt == 0:
                    wt("tensor", (s_dw, m_w), (s_dw, m_b))
                else:
                    wt("tensor", (s_dv, ev_k[hg][tt - 1]))
                for k in range(NK):
                    lhsT = xn_fm[:, k * 1024 + tt * 128:k * 1024 + (tt + 1) * 128]
                    do("tensor", "matmul", kps, lhsT,
                       wsB[:, k * 512:(k + 1) * 512],
                       start=(k == 0), stop=False, skip_group_check=True)
                do("tensor", "matmul", kps, t_ones[0:1, :],
                   t_bias4[0:1, 0:512], start=False, stop=True,
                   skip_group_check=True)
                inc_last("tensor", (s_pe, 1))
                wt("vector", (s_pe, s_pe.n))
                do("vector", "tensor_copy", ks[:, tt * 512:(tt + 1) * 512], kps,
                   inc=((s_dv, 1),))
                ev_k[hg][tt] = s_dv.n

            def emit_v_group(hg, tt, m_w):
                vs = v_tok2[hg % 2]
                vps = psw0[:, 512:1024]
                if tt == 0:
                    wt("tensor", (s_dw, m_w))
                else:
                    wt("tensor", (s_dv, ev_v[hg][tt - 1]))
                for k in range(NK):
                    lhsT = xn_fm[:, k * 1024 + tt * 128:k * 1024 + (tt + 1) * 128]
                    do("tensor", "matmul", vps, lhsT,
                       wsA[:, k * 512:(k + 1) * 512],
                       start=(k == 0), stop=False, skip_group_check=True)
                do("tensor", "matmul", vps, t_ones[0:1, :],
                   t_bias4[0:1, 512:1024], start=False, stop=True,
                   skip_group_check=True)
                inc_last("tensor", (s_pe, 1))
                wt("vector", (s_pe, s_pe.n))
                do("vector", "tensor_copy", vs[:, tt * 512:(tt + 1) * 512], vps,
                   inc=((s_dv, 1),))
                ev_v[hg][tt] = s_dv.n

            for tt in range(NT):
                emit_kv_group(0, tt, m_wkv)
            inc_last("tensor", (s_pe, 1))
            m_kvmm[0] = s_pe.n
            m_ktok[0] = ev_kv[0][NT - 1]

            wt("gpsimd", (s_pe, m_kvmm[0]))
            do("gpsimd", "dma_start", out=wsB.rearrange("p (k j) -> p k j", k=NK),
               in_=wslab_ap("wq", 0), inc=((s_dw, 16),))
            do("gpsimd", "dma_start",
               out=t_bias4[0:1, 512:1024], in_=d_b["bq"][0:1, 0:512],
               inc=((s_dw, 16),))
            m_wq = s_dw.n
            do("gpsimd", "dma_start", out=wsA.rearrange("p (k j) -> p k j", k=NK),
               in_=wslab_ap("wg", 0), inc=((s_dw, 16),))
            do("gpsimd", "dma_start",
               out=t_bias4[0:1, 0:512], in_=d_b["bg"][0:1, 0:512],
               inc=((s_dw, 16),))
            m_wg = s_dw.n
            m_qmm, m_qev = emit_q_proj(0, m_wq, m_wq, m_ga_last)

            def rope_hb(buf, hb):
                b4 = buf.rearrange("p (t h f) -> p t h f", t=NT, h=HPG)
                ropeT = [view(OFF_RT + i * 2048, 1024, BF16) for i in range(4)]
                x1 = b4[:, hb * 4:(hb + 1) * 4, :, 0:64]
                x2 = b4[:, hb * 4:(hb + 1) * 4, :, 64:128]
                cosb = t_cos[:, :].rearrange("p (t f) -> p t f", t=NT)[
                    :, hb * 4:(hb + 1) * 4, :].unsqueeze(2).broadcast_to(
                    [128, 4, HPG, 64])
                sinb = t_sin[:, :].rearrange("p (t f) -> p t f", t=NT)[
                    :, hb * 4:(hb + 1) * 4, :].unsqueeze(2).broadcast_to(
                    [128, 4, HPG, 64])
                r4 = [r.rearrange("p (t h f) -> p t h f", t=4, h=HPG)
                      for r in ropeT]
                do("vector", "tensor_tensor", r4[0], x1, cosb, ALU.mult)
                do("vector", "tensor_tensor", r4[1], x1, sinb, ALU.mult)
                do("vector", "tensor_tensor", r4[2], x2, sinb, ALU.mult)
                do("vector", "tensor_tensor", r4[3], x2, cosb, ALU.mult)
                do("vector", "tensor_tensor", x1, r4[0], r4[2], ALU.subtract)
                do("vector", "tensor_tensor", x2, r4[1], r4[3], ALU.add,
                   inc=((s_dv, 1),))

            def elu_q(buf, q4):
                ropeT = [view(OFF_RT + i * 2048, 1024, BF16) for i in range(4)]
                tmin = ropeT[0]
                texp = ropeT[2]
                sl = buf[:, q4 * 1024:(q4 + 1) * 1024]
                do("vector", "tensor_single_scalar", tmin, sl, 0.0, ALU.min,
                   inc=((s_dv, 1),))
                wt("scalar", (s_dv, s_dv.n))
                do("scalar", "activation", out=texp, in_=tmin, func=AF.Exp,
                   inc=((s_ac, 1),))
                wt("vector", (s_ac, s_ac.n))
                do("vector", "scalar_tensor_tensor", sl, sl, 0.0, texp,
                   ALU.max, ALU.add, inc=((s_dv, 1),))

            for hg in range(NHG):
                slot = hg % 2
                k_tok = k_tok2[slot]
                v_tok = v_tok2[slot]
                qtok = v_tok2[(hg + 1) % 2]

                if hg + 1 < NHG:
                    wt("gpsimd", (s_pe, m_qmm))
                    do("gpsimd", "dma_start",
                       out=wsB.rearrange("p (k j) -> p k j", k=NK),
                       in_=wslab_ap("wk", hg + 1), inc=((s_dw, 16),))
                    do("gpsimd", "dma_start",
                       out=t_bias4[0:1, 512:1024],
                       in_=d_b["bv"][0:1, (hg + 1) * 512:(hg + 2) * 512],
                       inc=((s_dw, 16),))
                    m_wkb = s_dw.n

                fm_stage = 0
                wt("tensor", (s_dw, m_wg))
                if hg > 0:
                    wt("tensor", (s_dv, m_gate[hg - 1]),
                       (s_ac, m_gatepsw0))
                else:
                    wt("tensor", (s_dv, ev_kv[0][NT - 1]), (s_dv, m_qev))
                ev_g = [0, 0]
                for tt in range(NT):
                    gps = psw0[:, (tt % 4) * 512:(tt % 4) * 512 + 512]
                    if tt >= 4:
                        wt("tensor", (s_ac, ev_g[(tt // 2) % 2]))
                    do("tensor", "matmul", gps, t_ones[0:1, :],
                       t_bias4[0:1, 0:512], start=True, stop=False,
                       skip_group_check=True)
                    for k in range(NK):
                        lhsT = xn_fm[:, k * 1024 + tt * 128:k * 1024 + (tt + 1) * 128]
                        do("tensor", "matmul", gps, lhsT,
                           wsA[:, k * 512:(k + 1) * 512],
                           start=False, stop=(k == NK - 1), skip_group_check=True)
                    inc_last("tensor", (s_pe, 1))
                    if tt % 2 == 1:
                        if tt == 1 and hg > 0:
                            wt("scalar", (s_pe, s_pe.n), (s_dv, m_gate[hg - 1]))
                        else:
                            wt("scalar", (s_pe, s_pe.n))
                        for t2 in (tt - 1, tt):
                            do("scalar", "activation",
                               out=g_tok[:, t2 * 512:(t2 + 1) * 512],
                               in_=psw0[:, (t2 % 4) * 512:(t2 % 4) * 512 + 512],
                               func=AF.Sigmoid,
                               inc=((s_ac, 1),) if t2 == tt else ())
                        ev_g[(tt // 2) % 2] = s_ac.n
                    if fm_stage < 8:
                        if fm_stage == 0:
                            wts = [(s_dv, m_ktok[hg]), (s_dw, m_tables)]
                            if hg == 0:
                                wts.append((s_ac, m_gffn))
                            wt("vector", *wts)
                            rope_hb(k_tok, 0)
                        elif fm_stage == 1:
                            rope_hb(k_tok, 1)
                        elif fm_stage == 2:
                            wt("vector", (s_dv, m_qev))
                            rope_hb(qtok, 0)
                        elif fm_stage == 3:
                            rope_hb(qtok, 1)
                        elif fm_stage == 4:
                            elu_q(k_tok, 0)
                            elu_q(k_tok, 1)
                        elif fm_stage == 5:
                            elu_q(k_tok, 2)
                            elu_q(k_tok, 3)
                        elif fm_stage == 6:
                            elu_q(qtok, 0)
                            elu_q(qtok, 1)
                        else:
                            elu_q(qtok, 2)
                            elu_q(qtok, 3)
                        fm_stage += 1
                inc_last("tensor", (s_pe, 1))
                m_gqmm[hg] = s_pe.n
                m_fmk = 0
                m_fmq = s_dv.n
                m_sig_last = s_ac.n

                if lvl == 2:
                    raise _Stop
                if lvl == 3:
                    raise _Stop

                wt("tensor", (s_dv, m_fmq))
                m_qT_ev = [0, 0]
                for tt in range(NT):
                    tp = tp_bf if tt % 2 == 0 else tp_bf2
                    if m_qT_ev[tt % 2]:
                        wt("tensor", (s_dv, m_qT_ev[tt % 2]))
                    for hh in range(HPG):
                        do("tensor", "transpose",
                           tp[:, hh * 128:(hh + 1) * 128],
                           qtok[:, tt * 512 + hh * 128:tt * 512 + (hh + 1) * 128],
                           t_id[:, :])
                    inc_last("tensor", (s_pe, 1))
                    wt("vector", (s_pe, s_pe.n))
                    do("vector", "tensor_copy",
                       q_fm.rearrange("p (h t) -> p h t", h=HPG)[
                           :, :, tt * 128:(tt + 1) * 128],
                       tp.rearrange("p (h t) -> p h t", h=HPG),
                       inc=((s_dv, 1),))
                    m_qT_ev[tt % 2] = s_dv.n
                m_qT = s_dv.n

                if hg + 1 < NHG:
                    wt("gpsimd", (s_pe, m_gqmm[hg]))
                    do("gpsimd", "dma_start",
                       out=t_bias4[0:1, 0:512],
                       in_=d_b["bk"][0:1, (hg + 1) * 512:(hg + 2) * 512],
                       inc=((s_dw, 16),))
                    m_bk = s_dw.n
                    do("gpsimd", "dma_start",
                       out=wsA.rearrange("p (k j) -> p k j", k=NK),
                       in_=wslab_ap("wv", hg + 1), inc=((s_dw, 16),))
                    m_wv = s_dw.n

                m_snap = {}
                m_numev = {}
                szh_buf = [t_szh, szh2]

                def emit_num(j):
                    jt, jhalf = j // 2, j % 2
                    jpb = jhalf * 64
                    wt("tensor", (s_dv, m_snap[j]))
                    if j == 0:
                        wt("tensor", (s_dv, m_qT))
                    if j == 1:
                        wt("tensor", (s_ac, m_sig_last))
                    if j >= 2:
                        wt("tensor", (s_dv, m_numev[j - 2]))
                    npair = (psw1[:, 1024:2048] if jhalf == 0
                             else psw0[:, 1024:2048])
                    szh = szh_buf[j % 2]
                    for hh in range(HPG):
                        hb, hl = hh // 2, hh % 2
                        do("tensor", "matmul",
                           npair[jpb:jpb + 64,
                                 hb * 512 + hl * 128:hb * 512 + hl * 128 + 128],
                           q_fm[:, hh * 1024 + j * 64:hh * 1024 + (j + 1) * 64],
                           szh[:, hh * 129:hh * 129 + 128],
                           start=(hl == 0), stop=False, skip_group_check=True,
                           tile_position=(0, jpb) if jpb else None)
                    for hh in range(HPG):
                        hb, hl = hh // 2, hh % 2
                        do("tensor", "matmul",
                           npair[jpb:jpb + 64,
                                 hb * 512 + 256 + hl:hb * 512 + 257 + hl],
                           q_fm[:, hh * 1024 + j * 64:hh * 1024 + (j + 1) * 64],
                           szh[:, hh * 129 + 128:hh * 129 + 129],
                           start=False, stop=(hl == 1), skip_group_check=True,
                           tile_position=(0, jpb) if jpb else None)
                    inc_last("tensor", (s_pe, 1))
                    wt("vector", (s_pe, s_pe.n))
                    n4 = npair.rearrange("p (b f) -> p b f", b=2)
                    dst = num_sb[jpb:jpb + 64,
                                 jt * 516:(jt + 1) * 516].rearrange(
                        "p (bh f) -> p bh f", f=129).rearrange(
                        "p (b h) f -> p b h f", b=2)
                    do("vector", "tensor_copy", dst[:, :, :, 0:128],
                       n4[jpb:jpb + 64, :, 0:256].rearrange(
                           "p b (h f) -> p b h f", h=2))
                    do("vector", "tensor_copy", dst[:, :, :, 128:129],
                       n4[jpb:jpb + 64, :, 256:258].rearrange(
                           "p b (h f) -> p b h f", h=2), inc=((s_dv, 1),))
                    m_numev[j] = s_dv.n

                for i in range(NCH):
                    tt, half = i // 2, i % 2
                    pb = half * 64
                    if i > 0:
                        wt("tensor", (s_dv, m_snap[i - 1]))
                    else:
                        wt("tensor", (s_dv, m_fmq))
                    for hh in range(HPG):
                        sbank = sA if hh < 2 else sB
                        hl = hh % 2
                        do("tensor", "matmul",
                           sbank[:, hl * 128:hl * 128 + 128],
                           k_tok[pb:pb + 64,
                                 tt * 512 + hh * 128:tt * 512 + (hh + 1) * 128],
                           v_tok[pb:pb + 64,
                                 tt * 512 + hh * 128:tt * 512 + (hh + 1) * 128],
                           start=(i == 0 and hl == 0), stop=False,
                           skip_group_check=True)
                    for hh in range(HPG):
                        sbank = sA if hh < 2 else sB
                        hl = hh % 2
                        do("tensor", "matmul", sbank[:, 256 + hl:257 + hl],
                           k_tok[pb:pb + 64,
                                 tt * 512 + hh * 128:tt * 512 + (hh + 1) * 128],
                           t_onescol[pb:pb + 64, 0:1],
                           start=False, stop=(i == NCH - 1 and hl == 1),
                           skip_group_check=True)
                    inc_last("tensor", (s_pe, 1))
                    m_mm1 = s_pe.n
                    wt("vector", (s_pe, m_mm1))
                    spair = psw1[:, 0:1024].rearrange("p (b f) -> p b f", b=2)
                    if i == NCH - 1:
                        t_sf16 = gath[:, 0:516]
                        fdst = t_sf16.rearrange(
                            "p (bh f) -> p bh f", f=129).rearrange(
                            "p (b h) f -> p b h f", b=2)
                        do("vector", "tensor_copy", fdst[:, :, :, 0:128],
                           spair[:, :, 0:256].rearrange(
                               "p b (h f) -> p b h f", h=2))
                        do("vector", "tensor_copy", fdst[:, :, :, 128:129],
                           spair[:, :, 256:258].rearrange(
                               "p b (h f) -> p b h f", h=2), inc=((s_dv, 1),))
                        m_sfin = s_dv.n
                        wt("sync", (s_dv, m_sfin))
                        do("sync", "dma_start", out=d_ccs[hg][:, :],
                           in_=t_sf16, inc=((s_dx, 16),))
                        m_stg = s_dx.n
                        wt("gpsimd", (s_dx, m_stg))
                        do("gpsimd", "collective_compute", "AllGather",
                           ALU.bypass, replica_groups=[list(range(NCORE))],
                           ins=[d_ccs[hg][:, :]], outs=[d_ccd[hg][:, :]],
                           inc=((s_cc, 1),))
                    zdst = szh_buf[i % 2][:, 0:516].rearrange(
                        "p (bh f) -> p bh f", f=129).rearrange(
                        "p (b h) f -> p b h f", b=2)
                    do("vector", "tensor_copy", zdst[:, :, :, 0:128],
                       spair[:, :, 0:256].rearrange("p b (h f) -> p b h f", h=2))
                    do("vector", "tensor_copy", zdst[:, :, :, 128:129],
                       spair[:, :, 256:258].rearrange("p b (h f) -> p b h f", h=2),
                       inc=((s_dv, 1),))
                    m_snap[i] = s_dv.n
                    if hg + 1 < NHG and 1 <= i <= 8:
                        emit_k_group(hg + 1, i - 1, m_wkb, m_bk)
                        if i == 8:
                            inc_last("tensor", (s_pe, 1))
                            m_kmm = s_pe.n
                            m_ktok[hg + 1] = ev_k[hg + 1][NT - 1]
                            wt("gpsimd", (s_pe, m_kmm))
                            do("gpsimd", "dma_start",
                               out=wsB.rearrange("p (k j) -> p k j", k=NK),
                               in_=wslab_ap("wq", hg + 1), inc=((s_dw, 16),))
                            do("gpsimd", "dma_start",
                               out=t_bias4[0:1, 0:512],
                               in_=d_b["bg"][0:1,
                                             (hg + 1) * 512:(hg + 2) * 512],
                               inc=((s_dw, 16),))
                            m_wq = s_dw.n
                    if hg + 1 < NHG and 9 <= i <= 15:
                        emit_v_group(hg + 1, i - 9, m_wv)
                    if i > 0:
                        emit_num(i - 1)
                if hg + 1 < NHG:
                    emit_v_group(hg + 1, 7, m_wv)
                    inc_last("tensor", (s_pe, 1))
                    m_vmm = s_pe.n
                emit_num(NCH - 1)
                m_scan = s_pe.n
                if hg + 1 < NHG:
                    wt("gpsimd", (s_pe, m_vmm))
                    do("gpsimd", "dma_start",
                       out=t_bias4[0:1, 512:1024],
                       in_=d_b["bq"][0:1, (hg + 1) * 512:(hg + 2) * 512],
                       inc=((s_dw, 16),))
                    m_bq = s_dw.n

                if lvl == 4:
                    raise _Stop

                if hg + 1 < NHG:
                    m_qmm, m_qev = emit_q_proj(hg + 1, m_wq, m_bq,
                                               m_numev[NCH - 1])
                wt("sync", (s_cc, s_cc.n), (s_dv, m_fmq))
                do("sync", "dma_start",
                   out=gath[:, 0:2064].rearrange("p (s f) -> p s f", s=4),
                   in_=d_ccd[hg].rearrange("(u v p) f -> p u v f", v=2, p=128)[
                       :, :, 0, :],
                   inc=((s_dx, 16),))
                m_gath = s_dx.n
                wt("vector", (s_dx, m_gath), (s_pe, m_scan))
                do("vector", "tensor_tensor",
                   gath[:, 0:2064].rearrange("p (s f) -> p s f", s=4),
                   gath[:, 0:2064].rearrange("p (s f) -> p s f", s=4),
                   t_msk[:, :].rearrange("p (u v) -> p u v", v=2)[
                       :, :, 0].unsqueeze(2).broadcast_to([128, 4, 516]),
                   ALU.mult)
                do("vector", "tensor_tensor", gath[:, 0:1032], gath[:, 0:1032],
                   gath[:, 1032:2064], ALU.add)
                do("vector", "tensor_tensor", t_szh[:, 0:516], gath[:, 0:516],
                   gath[:, 516:1032], ALU.add, inc=((s_dv, 1),))
                m_szh = s_dv.n
                if hg + 1 < NHG:
                    wt("gpsimd", (s_dv, m_szh), (s_pe, m_vmm))
                    do("gpsimd", "dma_start",
                       out=wsA.rearrange("p (k j) -> p k j", k=NK),
                       in_=wslab_ap("wg", hg + 1), inc=((s_dw, 16),))
                    m_wg = s_dw.n

                if lvl == 5:
                    raise _Stop

                gtp = [psw1[:, 0:256].bitcast(BF16),
                       psw1[:, 512:768].bitcast(BF16)]
                atok2 = [t_atok[:, 0:512], view(OFF_GATH + 4128, 512, BF16)]
                c129_b = [t_c129[:, 0:516], view(OFF_GATH + 5152, 516, F32)]
                m_c1war = m_qev if hg + 1 < NHG else m_numev[NCH - 1]
                m_corr = {}
                m_cfree = {}
                m_atok = {}
                m_attcopy = {}

                def emit_corr(tt):
                    if tt == 0:
                        wt("tensor", (s_dv, m_szh))
                    elif tt == 1:
                        wt("tensor", (s_dv, m_c1war))
                    else:
                        wt("tensor", (s_ac, m_cfree[tt - 2]))
                    cpair = (psw1[:, 1024:2048] if tt % 2 == 0
                             else psw0[:, 1024:2048])
                    for hh in range(HPG):
                        hb, hl = hh // 2, hh % 2
                        do("tensor", "matmul",
                           cpair[:, hb * 512 + hl * 128:hb * 512 + hl * 128 + 128],
                           q_fm[:, hh * 1024 + tt * 128:hh * 1024 + (tt + 1) * 128],
                           t_szh[:, hh * 129:hh * 129 + 128],
                           start=(hl == 0), stop=False, skip_group_check=True)
                    for hh in range(HPG):
                        hb, hl = hh // 2, hh % 2
                        do("tensor", "matmul",
                           cpair[:, hb * 512 + 256 + hl:hb * 512 + 257 + hl],
                           q_fm[:, hh * 1024 + tt * 128:hh * 1024 + (tt + 1) * 128],
                           t_szh[:, hh * 129 + 128:hh * 129 + 129],
                           start=False, stop=False, skip_group_check=True)
                    for hb in range(2):
                        do("tensor", "matmul",
                           cpair[:, hb * 512 + 256:hb * 512 + 258],
                           t_ones[0:1, :], t_eps[0:1, :],
                           start=False, stop=True, skip_group_check=True)
                    inc_last("tensor", (s_pe, 1))
                    m_corr[tt] = s_pe.n
                    return cpair

                def emit_gate_dve(tt, cpair):
                    c129 = c129_b[tt % 2].rearrange(
                        "p (bh f) -> p bh f", f=129).rearrange(
                        "p (b h) f -> p b h f", b=2)
                    c4 = cpair.rearrange("p (b f) -> p b f", b=2)
                    wt("scalar", (s_pe, m_corr[tt]))
                    do("scalar", "copy", c129[:, :, :, 128:129],
                       c4[:, :, 256:258].rearrange("p b (h f) -> p b h f", h=2))
                    do("scalar", "copy", c129[:, :, :, 0:128],
                       c4[:, :, 0:256].rearrange("p b (h f) -> p b h f", h=2),
                       inc=((s_ac, 1),))
                    m_cfree[tt] = s_ac.n
                    t129 = t_sfin[:, 0:516].rearrange(
                        "p (bh f) -> p bh f", f=129).rearrange(
                        "p (b h) f -> p b h f", b=2)
                    wt("vector", (s_ac, m_cfree[tt]))
                    do("vector", "tensor_tensor", t129,
                       num_sb[:, tt * 516:(tt + 1) * 516].rearrange(
                           "p (bh f) -> p bh f", f=129).rearrange(
                           "p (b h) f -> p b h f", b=2),
                       c129, ALU.add, inc=((s_dv, 1),))
                    wt("scalar", (s_dv, s_dv.n))
                    do("scalar", "copy", t_dtmp[:, :],
                       t129[:, :, :, 128:129].rearrange("p b h f -> p (b h f)"),
                       inc=((s_ac, 1),))
                    m_den = s_ac.n
                    a4 = atok2[tt % 2].rearrange(
                        "p (b h f) -> p b h f", b=2, h=2)
                    g4 = g_tok[:, tt * 512:(tt + 1) * 512].rearrange(
                        "p (b h f) -> p b h f", b=2, h=2)
                    do("vector", "tensor_tensor", a4[:, 0:1],
                       t129[:, 0:1, :, 0:128], g4[:, 0:1], ALU.mult)
                    wt("vector", (s_ac, m_den))
                    do("vector", "reciprocal", t_rden[:, :], t_dtmp[:, :])
                    do("vector", "tensor_tensor", a4[:, 1:2],
                       t129[:, 1:2, :, 0:128], g4[:, 1:2], ALU.mult)
                    do("vector", "tensor_tensor", a4, a4,
                       t_rden[:, :].rearrange("p (b h) -> p b h", b=2).unsqueeze(
                           3).broadcast_to([128, 2, 2, 128]), ALU.mult,
                       inc=((s_dv, 1),))
                    m_atok[tt] = s_dv.n

                def emit_attnT(tt):
                    wts = [(s_dv, m_atok[tt])]
                    if tt >= 2:
                        wts.append((s_dv, m_attcopy[tt - 2]))
                    wt("tensor", *wts)
                    tp = gtp[tt % 2]
                    src = atok2[tt % 2]
                    for hh in range(HPG):
                        do("tensor", "transpose",
                           tp[:, hh * 128:(hh + 1) * 128],
                           src[:, hh * 128:(hh + 1) * 128], t_id[:, :])
                    inc_last("tensor", (s_pe, 1))
                    wt("vector", (s_pe, s_pe.n))
                    do("vector", "tensor_copy",
                       attn_fm.rearrange("p (a t) -> p a t", a=NK)[
                           :, hg * 4:(hg + 1) * 4, tt * 128:(tt + 1) * 128],
                       tp.rearrange("p (a t) -> p a t", a=HPG),
                       inc=((s_dv, 1),))
                    m_attcopy[tt] = s_dv.n

                emit_gate_dve(0, emit_corr(0))
                for tt in range(1, NT):
                    emit_gate_dve(tt, emit_corr(tt))
                    emit_attnT(tt - 1)
                emit_attnT(NT - 1)
                m_gate[hg] = m_attcopy[NT - 1]
                m_gatepsw0 = m_cfree[NT - 1]

            if lvl < 7:
                raise _Stop


            wo_h = [view(OFF_WO, NK * 1024, BF16), view(OFF_XN, NK * 1024, BF16)]
            xld = [view(OFF_P3X0, 1024, F32), view(OFF_P3X1, 1024, F32)]
            ost = [view(OFF_P3O0, 1024, F32), view(OFF_P3O1, 1024, F32)]

            def wohalf_ap(hf):
                return d_w["wo"][:, hf * 1024:(hf + 1) * 1024].rearrange(
                    "(k p) j -> p k j", p=128)

            wt("gpsimd", (s_pe, m_gqmm[NHG - 1]))
            do("gpsimd", "dma_start", out=t_bias4[0:1, :],
               in_=d_b["bo"][0:1, 0:1024], inc=((s_dw, 16),))
            m_bo = [s_dw.n]
            wt("gpsimd", (s_cc, s_cc.n))
            do("gpsimd", "dma_start",
               out=wo_h[0].rearrange("p (k j) -> p k j", k=NK), in_=wohalf_ap(0),
               inc=((s_dw, 16),))
            m_wo = [s_dw.n]
            do("gpsimd", "dma_start",
               out=wo_h[1].rearrange("p (k j) -> p k j", k=NK), in_=wohalf_ap(1),
               inc=((s_dw, 16),))
            m_wo.append(s_dw.n)

            m_comb = {}
            m_p3mm = {}
            for hf in range(2):
                wo_half = wo_h[hf]
                wt("tensor", (s_dw, m_wo[hf]))
                if hf == 0:
                    wt("tensor", (s_dv, m_gate[NHG - 1]))
                for tt in range(NT):
                    ps = psw0 if tt % 2 == 0 else psw1
                    pshalf = ps[:, 0:1024]
                    prev = None
                    if tt >= 2:
                        prev = m_comb[(hf, tt - 2)]
                    elif hf == 1:
                        prev = m_comb[(0, NT - 2 + tt)]
                    if prev is not None:
                        wt("tensor", (s_dv, prev))
                    for k in range(NK):
                        lhsT = attn_fm[:, k * 1024 + tt * 128:k * 1024 + (tt + 1) * 128]
                        for jc in range(2):
                            do("tensor", "matmul",
                               pshalf[:, jc * 512:(jc + 1) * 512], lhsT,
                               wo_half[:, k * 1024 + jc * 512:k * 1024 + (jc + 1) * 512],
                               start=(k == 0), stop=False,
                               skip_group_check=True)
                    if tt == 0:
                        wt("tensor", (s_dw, m_bo[hf]))
                    for jc in range(2):
                        do("tensor", "matmul", pshalf[:, jc * 512:(jc + 1) * 512],
                           t_ones[0:1, :],
                           t_bias4[0:1, jc * 512:(jc + 1) * 512],
                           start=False, stop=True, skip_group_check=True)
                    inc_last("tensor", (s_pe, 1))
                    m_smm = s_pe.n
                    m_p3mm[(hf, tt)] = m_smm
                    if prev is not None:
                        wt("sync", (s_dv, prev))
                    elif hf == 0 and tt == 0:
                        wt("sync", (s_dv, m_gate[NHG - 1]))
                    do("sync", "dma_start", out=xld[tt % 2],
                       in_=d_x[tt * 128:(tt + 1) * 128, hf * 1024:(hf + 1) * 1024],
                       inc=((s_dx, 16),))
                    m_xr = s_dx.n
                    wt("vector", (s_pe, m_smm), (s_dx, m_xr))
                    if hf == 0 and tt == 0:
                        wt("vector", (s_ac, m_gffn))
                    do("vector", "tensor_tensor", ost[tt % 2], pshalf,
                       gffn[:, tt * 2048 + hf * 1024:tt * 2048 + (hf + 1) * 1024],
                       ALU.mult)
                    do("vector", "tensor_tensor", ost[tt % 2], ost[tt % 2],
                       xld[tt % 2], ALU.add, inc=((s_dv, 1),))
                    m_comb[(hf, tt)] = s_dv.n
                    wt("sync", (s_dv, m_comb[(hf, tt)]))
                    do("sync", "dma_start",
                       out=d_y[tt * 128:(tt + 1) * 128, hf * 1024:(hf + 1) * 1024],
                       in_=ost[tt % 2], inc=((s_out, 16),))
                if hf == 0:
                    wt("gpsimd", (s_pe, m_p3mm[(0, NT - 1)]))
                    do("gpsimd", "dma_start", out=t_bias4[0:1, :],
                       in_=d_b["bo"][0:1, 1024:2048], inc=((s_dw, 16),))
                    m_bo.append(s_dw.n)


        except _Stop:
            pass

        dbg_srcs = {
            "xn_fm": xn_fm, "gffn": gffn, "ga": ga, "k_tok": k_tok2[0],
            "v_tok": v_tok2[0], "g_tok": g_tok, "q_fm": q_fm, "num_sb": num_sb,
            "den": t_den[:, :], "attn_fm": attn_fm, "szh": t_szh[:, :],
            "stats": t_stats[:, :], "xns1": xns[1], "xb1": xb[1],
        }
        if debug:
            dchunk = view(OFF_BIG, 2048, F32)
            for (nm, shape) in debug:
                src_ap = dbg_srcs[nm]
                n = src_ap.shape[1]
                for c0 in range(0, n, 2048):
                    c1 = min(c0 + 2048, n)
                    wt("vector", (s_dv, s_dv.n), (s_ac, s_ac.n),
                       (s_pe, s_pe.n), (s_dx, s_dx.n), (s_dw, s_dw.n))
                    do("vector", "tensor_copy", dchunk[:, 0:c1 - c0],
                       src_ap[:, c0:c1], inc=((s_dv, 1),))
                    wt("gpsimd", (s_dv, s_dv.n))
                    do("gpsimd", "dma_start", out=d_dbg[nm][:, c0:c1],
                       in_=dchunk[:, 0:c1 - c0], inc=((s_dw, 16),))

        wt("sync", (s_out, s_out.n))
        wt("gpsimd", (s_dw, s_dw.n))

        with nc.Block() as block:
            @block.tensor
            def _(t):
                for f in P["tensor"]:
                    f(t)

            @block.vector
            def _(v):
                for f in P["vector"]:
                    f(v)

            @block.scalar
            def _(sc):
                for f in P["scalar"]:
                    f(sc)

            @block.gpsimd
            def _(g):
                for f in P["gpsimd"]:
                    f(g)

            @block.sync
            def _(sy):
                for f in P["sync"]:
                    f(sy)

    return nc


def _perm_deint():
    p = np.zeros(D, np.int64)
    for h in range(H):
        b = h * DK
        for i in range(DK // 2):
            p[b + i] = b + 2 * i
            p[b + 64 + i] = b + 2 * i + 1
    return p


def _rope_tables():
    inv = (1.0 / 10000.0 ** (np.arange(0, DK, 2, dtype=np.float32)
                             / np.float32(DK))).astype(np.float32)
    pos = np.arange(2048, dtype=np.float32)
    ang = pos[:, None] * inv[None, :]
    return np.sin(ang).astype(np.float32), np.cos(ang).astype(np.float32)


_DEBUG = ()


def kernel(L0_attn_x, L0_attn_Wq, L0_attn_bq, L0_attn_Wk, L0_attn_bk,
           L0_attn_Wv, L0_attn_bv, L0_attn_Wg, L0_attn_bg, L0_attn_Wo,
           L0_attn_bo, L0_ffn_up_weight, L0_ffn_down_weight, L0_ffn_up_scale,
           L0_ffn_down_scale, L0_ffn_output_scale):
    x = np.ascontiguousarray(
        np.asarray(L0_attn_x, np.float32).reshape(4 * 2048, D))
    p = _perm_deint()
    wq = np.ascontiguousarray(np.asarray(L0_attn_Wq, np.float32).T[:, p])
    wk = np.ascontiguousarray(np.asarray(L0_attn_Wk, np.float32).T[:, p])
    wv = np.ascontiguousarray(np.asarray(L0_attn_Wv, np.float32).T)
    wg = np.ascontiguousarray(np.asarray(L0_attn_Wg, np.float32).T)
    wo = np.ascontiguousarray(np.asarray(L0_attn_Wo, np.float32).T)
    upw = np.ascontiguousarray(np.asarray(L0_ffn_up_weight, np.float32).T)
    dnw = np.ascontiguousarray(np.asarray(L0_ffn_down_weight, np.float32).T)
    bq = np.ascontiguousarray(np.asarray(L0_attn_bq, np.float32)[p][None, :])
    bk = np.ascontiguousarray(np.asarray(L0_attn_bk, np.float32)[p][None, :])
    bv = np.ascontiguousarray(np.asarray(L0_attn_bv, np.float32)[None, :])
    bg = np.ascontiguousarray(np.asarray(L0_attn_bg, np.float32)[None, :])
    bo = np.ascontiguousarray(np.asarray(L0_attn_bo, np.float32)[None, :])
    usc = np.ascontiguousarray(
        np.asarray(L0_ffn_up_scale, np.float32).reshape(64, 128).T)
    dsc = np.ascontiguousarray(
        np.asarray(L0_ffn_down_scale, np.float32)[None, :])
    osc = np.ascontiguousarray(
        np.asarray(L0_ffn_output_scale, np.float32).reshape(1, 1))
    sin_f, cos_f = _rope_tables()
    ident = np.eye(128, dtype=np.float32)

    nc = build_nc(debug=_DEBUG)

    in_maps = []
    for c in range(NCORE):
        half = c % 2
        t0 = half * 1024
        mask = np.zeros((128, 8), np.float32)
        if half == 1:
            mask[:, c - 1] = 1.0
        in_maps.append({
            "x": np.ascontiguousarray(x[c * TPC:(c + 1) * TPC]),
            "wq": wq, "wk": wk, "wv": wv, "wg": wg, "wo": wo,
            "upw": upw, "dnw": dnw,
            "bq": bq, "bk": bk, "bv": bv, "bg": bg, "bo": bo,
            "usc": usc, "dsc": dsc, "osc": osc,
            "rsin": np.ascontiguousarray(sin_f[t0:t0 + 1024]),
            "rcos": np.ascontiguousarray(cos_f[t0:t0 + 1024]),
            "ident": ident,
            "mask8": mask,
        })

    res = run_bass_kernel_spmd(nc, in_maps, core_ids=list(range(NCORE)))
    out = np.concatenate([res.results[c]["y"] for c in range(NCORE)], axis=0)
    if _DEBUG:
        kernel._dbg = [{nm: res.results[c]["dbg_" + nm] for (nm, _s) in _DEBUG}
                       for c in range(NCORE)]
    return out.reshape(4, 2048, D)
